# version 17
# baseline (speedup 1.0000x reference)
import numpy as np
import ml_dtypes
from contextlib import ExitStack
import concourse.bass as bass
import concourse.mybir as mybir
from concourse.bass_utils import run_bass_kernel_spmd

F32 = mybir.dt.float32
BF16 = mybir.dt.bfloat16
I32 = mybir.dt.int32
AF = mybir.ActivationFunctionType
ALU = mybir.AluOpType
AX = mybir.AxisListType

T = 2048
E = 1024
NH = 16
D = 64
IF = 2816
NT = T // 128
NE = E // 128
NI = IF // 128
NTI = 4
NEG = -1e9
LN_EPS = 1e-5
RMS_EPS = float(np.finfo(np.float32).eps)
QMAX = 127.0
BF = ml_dtypes.bfloat16
SERIAL_P5 = 0
SERIAL_P4 = False
SERIAL_P1 = False
SERIAL_P0 = False
SERIAL_Q = False
SERIAL_P2 = False
SERIAL_P3 = False


def _host_consts():
    ident = np.eye(128, dtype=np.float32)
    dq = np.arange(128)[:, None]
    o = np.arange(128)[None, :]
    m0 = np.where(o <= dq, NEG, 0.0).astype(np.float32)
    m2 = np.where(o > dq, NEG, 0.0).astype(np.float32)
    maskrow = np.concatenate([m0, np.zeros((128, 128), np.float32), m2], axis=1)
    maskT = np.ascontiguousarray(
        np.concatenate([m0.T, np.zeros((128, 128), np.float32), m2.T], axis=1))
    qh = np.zeros((128, 128), np.float32)
    for ec in range(8):
        qh[0:64, 16 * ec + 2 * ec] = 1.0
        qh[64:128, 16 * ec + 2 * ec + 1] = 1.0
    rbc = np.zeros((16, 1024), np.float32)
    for ec in range(8):
        rbc[2 * ec, 128 * ec: 128 * ec + 64] = 1.0
        rbc[2 * ec + 1, 128 * ec + 64: 128 * ec + 128] = 1.0
    return {
        "identbf": ident.astype(BF),
        "identf": ident,
        "onesg": np.ones((128, 128), np.float32).astype(BF),
        "onesf": np.ones((1, 128), np.float32),
        "maskrow": maskrow.astype(BF),
        "maskTt": maskT.astype(BF),
        "qh_ind": qh.astype(BF),
        "rbc_ind": rbc.astype(BF),
    }


class RES:
    __slots__ = ("w", "r", "dram")
    def __init__(self, dram=False):
        self.w = None
        self.r = []
        self.dram = dram


class Walk:
    SEM_OF = {"sync": "LX", "gpsimd": "G", "tensor": "T", "vector": "V",
              "scalar": "S"}
    DMA_SEMS = ("LX", "LG", "ST", "LA0", "LA1", "LB0", "LB1", "LC0", "LC1",
                "LD0", "LD1", "SA0", "SA1")

    def __init__(self, semh, counts, res, eng, me, consts_res, zst):
        self.semh = semh
        self.c = counts
        self.res = res
        self.eng = eng
        self.me = me
        self.consts = consts_res
        self.fenced = False
        self.Z = zst
        self.serial = False

    def sync_all(self):
        for s_ in ("T", "V", "S", "G") + self.DMA_SEMS:
            if self.c[s_] > 0:
                self.eng.wait_ge(self.semh[s_], self.c[s_])

    def zs(self, e, instr):
        k = self.me
        self.Z[k] += 1
        instr.then_inc(self.semh["Z" + k], 1)
        e.wait_ge(self.semh["Z" + k], self.Z[k])
        return instr

    def step(self, who, emit, reads=(), writes=(), sem=None, n=1):
        sem = sem or self.SEM_OF[who]
        on = (who == self.me)
        instr = None
        rd = list(reads)
        if on:
            if self.serial:
                self.sync_all()
            if not self.fenced and self.consts is not None:
                rd.append(self.consts)
                self.fenced = True
            need_r = {}
            need_w = {}
            def _need(d, tok):
                if tok is not None:
                    s_, v_ = tok
                    if v_ > d.get(s_, -1):
                        d[s_] = v_
            for b in rd:
                _need(need_r, b.w)
            for b in writes:
                _need(need_w, b.w)
                for tok in b.r:
                    _need(need_w, tok)
            for s_, v_ in sorted(need_r.items()):
                self.eng.wait_ge(self.semh[s_], v_)
            for s_, v_ in sorted(need_w.items()):
                if need_r.get(s_, -1) >= v_:
                    continue
                if s_ == sem and s_ not in self.DMA_SEMS:
                    continue
                self.eng.wait_ge(self.semh[s_], v_)
            instr = emit(self.eng)
        self.c[sem] += n
        tok = (sem, self.c[sem])
        if on and instr is not None:
            instr.then_inc(self.semh[sem], n)
        for b in reads:
            b.r.append(tok)
        for b in writes:
            b.w = tok
            b.r = []
        return instr


def build_nc(debug=False):
    nc = bass.Bass("TRN2", target_bir_lowering=False)
    dti = lambda n, s, d: nc.dram_tensor(n, s, d, kind="ExternalInput").ap()
    x_d = dti("x", [T, E], F32)
    wq_d = dti("wq", [E, E], F32); wk_d = dti("wk", [E, E], F32)
    wv_d = dti("wv", [E, E], F32); wo_d = dti("wo", [E, E], F32)
    bq_d = dti("bq", [1, E], F32); bk_d = dti("bk", [1, E], F32)
    bv_d = dti("bv", [1, E], F32); bo_d = dti("bo", [1, E], F32)
    qn_d = dti("qn", [1, D], F32); kn_d = dti("kn", [1, D], F32)
    clipx_d = dti("clipx", [1, 1], F32); clipd_d = dti("clipd", [1, 1], F32)
    wg_d = dti("wg_sh", [3, 128, E], F32)
    wu_d = dti("wu_sh", [3, 128, E], F32)
    wd_d = dti("wd_sh", [128, IF], F32)
    identbf_d = dti("identbf", [128, 128], BF16)
    identf_d = dti("identf", [128, 128], F32)
    onesg_d = dti("onesg", [128, 128], BF16)
    onesf_d = dti("onesf", [1, 128], F32)
    maskrow_d = dti("maskrow", [128, 384], BF16)
    maskTt_d = dti("maskTt", [128, 384], BF16)
    qh_d = dti("qh_ind", [128, 128], BF16)
    rbc_d = dti("rbc_ind", [16, 1024], BF16)
    out_d = nc.dram_tensor("out", [T, E], F32, kind="ExternalOutput").ap()
    h_hbm = nc.dram_tensor("h_hbm", [NI, 128, T], F32).ap()
    attno_hbm = nc.dram_tensor("attno_hbm", [T, E], BF16).ap()
    cc1i = nc.dram_tensor("cc1i", [1, 1], F32).ap()
    cc1o = nc.dram_tensor("cc1o", [1, 1], F32).ap()
    cc2i = nc.dram_tensor("cc2i", [1, 1], F32).ap()
    cc2o = nc.dram_tensor("cc2o", [1, 1], F32).ap()
    agg_i = nc.dram_tensor("agg_i", [3, 128, 8, 128], BF16).ap()
    agg_o = nc.dram_tensor("agg_o", [8, 3, 128, 8, 128], BF16).ap()
    agu_i = nc.dram_tensor("agu_i", [3, 128, 8, 128], BF16).ap()
    agu_o = nc.dram_tensor("agu_o", [8, 3, 128, 8, 128], BF16).ap()
    agd_i = nc.dram_tensor("agd_i", [NI, 128, 128], BF16).ap()
    agd_o = nc.dram_tensor("agd_o", [8, NI, 128, 128], BF16).ap()
    scl_i = nc.dram_tensor("scl_i", [128, 8], F32).ap()
    scl_o = nc.dram_tensor("scl_o", [8, 128, 8], F32).ap()
    wdT_hbm = nc.dram_tensor("wdT_hbm", [NI, 128, E], BF16).ap()
    if debug:
        dbg_xn = nc.dram_tensor("dbg_xn", [128, 8, T], BF16, kind="ExternalOutput").ap()
        dbg_qh = nc.dram_tensor("dbg_qh", [128, 8, T], BF16, kind="ExternalOutput").ap()
        dbg_kh = nc.dram_tensor("dbg_kh", [128, 8, T], BF16, kind="ExternalOutput").ap()
        dbg_v = nc.dram_tensor("dbg_v", [128, 16, NH * 65], BF16, kind="ExternalOutput").ap()
        dbg_sc = nc.dram_tensor("dbg_sc", [1, 8], F32, kind="ExternalOutput").ap()
        dbg_rx = nc.dram_tensor("dbg_rx", [128, 4], F32, kind="ExternalOutput").ap()
        dbg_at = nc.dram_tensor("dbg_at", [T, E], BF16, kind="ExternalOutput").ap()
        dbg_h = nc.dram_tensor("dbg_h", [NI, 128, T], F32, kind="ExternalOutput").ap()
        dbg_st = nc.dram_tensor("dbg_st", [128, 96], F32, kind="ExternalOutput").ap()
        dbg_hq = nc.dram_tensor("dbg_hq", [128, NI, 512], BF16, kind="ExternalOutput").ap()
        dbg_ssw = nc.dram_tensor("dbg_ssw", [128, 24], F32, kind="ExternalOutput").ap()
        dbg_agg = nc.dram_tensor("dbg_agg", [8, 3, 128, 8, 128], BF16, kind="ExternalOutput").ap()
        dbg_sl = nc.dram_tensor("dbg_sl", [128, 4, 512], F32, kind="ExternalOutput").ap()
        dbg_wg0 = nc.dram_tensor("dbg_wg0", [128, 8, 128], BF16, kind="ExternalOutput").ap()
        dbg_ag0 = nc.dram_tensor("dbg_ag0", [128, 8, 128], BF16, kind="ExternalOutput").ap()
        dbg_wp = nc.dram_tensor("dbg_wp", [128, 8, E], BF16, kind="ExternalOutput").ap()

    octx = ExitStack()
    sb = lambda st, n, s, d: st.enter_context(nc.sbuf_tensor(n, s, d))
    pt = lambda n, s, d: octx.enter_context(nc.psum_tensor(n, s, d))

    identbf = sb(octx, "identbf_s", [128, 128], BF16)
    identf = sb(octx, "identf_s", [128, 128], F32)
    onesg = sb(octx, "onesg_s", [128, 128], BF16)
    onesf = sb(octx, "onesf_s", [1, 128], F32)
    maskrow = sb(octx, "maskrow_s", [128, 384], BF16)
    maskTt = sb(octx, "maskTt_s", [128, 384], BF16)
    qh_ind = sb(octx, "qh_ind_s", [128, 128], BF16)
    rbc_ind = sb(octx, "rbc_ind_s", [16, 1024], BF16)
    bqc = sb(octx, "bqc", [128, 8], F32); bkc = sb(octx, "bkc", [128, 8], F32)
    bvrow = sb(octx, "bvrow", [1, E], BF16); borow = sb(octx, "borow", [1, E], BF16)
    qncol = sb(octx, "qncol", [128, 1], F32); kncol = sb(octx, "kncol", [128, 1], F32)
    qn8 = sb(octx, "qn8", [128, 1], F32)
    bqq = sb(octx, "bqq", [128, 8], F32); bkk = sb(octx, "bkk", [128, 8], F32)
    clipx_s = sb(octx, "clipx_s", [1, 1], F32); clipd_s = sb(octx, "clipd_s", [1, 1], F32)
    stat = sb(octx, "stat", [128, 96], F32)
    sumsq = sb(octx, "sumsq", [128, 16], F32)
    rabs = sb(octx, "rabs", [128, 16], F32)
    habs = sb(octx, "habs", [128, 96], F32)
    sc11 = sb(octx, "sc11", [1, 8], F32)
    rxsx = sb(octx, "rxsx", [128, 4], F32)
    acol = sb(octx, "acol", [128, 16], F32); b2col = sb(octx, "b2col", [128, 16], F32)
    sclsb = sb(octx, "sclsb", [128, 8, 8], F32)
    sxswg = sb(octx, "sxswg", [128, 24], F32)
    sxswu = sb(octx, "sxswu", [128, 24], F32)
    vsmall = sb(octx, "vsmall", [128, 2], F32)
    gtr = sb(octx, "gtr", [128, 32], F32)
    epsln = sb(octx, "epsln", [128, 1], F32)
    epsrms = sb(octx, "epsrms", [128, 1], F32)
    xbuf = sb(octx, "xbuf", [128, 2, E], F32)
    mnegf = sb(octx, "mnegf", [128, 2], F32)
    recipz = sb(octx, "recipz", [128, 2], F32)
    ptsb = sb(octx, "ptsb", [128, 2, 384], BF16)
    prow = sb(octx, "prow", [128, 2, 384], BF16)
    attok = sb(octx, "attok", [128, E], BF16)
    afm = sb(octx, "afm", [128, 8, 128], BF16)
    attst = sb(octx, "attst", [128, 2, E], BF16)

    psA0 = pt("psA0", [128, 512], F32)
    psA1 = pt("psA1", [128, 512], F32)
    psB0 = pt("psB0", [128, 512], F32)
    psB1 = pt("psB1", [128, 512], F32)
    psC0 = pt("psC0", [128, 512], F32)
    psC1 = pt("psC1", [128, 512], F32)
    psT = pt("psT", [128, 1024], BF16)
    psU = pt("psU", [128, 512], BF16)
    psA = [psA0, psA1]; psB = [psB0, psB1]; psC = [psC0, psC1]
    psUT = [psU, psT]

    semh = {}
    for n_ in ("LX", "LG", "T", "V", "S", "G", "ST",
               "LA0", "LA1", "LB0", "LB1", "LC0", "LC1", "LD0", "LD1",
               "SA0", "SA1",
               "Zvector", "Zscalar", "Zgpsimd", "Ztensor", "Zsync"):
        semh[n_] = octx.enter_context(nc.semaphore(n_))
    block = octx.enter_context(nc.Block())

    RNAMES = ["psA0", "psA1", "psB0", "psB1", "psC0", "psC1", "psT", "psU",
              "xbuf0", "xbuf1", "xnbuf0", "xnbuf1", "sqscr0", "sqscr1",
              "stat", "sumsq", "rabs", "habs", "sc11", "rxsx", "acol",
              "sclsb", "sxsw", "small", "xnfm0", "xnfm1", "xnfm2", "xnfm3", "qhat", "khat", "vsb",
              "wv", "wo", "prel0", "prel1", "qsqb0", "qsqb1", "sdq", "rqf",
              "rqbf0", "rqbf1",
              "mneg0", "mneg1", "mrow0", "mrow1", "recipz0", "recipz1",
              "ptsb0", "ptsb1", "prow0", "prow1", "attok", "afm",
              "attst0", "attst1",
              "wshard", "wdshard", "wint", "stg", "sclpack",
              "xqi", "xqb0", "xqb1",
              "wgt0", "wgt1", "wut0", "wut1", "silub", "upsb",
              "hbuf0", "hbuf1", "wdt", "hstg0", "hstg1", "hint",
              "outsb0", "outsb1", "tmpadd", "biases", "consts", "tmpdbg"]
    RNAMES += [f"hq{hb}_{g}" for hb in range(2) for g in range(NI)]
    RNAMES += [f"wqk{s}" for s in range(4)]
    DRAM = ["d_hhbm", "d_attno", "d_cc1i", "d_cc1o", "d_cc2i", "d_cc2o",
            "d_aggi", "d_aggo", "d_agui", "d_aguo", "d_agdi", "d_agdo",
            "d_scli", "d_sclo", "d_wdT"]

    def make_res():
        r = {n_: RES() for n_ in RNAMES}
        r.update({n_: RES(dram=True) for n_ in DRAM})
        return r

    ENGS = ("sync", "gpsimd", "tensor", "vector", "scalar")
    EOBJ = {"sync": nc.sync, "gpsimd": nc.gpsimd, "tensor": nc.tensor,
            "vector": nc.vector, "scalar": nc.scalar}
    state = {"c": {n_: 0 for n_ in semh}, "res": make_res(),
             "z": {e_: 0 for e_ in ENGS}}

    def run_phase(phase_fn, barrier=False):
        final_c = final_r = None
        base = dict(state["c"])
        for me in ENGS:
            counts = dict(base)
            res = {k: _res_copy(v) for k, v in state["res"].items()}
            def body(e, me=me, counts=counts, res=res):
                if barrier:
                    for s_ in ("T", "V", "S") + Walk.DMA_SEMS:
                        if base[s_] > 0:
                            e.wait_ge(semh[s_], base[s_])
                w = Walk(semh, counts, res, e, me, res["consts"], state["z"])
                phase_fn(w, res)
            getattr(block, me)(body)
            final_c, final_r = counts, res
        state["c"] = final_c
        state["res"] = final_r

    def _res_copy(b):
        n = RES(b.dram)
        n.w = b.w
        n.r = list(b.r)
        return n

    def ph_loads(w, R):
        loads = [(identbf, identbf_d), (identf, identf_d),
                 (onesg, onesg_d), (maskrow, maskrow_d),
                 (maskTt, maskTt_d), (qh_ind, qh_d), (rbc_ind, rbc_d),
                 (onesf, onesf_d), (clipx_s, clipx_d), (clipd_s, clipd_d)]
        for dst, src in loads:
            w.step("sync", lambda e, dst=dst, src=src:
                   e.dma_start(out=dst[:], in_=src),
                   writes=[R["consts"]], sem="LX", n=16)
        w.step("sync", lambda e: e.dma_start(
            out=bqc[:], in_=bq_d.rearrange("a (c p) -> p (a c)", p=128),
            allow_slow_non_contiguous=True),
            writes=[R["biases"]], sem="LX", n=16)
        w.step("sync", lambda e: e.dma_start(
            out=bkc[:], in_=bk_d.rearrange("a (c p) -> p (a c)", p=128),
            allow_slow_non_contiguous=True),
            writes=[R["biases"]], sem="LX", n=16)
        for dst, src in ((qncol, qn_d), (kncol, kn_d)):
            w.step("sync", lambda e, dst=dst, src=src: e.dma_start(
                out=dst[0:64, :], in_=src.rearrange("a d -> d a"),
                allow_slow_non_contiguous=True),
                writes=[R["biases"]], sem="LX", n=16)
            w.step("sync", lambda e, dst=dst, src=src: e.dma_start(
                out=dst[64:128, :], in_=src.rearrange("a d -> d a"),
                allow_slow_non_contiguous=True),
                writes=[R["biases"]], sem="LX", n=16)
        w.step("gpsimd", lambda e: e.dma_start(out=bvrow[:], in_=bv_d),
               writes=[R["biases"]], sem="LG", n=16)
        w.step("gpsimd", lambda e: e.dma_start(out=borow[:], in_=bo_d),
               writes=[R["biases"]], sem="LG", n=16)

        def _pre(e):
            w.zs(e, e.tensor_scalar(out=qn8[:], in0=qncol[:], scalar1=0.125,
                                    scalar2=None, op0=ALU.mult))
            e.tensor_scalar(out=bqq[:], in0=bqc[:], scalar1=qn8[:], scalar2=None,
                            op0=ALU.mult)
            e.tensor_scalar(out=bkk[:], in0=bkc[:], scalar1=kncol[:], scalar2=None,
                            op0=ALU.mult)
            e.memset(epsln[:], LN_EPS)
            return e.memset(epsrms[:], RMS_EPS)
        w.step("vector", _pre, reads=[R["biases"]], writes=[R["biases"]])

    run_phase(ph_loads)

    xctx = ExitStack()
    xnfm = sb(xctx, "xnfm", [128, 8, T], BF16)
    lctx = ExitStack()
    qhat = sb(lctx, "qhat", [128, 8, T], BF16)
    khat = sb(lctx, "khat", [128, 8, T], BF16)
    vsb = sb(lctx, "vsb", [128, 16, NH * 65], BF16)

    qctx = ExitStack()
    wshard = sb(qctx, "wshard", [128, 3, E], F32)
    wdshard = sb(qctx, "wdshard", [128, IF], F32)
    wi32 = sb(qctx, "wi32", [128, E], I32)
    wbf = sb(qctx, "wbf", [128, E], BF16)
    stg = sb(qctx, "stg", [128, 8, 128], BF16)
    sclpack = sb(qctx, "sclpack", [128, 8], F32)
    absq = sb(qctx, "absq", [128, 8], F32)
    rq127 = sb(qctx, "rq127", [128, 8], F32)

    def ph_quant(w, R):
        w.serial = SERIAL_Q
        w.step("sync", lambda e: e.dma_start(
            out=wshard[:], in_=wg_d.rearrange("j (a p) e -> p j (a e)", p=128)),
            writes=[R["wshard"]], sem="LB0", n=16)
        w.step("sync", lambda e: e.dma_start(out=wdshard[:], in_=wd_d),
               writes=[R["wdshard"]], sem="LB0", n=16)
        for grp in range(2):
            agdst = agg_i if grp == 0 else agu_i
            dram_i = R["d_aggi"] if grp == 0 else R["d_agui"]
            if grp == 1:
                w.step("sync", lambda e: e.dma_start(
                    out=wshard[:],
                    in_=wu_d.rearrange("j (a p) e -> p j (a e)", p=128)),
                    writes=[R["wshard"]], sem="LB0", n=16)
            for j in range(3):
                k = grp * 3 + j
                def _q(e, j=j, k=k):
                    w.zs(e, e.tensor_reduce(out=absq[:, k:k+1], in_=wshard[:, j, :],
                                            axis=AX.X, op=ALU.max,
                                            apply_absolute_value=True))
                    w.zs(e, e.reciprocal(out=rq127[:, k:k+1], in_=absq[:, k:k+1]))
                    e.tensor_scalar(out=sclpack[:, k:k+1], in0=absq[:, k:k+1],
                                    scalar1=1.0 / QMAX, scalar2=None, op0=ALU.mult)
                    e.tensor_scalar(out=wi32[:], in0=wshard[:, j, :],
                                    scalar1=rq127[:, k:k+1], scalar2=QMAX,
                                    op0=ALU.mult, op1=ALU.mult)
                    return e.tensor_copy(out=wbf[:], in_=wi32[:])
                w.step("vector", _q, reads=[R["wshard"]],
                       writes=[R["wint"], R["sclpack"]])
                def _t(e):
                    for ec in range(8):
                        i = e.transpose(out=psT[:, 128*ec:128*(ec+1)],
                                        in_=wbf[:, 128*ec:128*(ec+1)],
                                        identity=identbf[:])
                    return i
                w.step("tensor", _t, reads=[R["wint"]], writes=[R["psT"]])
                w.step("scalar", lambda e: e.copy(
                    out=stg[:], in_=psT[:].rearrange("p (ec t) -> p ec t", ec=8)),
                    reads=[R["psT"]], writes=[R["stg"]])
                w.step("gpsimd", lambda e, agdst=agdst, j=j: e.dma_start(
                    out=agdst[j], in_=stg[:]),
                    reads=[R["stg"]], writes=[dram_i], sem="SA0", n=16)
        def _qd(e):
            w.zs(e, e.tensor_reduce(out=absq[:, 6:7], in_=wdshard[:], axis=AX.X,
                                    op=ALU.max, apply_absolute_value=True))
            w.zs(e, e.reciprocal(out=rq127[:, 6:7], in_=absq[:, 6:7]))
            return e.tensor_scalar(out=sclpack[:, 6:7], in0=absq[:, 6:7],
                                   scalar1=1.0 / QMAX, scalar2=None, op0=ALU.mult)
        w.step("vector", _qd, reads=[R["wdshard"]], writes=[R["sclpack"]])
        for b in range(NI):
            def _qb(e, b=b):
                w.zs(e, e.tensor_scalar(out=wi32[:, 0:128],
                                        in0=wdshard[:, 128*b:128*(b+1)],
                                        scalar1=rq127[:, 6:7], scalar2=QMAX,
                                        op0=ALU.mult, op1=ALU.mult))
                return e.tensor_scalar(out=wbf[:, 0:128], in0=wi32[:, 0:128],
                                       scalar1=sclpack[:, 6:7], scalar2=None,
                                       op0=ALU.mult)
            w.step("vector", _qb, reads=[R["wdshard"], R["sclpack"]],
                   writes=[R["wint"]])
            w.step("tensor", lambda e: e.transpose(
                out=psT[:, 0:128], in_=wbf[:, 0:128], identity=identbf[:]),
                reads=[R["wint"]], writes=[R["psT"]])
            w.step("scalar", lambda e: e.copy(out=stg[:, 0, :], in_=psT[:, 0:128]),
                   reads=[R["psT"]], writes=[R["stg"]])
            w.step("gpsimd", lambda e, b=b: e.dma_start(out=agd_i[b], in_=stg[:, 0, :]),
                   reads=[R["stg"]], writes=[R["d_agdi"]], sem="SA0", n=16)
        w.step("gpsimd", lambda e: e.dma_start(out=scl_i, in_=sclpack[:]),
               reads=[R["sclpack"]], writes=[R["d_scli"]], sem="SA0", n=16)
        for ins, outs, ri, ro in ((agg_i, agg_o, "d_aggi", "d_aggo"),
                                  (agu_i, agu_o, "d_agui", "d_aguo"),
                                  (agd_i, agd_o, "d_agdi", "d_agdo"),
                                  (scl_i, scl_o, "d_scli", "d_sclo")):
            w.step("gpsimd", lambda e, ins=ins, outs=outs: e.collective_compute(
                "AllGather", ALU.bypass, replica_groups=[list(range(8))],
                ins=[ins.opt()], outs=[outs.opt()]),
                reads=[R[ri]], writes=[R[ro]], sem="G")
        for ro in ("d_aggo", "d_aguo", "d_agdo", "d_sclo"):
            R[ro].w = ("G", w.c["G"])
        for r8 in range(8):
            w.step("gpsimd", lambda e, r8=r8: e.dma_start(
                out=wdT_hbm[:, :, 128*r8:128*(r8+1)], in_=agd_o[r8]),
                reads=[R["d_agdo"]], writes=[R["d_wdT"]], sem="LG", n=16)
        w.step("gpsimd", lambda e: e.dma_start(
            out=sclsb[:], in_=scl_o.rearrange("r p c -> p r c")),
            reads=[R["d_sclo"]], writes=[R["sclsb"]], sem="LD1", n=16)

    run_phase(ph_quant)

    sqscr = sb(qctx, "sqscr", [128, 2, E], F32)
    xnbuf = sb(qctx, "xnbuf", [128, 2, E], BF16)

    def ph_p0(w, R):
        w.serial = SERIAL_P0
        w.step("vector", lambda e: e.memset(
            vsb[:].rearrange("p tc (h d) -> p tc h d", d=65)[:, :, :, 64:65], 1.0),
            writes=[R["vsb"]])
        for tc in range(NT):
            par = tc % 2
            xb = R[f"xbuf{par}"]
            w.step("sync", lambda e, tc=tc, par=par: e.dma_start(
                out=xbuf[:, par, :], in_=x_d[128*tc:128*(tc+1), :]),
                writes=[xb], sem=f"LA{par}", n=16)
            def _red(e, tc=tc, par=par):
                e.reduce_sum(out=stat[:, tc:tc+1], in_=xbuf[:, par, :], axis=AX.X)
                e.reduce_max(out=stat[:, 16+tc:17+tc], in_=xbuf[:, par, :], axis=AX.X)
                return e.tensor_reduce(out=stat[:, 32+tc:33+tc], in_=xbuf[:, par, :],
                                       axis=AX.X, op=ALU.min)
            w.step("vector", _red, reads=[xb], writes=[R["stat"]])
            w.step("scalar", lambda e, tc=tc, par=par: e.activation(
                out=sqscr[:, par, :], in_=xbuf[:, par, :], func=AF.Square,
                accum_out=sumsq[:, tc:tc+1]),
                reads=[xb], writes=[R[f"sqscr{par}"], R["sumsq"]])
            def _st(e, tc=tc):
                w.zs(e, e.tensor_scalar(out=stat[:, 48+tc:49+tc], in0=stat[:, tc:tc+1],
                                        scalar1=1.0/E, scalar2=None, op0=ALU.mult))
                w.zs(e, e.tensor_scalar(out=vsmall[:, 0:1], in0=sumsq[:, tc:tc+1],
                                        scalar1=1.0/E, scalar2=None, op0=ALU.mult))
                w.zs(e, e.tensor_tensor(out=vsmall[:, 1:2], in0=stat[:, 48+tc:49+tc],
                                        in1=stat[:, 48+tc:49+tc], op=ALU.mult))
                return e.tensor_tensor(out=vsmall[:, 0:1], in0=vsmall[:, 0:1],
                                       in1=vsmall[:, 1:2], op=ALU.subtract)
            w.step("vector", _st, reads=[R["sumsq"], R["stat"]],
                   writes=[R["small"], R["stat"]])
            w.step("scalar", lambda e: e.activation(
                out=vsmall[:, 1:2], in_=vsmall[:, 0:1], func=AF.Sqrt,
                bias=epsln[:], scale=1.0),
                reads=[R["small"]], writes=[R["small"]])
            def _rr(e, tc=tc):
                w.zs(e, e.reciprocal(out=stat[:, 64+tc:65+tc], in_=vsmall[:, 1:2]))
                return e.tensor_scalar(out=stat[:, 80+tc:81+tc],
                                       in0=stat[:, 48+tc:49+tc],
                                       scalar1=stat[:, 64+tc:65+tc], scalar2=-1.0,
                                       op0=ALU.mult, op1=ALU.mult)
            w.step("vector", _rr, reads=[R["small"]], writes=[R["stat"]])
            w.step("scalar", lambda e, tc=tc, par=par: e.activation(
                out=xnbuf[:, par, :], in_=xbuf[:, par, :], func=AF.Identity,
                scale=stat[:, 64+tc:65+tc], bias=stat[:, 80+tc:81+tc]),
                reads=[xb, R["stat"]], writes=[R[f"xnbuf{par}"]])
            def _ra(e, tc=tc):
                w.zs(e, e.tensor_tensor(out=vsmall[:, 0:1], in0=stat[:, 16+tc:17+tc],
                                        in1=stat[:, 48+tc:49+tc], op=ALU.subtract))
                w.zs(e, e.tensor_tensor(out=rabs[:, tc:tc+1], in0=stat[:, 48+tc:49+tc],
                                        in1=stat[:, 32+tc:33+tc], op=ALU.subtract))
                w.zs(e, e.tensor_tensor(out=rabs[:, tc:tc+1], in0=rabs[:, tc:tc+1],
                                        in1=vsmall[:, 0:1], op=ALU.max))
                return e.tensor_scalar(out=rabs[:, tc:tc+1], in0=rabs[:, tc:tc+1],
                                       scalar1=stat[:, 64+tc:65+tc], scalar2=None,
                                       op0=ALU.mult)
            w.step("vector", _ra, reads=[R["stat"]], writes=[R["rabs"], R["small"]])
            def _tr(e, par=par):
                for s8 in range(8):
                    i = e.transpose(out=psT[:, 128*s8:128*(s8+1)],
                                    in_=xnbuf[:, par, 128*s8:128*(s8+1)],
                                    identity=identbf[:])
                return i
            w.step("tensor", _tr, reads=[R[f"xnbuf{par}"]], writes=[R["psT"]])
            w.step("scalar", lambda e, tc=tc: e.copy(
                out=xnfm[:, :, 128*tc:128*(tc+1)],
                in_=psT[:].rearrange("p (ec t) -> p ec t", ec=8)),
                reads=[R["psT"]], writes=[R[f"xnfm{tc//4}"]])
        def _gr(e):
            e.memset(gtr[:], 0.0)
            return e.tensor_reduce(out=gtr[:, 0:1], in_=rabs[:], axis=AX.X,
                                   op=ALU.max)
        w.step("vector", _gr, reads=[R["rabs"]], writes=[R["small"]])
        w.step("tensor", lambda e: e.transpose(out=psC1[0:32, 0:128],
                                               in_=gtr[:, 0:32], identity=identf[:]),
               reads=[R["small"]], writes=[R["psC1"]])
        w.step("vector", lambda e: e.reduce_max(out=sc11[:, 0:1],
                                                in_=psC1[0:1, 0:128], axis=AX.X),
               reads=[R["psC1"]], writes=[R["sc11"]])
        w.step("gpsimd", lambda e: e.dma_start(out=cc1i, in_=sc11[:, 0:1]),
               reads=[R["sc11"]], writes=[R["d_cc1i"]], sem="LC1", n=16)
        w.step("gpsimd", lambda e: e.collective_compute(
            "AllReduce", ALU.max, replica_groups=[list(range(8))],
            ins=[cc1i.opt()], outs=[cc1o.opt()]),
            reads=[R["d_cc1i"]], writes=[R["d_cc1o"]], sem="G")
        w.step("gpsimd", lambda e: e.dma_start(out=sc11[:, 0:1], in_=cc1o),
               reads=[R["d_cc1o"]], writes=[R["sc11"]], sem="LD0", n=16)
        w.step("scalar", lambda e: e.activation(out=sc11[:, 1:2], in_=clipx_s[:],
                                                func=AF.Sigmoid),
               reads=[R["sc11"]], writes=[R["sc11"]])
        def _sx(e):
            w.zs(e, e.tensor_tensor(out=sc11[:, 2:3], in0=sc11[:, 1:2],
                                    in1=sc11[:, 0:1], op=ALU.mult))
            w.zs(e, e.reciprocal(out=sc11[:, 3:4], in_=sc11[:, 2:3]))
            w.zs(e, e.tensor_scalar(out=sc11[:, 3:4], in0=sc11[:, 3:4], scalar1=QMAX,
                                    scalar2=None, op0=ALU.mult))
            return e.tensor_scalar(out=sc11[:, 4:5], in0=sc11[:, 2:3],
                                   scalar1=1.0/QMAX, scalar2=None, op0=ALU.mult)
        w.step("vector", _sx, reads=[R["sc11"]], writes=[R["sc11"]])
        def _bc(e):
            e.matmul(out=psC0[:, 0:1], lhsT=onesf[:], rhs=sc11[:, 3:4],
                     start=True, stop=False)
            return e.matmul(out=psC0[:, 1:2], lhsT=onesf[:], rhs=sc11[:, 4:5],
                            start=False, stop=True)
        w.step("tensor", _bc, reads=[R["sc11"]], writes=[R["psC0"]])
        w.step("vector", lambda e: e.tensor_copy(out=rxsx[:, 0:2], in_=psC0[:, 0:2]),
               reads=[R["psC0"]], writes=[R["rxsx"]])
        def _ssw(e):
            e.tensor_scalar(out=sxswg[:].rearrange("p (r c) -> p r c", c=3),
                            in0=sclsb[:, :, 0:3],
                            scalar1=rxsx[:, 1:2], scalar2=None, op0=ALU.mult)
            return e.tensor_scalar(out=sxswu[:].rearrange("p (r c) -> p r c", c=3),
                                   in0=sclsb[:, :, 3:6],
                                   scalar1=rxsx[:, 1:2], scalar2=None, op0=ALU.mult)
        w.step("vector", _ssw, reads=[R["sclsb"], R["rxsx"]], writes=[R["sxsw"]])

    run_phase(ph_p0)
    qctx.close()

    p1ctx = ExitStack()
    wpbuf = sb(p1ctx, "wpbuf", [128, 8, E], BF16)
    prel = sb(p1ctx, "prel", [128, 8, 512], BF16)
    qsqb = sb(p1ctx, "qsqb", [128, 2, 512], BF16)
    sdq = sb(p1ctx, "sdq", [16, 512], F32)
    rqbf = sb(p1ctx, "rqbf", [16, 512], BF16)
    wosb = sb(p1ctx, "wosb", [128, 8, E], BF16)

    def ph_p1(w, R):
        w.serial = SERIAL_P1
        pend = []

        def flush_pend():
            while pend:
                pend.pop(0)()

        def unit(ti, pj):
            t0 = 512 * ti
            flush_pend()
            bcols = bqc if pj == 0 else bkc
            ncol = qn8 if pj == 0 else kncol
            bnn = bqq if pj == 0 else bkk
            pssum, sumR = (psC0, R["psC0"]) if pj == 0 else (psC1, R["psC1"])
            dst, dstR = (qhat, R["qhat"]) if pj == 0 else (khat, R["khat"])
            for ec in range(8):
                pa = ec % 2
                def _mm(e, ec=ec, pa=pa):
                    for ei in range(8):
                        i = e.matmul(out=psA[pa][:],
                                     lhsT=wpbuf[:, ei, 128*ec:128*(ec+1)],
                                     rhs=xnfm[:, ei, t0:t0+512],
                                     start=(ei == 0), stop=(ei == 7))
                    return i
                w.step("tensor", _mm, reads=[R["wv"], R[f"xnfm{ti}"]],
                       writes=[R[f"psA{pa}"]])
                def _ep(e, ec=ec, pa=pa):
                    e.activation(out=qsqb[:, pa, :], in_=psA[pa][:], func=AF.Square,
                                 bias=bcols[:, ec:ec+1], scale=1.0)
                    return e.activation(out=prel[:, ec, :], in_=psA[pa][:],
                                        func=AF.Identity, scale=ncol[:],
                                        bias=bnn[:, ec:ec+1])
                w.step("scalar", _ep, reads=[R[f"psA{pa}"], R["biases"]],
                       writes=[R[f"qsqb{pa}"], R["prel0"]])
                if ec >= 1:
                    hs(ti, pj, ec - 1, pssum, sumR)
            hs(ti, pj, 7, pssum, sumR)
            w.step("scalar", lambda e: e.activation(
                out=sdq[:], in_=pssum[0:16, :], func=AF.Sqrt,
                scale=1.0 / D, bias=epsrms[0:16, :]),
                reads=[sumR], writes=[R["sdq"]])
            def _rc(e):
                w.zs(e, e.reciprocal(out=sdq[:], in_=sdq[:]))
                return e.tensor_copy(out=rqbf[:], in_=sdq[:])
            w.step("vector", _rc, reads=[R["sdq"]], writes=[R["sdq"], R["rqbf0"]])
            for ec in range(8):
                pend.append(lambda ec=ec, dst=dst, dstR=dstR, t0=t0:
                            fa(ec, dst, dstR, t0))

        def hs(ti, pj, ec, pssum, sumR):
            pa = ec % 2
            w.step("tensor", lambda e: e.matmul(
                out=pssum[0:16, :], lhsT=qh_ind[:, 16*ec:16*(ec+1)],
                rhs=qsqb[:, pa, :], start=(ec == 0), stop=(ec == 7)),
                reads=[R[f"qsqb{pa}"]], writes=[sumR])

        def fa(ec, dst, dstR, t0):
            pb = ec % 2
            w.step("tensor", lambda e: e.matmul(
                out=psB[pb][:], lhsT=rbc_ind[:, 128*ec:128*(ec+1)],
                rhs=rqbf[:], start=True, stop=True),
                reads=[R["rqbf0"]], writes=[R[f"psB{pb}"]])
            w.step("vector", lambda e: e.scalar_tensor_tensor(
                out=dst[:, ec, t0:t0+512], in0=prel[:, ec, :],
                scalar=1.0, in1=psB[pb][:], op0=ALU.mult, op1=ALU.mult),
                reads=[R["prel0"], R[f"psB{pb}"]], writes=[dstR])

        for pj in range(2):
            wsrc = wq_d if pj == 0 else wk_d
            w.step("gpsimd", lambda e, wsrc=wsrc: e.dma_start(
                out=wpbuf[:], in_=wsrc.rearrange("(ei p) eo -> p ei eo", p=128)),
                writes=[R["wv"]], sem="LG", n=16)
            for ti in range(NTI):
                unit(ti, pj)
        flush_pend()
        w.step("gpsimd", lambda e: e.dma_start(
            out=wpbuf[:], in_=wv_d.rearrange("(ei p) eo -> p ei eo", p=128)),
            writes=[R["wv"]], sem="LG", n=16)
        for tc in range(NT):
            for half in range(2):
                pa = (2 * tc + half) % 2
                def _vm(e, tc=tc, half=half, pa=pa):
                    for ei in range(8):
                        e.matmul(out=psA[pa][:], lhsT=xnfm[:, ei, 128*tc:128*(tc+1)],
                                 rhs=wpbuf[:, ei, 512*half:512*(half+1)],
                                 start=(ei == 0), stop=False)
                    return e.matmul(out=psA[pa][:], lhsT=onesg[0:1, :],
                                    rhs=bvrow[:, 512*half:512*(half+1)],
                                    start=False, stop=True)
                w.step("tensor", _vm, reads=[R[f"xnfm{tc//4}"], R["wv"], R["biases"]],
                       writes=[R[f"psA{pa}"]])
                w.step("scalar", lambda e, tc=tc, half=half, pa=pa: e.copy(
                    out=vsb[:, tc, :].rearrange("p (h d) -> p h d", d=65)
                        [:, 8*half:8*(half+1), 0:64],
                    in_=psA[pa][:].rearrange("p (h d) -> p h d", d=64)),
                    reads=[R[f"psA{pa}"]], writes=[R["vsb"]])

    run_phase(ph_p1, barrier=True)
    if debug:
        def ph_dbg1b(w, R):
            w.step("sync", lambda e: e.dma_start(out=dbg_wp, in_=wpbuf[:]),
                   reads=[R["wv"]], sem="ST", n=16)
            w.step("sync", lambda e: e.dma_start(out=dbg_qh, in_=qhat[:]),
                   reads=[R["qhat"]], sem="ST", n=16)
            w.step("sync", lambda e: e.dma_start(out=dbg_kh, in_=khat[:]),
                   reads=[R["khat"]], sem="ST", n=16)
            w.step("sync", lambda e: e.dma_start(out=dbg_v, in_=vsb[:]),
                   reads=[R["vsb"]], sem="ST", n=16)
        run_phase(ph_dbg1b)


    def ph_p4(w, R):
        w.serial = SERIAL_P4
        w.step("gpsimd", lambda e: e.dma_start(
            out=wosb[:], in_=wo_d.rearrange("(ei p) eo -> p ei eo", p=128)),
            writes=[R["wo"]], sem="LC0", n=16)

        def geom(qi):
            nw = min(qi + 1, 3)
            return nw, qi - nw + 1, 128 * nw, 384 - 128 * nw

        def stA(qi, h):
            k = qi * NH + h
            nw, wc0, Wd, moff = geom(qi)
            ch, rb, pa = h // 2, 64 * (h % 2), k % 2
            def _row(e):
                e.matmul(out=psA[pa][:, 0:Wd],
                         lhsT=qhat[rb:rb+64, ch, 128*qi:128*(qi+1)],
                         rhs=khat[rb:rb+64, ch, 128*wc0:128*wc0+Wd],
                         start=True, stop=False)
                return e.matmul(out=psA[pa][:, 0:Wd], lhsT=identbf[:],
                                rhs=maskrow[:, moff:384], start=False, stop=True)
            w.step("tensor", _row, reads=[R["qhat"], R["khat"]],
                   writes=[R[f"psA{pa}"]])

        def stM(qi, h):
            k = qi * NH + h
            nw, wc0, Wd, moff = geom(qi)
            pa = k % 2
            w.step("vector", lambda e: e.reduce_max(
                out=mnegf[:, pa:pa+1], in_=psA[pa][:, 0:Wd], axis=AX.X, negate=True),
                reads=[R[f"psA{pa}"]], writes=[R[f"mneg{pa}"]])

        def stE(qi, h):
            k = qi * NH + h
            nw, wc0, Wd, moff = geom(qi)
            pa = k % 2
            w.step("scalar", lambda e: e.activation(
                out=prow[:, pa, 0:Wd], in_=psA[pa][:, 0:Wd], func=AF.Exp,
                bias=mnegf[:, pa:pa+1], scale=1.0),
                reads=[R[f"psA{pa}"], R[f"mneg{pa}"]], writes=[R[f"prow{pa}"]])

        def stT(qi, h):
            k = qi * NH + h
            nw, wc0, Wd, moff = geom(qi)
            pa = k % 2
            def _tt(e):
                for c in range(nw):
                    i = e.transpose(out=psUT[pa][:, 128*c:128*(c+1)],
                                    in_=prow[:, pa, 128*c:128*(c+1)],
                                    identity=identbf[:])
                return i
            w.step("tensor", _tt, reads=[R[f"prow{pa}"]],
                   writes=[R["psU"] if pa == 0 else R["psT"]])

        def stPC(qi, h):
            k = qi * NH + h
            nw, wc0, Wd, moff = geom(qi)
            pa = k % 2
            w.step("scalar", lambda e: e.copy(
                out=ptsb[:, pa, 0:Wd], in_=psUT[pa][:, 0:Wd]),
                reads=[R["psU"] if pa == 0 else R["psT"]], writes=[R[f"ptsb{pa}"]])

        def stPV(qi, h):
            k = qi * NH + h
            nw, wc0, Wd, moff = geom(qi)
            pa = k % 2
            def _pv(e):
                for c in range(nw):
                    i = e.matmul(out=psC[pa][:, 0:65],
                                 lhsT=ptsb[:, pa, 128*c:128*(c+1)],
                                 rhs=vsb[:, wc0+c, 65*h:65*(h+1)],
                                 start=(c == 0), stop=(c == nw - 1))
                return i
            w.step("tensor", _pv, reads=[R[f"ptsb{pa}"], R["vsb"]],
                   writes=[R[f"psC{pa}"]])

        def stRZ(qi, h):
            k = qi * NH + h
            pa = k % 2
            w.step("vector", lambda e: e.reciprocal(
                out=recipz[:, pa:pa+1], in_=psC[pa][:, 64:65]),
                reads=[R[f"psC{pa}"]], writes=[R[f"recipz{pa}"]])

        def stEV(qi, h):
            k = qi * NH + h
            pa = k % 2
            w.step("scalar", lambda e: e.activation(
                out=attok[:, 64*h:64*(h+1)], in_=psC[pa][:, 0:64],
                func=AF.Copy, scale=recipz[:, pa:pa+1]),
                reads=[R[f"psC{pa}"], R[f"recipz{pa}"]], writes=[R["attok"]])

        def tail(qi):
            def _tr(e):
                for s8 in range(8):
                    i = e.transpose(out=psT[:, 128*s8:128*(s8+1)],
                                    in_=attok[:, 128*s8:128*(s8+1)],
                                    identity=identbf[:])
                return i
            w.step("tensor", _tr, reads=[R["attok"]], writes=[R["psT"]])
            w.step("scalar", lambda e: e.copy(
                out=afm[:], in_=psT[:].rearrange("p (ec t) -> p ec t", ec=8)),
                reads=[R["psT"]], writes=[R["afm"]])
            def _wo(e):
                for half in range(2):
                    for s8 in range(8):
                        e.matmul(out=psB[half][:], lhsT=afm[:, s8, :],
                                 rhs=wosb[:, s8, 512*half:512*(half+1)],
                                 start=(s8 == 0), stop=False)
                    i = e.matmul(out=psB[half][:], lhsT=onesg[0:1, :],
                                 rhs=borow[:, 512*half:512*(half+1)],
                                 start=False, stop=True)
                return i
            w.step("tensor", _wo, reads=[R["afm"], R["wo"], R["biases"]],
                   writes=[R["psB0"], R["psB1"]])
            pq = qi % 2
            def _ev(e):
                e.activation(out=attst[:, pq, 0:512], in_=psB[0][:], func=AF.Copy)
                return e.activation(out=attst[:, pq, 512:1024], in_=psB[1][:],
                                    func=AF.Copy)
            w.step("scalar", _ev, reads=[R["psB0"], R["psB1"]],
                   writes=[R[f"attst{pq}"]])
            w.step("sync", lambda e: e.dma_start(
                out=attno_hbm[128*qi:128*(qi+1), :], in_=attst[:, pq, :]),
                reads=[R[f"attst{pq}"]], writes=[R["d_attno"]], sem=f"SA{pq}",
                n=16)

        KK = NT * NH
        def hk(k):
            return k // NH, k % NH
        for k in range(KK + 3):
            if k < KK:
                qi, h = hk(k)
                stA(qi, h)
                stM(qi, h)
                stE(qi, h)
            if 1 <= k < KK + 1:
                qi, h = hk(k - 1)
                stT(qi, h)
                stPC(qi, h)
            if 2 <= k < KK + 2:
                qi, h = hk(k - 2)
                stPV(qi, h)
                stRZ(qi, h)
            if 3 <= k < KK + 3:
                qi, h = hk(k - 3)
                stEV(qi, h)
                if h == NH - 1:
                    tail(qi)

    run_phase(ph_p4)
    if debug:
        def ph_dbg4(w, R):
            w.step("sync", lambda e: e.dma_start(out=dbg_at, in_=attno_hbm),
                   reads=[R["d_attno"]], sem="ST", n=16)
        run_phase(ph_dbg4)
    p1ctx.close()
    lctx.close()

    pwctx = ExitStack()
    wdt = sb(pwctx, "wdt", [128, NI, E], BF16)
    p2ctx = ExitStack()
    xqi = sb(p2ctx, "xqi", [128, E], I32)
    xqb = sb(p2ctx, "xqb", [128, 2, E], BF16)

    def ph_p2(w, R):
        w.serial = SERIAL_P2
        def _ab(e):
            e.tensor_scalar(out=acol[:], in0=stat[:, 64:80], scalar1=rxsx[:, 0:1],
                            scalar2=None, op0=ALU.mult)
            return e.tensor_scalar(out=b2col[:], in0=stat[:, 80:96],
                                   scalar1=rxsx[:, 0:1], scalar2=None, op0=ALU.mult)
        w.step("vector", _ab, reads=[R["stat"], R["rxsx"]], writes=[R["acol"]])
        for tc in range(NT):
            par = tc % 2
            xb = R[f"xbuf{par}"]
            w.step("sync", lambda e, tc=tc, par=par: e.dma_start(
                out=xbuf[:, par, :], in_=x_d[128*tc:128*(tc+1), :]),
                writes=[xb], sem=f"LA{par}", n=16)
            def _q(e, tc=tc, par=par):
                w.zs(e, e.tensor_scalar(out=xqi[:], in0=xbuf[:, par, :],
                                        scalar1=acol[:, tc:tc+1],
                                        scalar2=b2col[:, tc:tc+1],
                                        op0=ALU.mult, op1=ALU.add))
                return e.tensor_scalar(out=xqb[:, par, :], in0=xqi[:],
                                       scalar1=-128.0, scalar2=127.0,
                                       op0=ALU.max, op1=ALU.min)
            w.step("vector", _q, reads=[xb, R["acol"]],
                   writes=[R["xqi"], R[f"xqb{par}"]])
            def _tr(e, par=par):
                for s8 in range(8):
                    i = e.transpose(out=psT[:, 128*s8:128*(s8+1)],
                                    in_=xqb[:, par, 128*s8:128*(s8+1)],
                                    identity=identbf[:])
                return i
            w.step("tensor", _tr, reads=[R[f"xqb{par}"]], writes=[R["psT"]])
            w.step("scalar", lambda e, tc=tc: e.copy(
                out=xnfm[:, :, 128*tc:128*(tc+1)],
                in_=psT[:].rearrange("p (ec t) -> p ec t", ec=8)),
                reads=[R["psT"]], writes=[R[f"xnfm{tc//4}"]])

    run_phase(ph_p2, barrier=True)

    p3ctx = ExitStack()
    wgt = sb(p3ctx, "wgt", [128, 2, 8, 128], BF16)
    wut = sb(p3ctx, "wut", [128, 2, 8, 128], BF16)
    silub = sb(p3ctx, "silub", [128, 512], F32)
    dbgslb = sb(p3ctx, "dbgslb", [128, 4, 512], F32) if debug else None
    upsb = sb(p3ctx, "upsb", [128, 512], F32)
    hbuf = sb(p3ctx, "hbuf", [128, 2, 512], F32)

    def ph_p3(w, R):
        w.serial = SERIAL_P3
        for c4 in range(4):
            w.step("gpsimd", lambda e, c4=c4: e.dma_start(
                out=wdt[:, :, 256*c4:256*(c4+1)],
                in_=wdT_hbm[:, :, 256*c4:256*(c4+1)].rearrange("g p e -> p g e")),
                reads=[R["d_wdT"]], writes=[R["wdt"]], sem="LG", n=16)
        for g in range(NI):
            r8, j = g // 3, g % 3
            gp = g % 2
            w.step("gpsimd", lambda e, r8=r8, j=j, gp=gp: e.dma_start(
                out=wgt[:, gp, :, :], in_=agg_o[r8, j]),
                reads=[R["d_aggo"]], writes=[R[f"wgt{gp}"]], sem=f"LB{gp}", n=16)
            w.step("gpsimd", lambda e, r8=r8, j=j, gp=gp: e.dma_start(
                out=wut[:, gp, :, :], in_=agu_o[r8, j]),
                reads=[R["d_aguo"]], writes=[R[f"wut{gp}"]], sem=f"LB{gp}", n=16)
            for ti in range(NTI):
                t0 = 512 * ti
                pg = ti % 2
                def _mm(e, gp=gp, pg=pg, t0=t0):
                    for ei in range(8):
                        e.matmul(out=psA[pg][:], lhsT=wgt[:, gp, ei, :],
                                 rhs=xnfm[:, ei, t0:t0+512],
                                 start=(ei == 0), stop=(ei == 7))
                    for ei in range(8):
                        i = e.matmul(out=psB[pg][:], lhsT=wut[:, gp, ei, :],
                                     rhs=xnfm[:, ei, t0:t0+512],
                                     start=(ei == 0), stop=(ei == 7))
                    return i
                w.step("tensor", _mm,
                       reads=[R[f"wgt{gp}"], R[f"wut{gp}"], R[f"xnfm{ti}"]],
                       writes=[R[f"psA{pg}"], R[f"psB{pg}"]])
                def _ep(e, g=g, pg=pg):
                    e.activation(out=silub[:], in_=psA[pg][:], func=AF.Silu,
                                 scale=sxswg[:, g:g+1])
                    return e.activation(out=upsb[:], in_=psB[pg][:], func=AF.Copy,
                                        scale=sxswu[:, g:g+1])
                w.step("scalar", _ep,
                       reads=[R[f"psA{pg}"], R[f"psB{pg}"], R["sxsw"]],
                       writes=[R["silub"], R["upsb"]])
                def _h(e, g=g, ti=ti, pg=pg):
                    w.zs(e, e.tensor_tensor(out=hbuf[:, pg, :], in0=silub[:],
                                            in1=upsb[:], op=ALU.mult))
                    return e.tensor_reduce(out=habs[:, 4*g+ti:4*g+ti+1],
                                           in_=hbuf[:, pg, :], axis=AX.X, op=ALU.max,
                                           apply_absolute_value=True)
                if debug and g == 0 and ti == 0:
                    def _dsl(e):
                        e.tensor_copy(out=dbgslb[:, 0, :], in_=silub[:])
                        e.tensor_copy(out=dbgslb[:, 1, :], in_=upsb[:])
                        e.tensor_copy(out=dbgslb[:, 2, :], in_=psA[0][:])
                        return e.tensor_copy(out=dbgslb[:, 3, :], in_=psB[0][:])
                    w.step("vector", _dsl,
                           reads=[R["silub"], R["upsb"], R["psA0"], R["psB0"]],
                           writes=[R["tmpdbg"]])
                    w.step("sync", lambda e: e.dma_start(out=dbg_sl, in_=dbgslb[:]),
                           reads=[R["tmpdbg"]], sem="ST", n=16)
                    w.step("sync", lambda e: e.dma_start(out=dbg_wg0, in_=wgt[:, 0, :, :]),
                           reads=[R["wgt0"]], sem="ST", n=16)
                    w.step("sync", lambda e: e.dma_start(out=dbg_ag0, in_=agg_o[0, 0]),
                           reads=[R["d_aggo"]], sem="ST", n=16)
                w.step("vector", _h, reads=[R["silub"], R["upsb"]],
                       writes=[R[f"hbuf{pg}"], R["habs"]])
                w.step("sync", lambda e, g=g, t0=t0, pg=pg: e.dma_start(
                    out=h_hbm[g, :, t0:t0+512], in_=hbuf[:, pg, :]),
                    reads=[R[f"hbuf{pg}"]], writes=[R["d_hhbm"]], sem=f"SA{pg}",
                    n=16)
        def _hr(e):
            e.memset(gtr[:], 0.0)
            return e.tensor_reduce(out=gtr[:, 0:1], in_=habs[:, 0:88], axis=AX.X,
                                   op=ALU.max)
        w.step("vector", _hr, reads=[R["habs"]], writes=[R["small"]])
        w.step("tensor", lambda e: e.transpose(out=psC1[0:32, 0:128],
                                               in_=gtr[:, 0:32], identity=identf[:]),
               reads=[R["small"]], writes=[R["psC1"]])
        w.step("vector", lambda e: e.reduce_max(out=sc11[:, 5:6],
                                                in_=psC1[0:1, 0:128], axis=AX.X),
               reads=[R["psC1"]], writes=[R["sc11"]])
        w.step("gpsimd", lambda e: e.dma_start(out=cc2i, in_=sc11[:, 5:6]),
               reads=[R["sc11"]], writes=[R["d_cc2i"]], sem="LC1", n=16)
        w.step("gpsimd", lambda e: e.collective_compute(
            "AllReduce", ALU.max, replica_groups=[list(range(8))],
            ins=[cc2i.opt()], outs=[cc2o.opt()]),
            reads=[R["d_cc2i"]], writes=[R["d_cc2o"]], sem="G")
        w.step("gpsimd", lambda e: e.dma_start(out=sc11[:, 5:6], in_=cc2o),
               reads=[R["d_cc2o"]], writes=[R["sc11"]], sem="LD0", n=16)
        w.step("scalar", lambda e: e.activation(out=sc11[:, 6:7], in_=clipd_s[:],
                                                func=AF.Sigmoid),
               reads=[R["sc11"]], writes=[R["sc11"]])
        def _sh(e):
            w.zs(e, e.tensor_tensor(out=sc11[:, 7:8], in0=sc11[:, 6:7],
                                    in1=sc11[:, 5:6], op=ALU.mult))
            w.zs(e, e.reciprocal(out=sc11[:, 5:6], in_=sc11[:, 7:8]))
            w.zs(e, e.tensor_scalar(out=sc11[:, 5:6], in0=sc11[:, 5:6], scalar1=QMAX,
                                    scalar2=None, op0=ALU.mult))
            return e.tensor_scalar(out=sc11[:, 6:7], in0=sc11[:, 7:8],
                                   scalar1=1.0/QMAX, scalar2=None, op0=ALU.mult)
        w.step("vector", _sh, reads=[R["sc11"]], writes=[R["sc11"]])
        def _bc(e):
            e.matmul(out=psC0[:, 2:3], lhsT=onesf[:], rhs=sc11[:, 5:6],
                     start=True, stop=False)
            return e.matmul(out=psC0[:, 3:4], lhsT=onesf[:], rhs=sc11[:, 6:7],
                            start=False, stop=True)
        w.step("tensor", _bc, reads=[R["sc11"]], writes=[R["psC0"]])
        w.step("vector", lambda e: e.tensor_copy(out=rxsx[:, 2:4], in_=psC0[:, 2:4]),
               reads=[R["psC0"]], writes=[R["rxsx"]])

    run_phase(ph_p3)
    if debug:
        def ph_dbg3(w, R):
            w.step("sync", lambda e: e.dma_start(out=dbg_h, in_=h_hbm),
                   reads=[R["d_hhbm"]], sem="ST", n=16)
            w.step("sync", lambda e: e.dma_start(out=dbg_sc, in_=sc11[:]),
                   reads=[R["sc11"]], sem="ST", n=16)
            w.step("sync", lambda e: e.dma_start(out=dbg_rx, in_=rxsx[:]),
                   reads=[R["rxsx"]], sem="ST", n=16)
            w.step("sync", lambda e: e.dma_start(out=dbg_ssw, in_=sxswg[:]),
                   reads=[R["sxsw"]], sem="ST", n=16)
            w.step("sync", lambda e: e.dma_start(out=dbg_agg, in_=agg_o),
                   reads=[R["d_aggo"]], sem="ST", n=16)
        run_phase(ph_dbg3)
    p3ctx.close()
    p2ctx.close()

    p5ctx = ExitStack()
    hqsb = sb(p5ctx, "hqsb", [128, 2, NI, 512], BF16)
    hstg = sb(p5ctx, "hstg", [128, 2, 512], F32)
    hi32 = sb(p5ctx, "hi32", [128, 512], I32)
    hclip = sb(p5ctx, "hclip", [128, 512], BF16)
    outsb = sb(p5ctx, "outsb", [128, 2, E], F32)
    tmpadd = sb(p5ctx, "tmpadd", [128, 512], F32)

    def ph_p5(w, R):
        def qblock(ti):
            t0 = 512 * ti
            hb = ti % 2
            w.serial = SERIAL_P5 in (1, 2)
            for g in range(NI):
                gp = g % 2
                w.step("sync", lambda e, g=g, t0=t0, gp=gp: e.dma_start(
                    out=hstg[:, gp, :], in_=h_hbm[g, :, t0:t0+512]),
                    reads=[R["d_hhbm"]], writes=[R[f"hstg{gp}"]], sem=f"LA{gp}",
                    n=16)
                def _q(e, gp=gp):
                    w.zs(e, e.tensor_scalar(out=hi32[:], in0=hstg[:, gp, :],
                                            scalar1=rxsx[:, 2:3], scalar2=None,
                                            op0=ALU.mult))
                    return e.tensor_scalar(out=hclip[:], in0=hi32[:],
                                           scalar1=-128.0, scalar2=127.0,
                                           op0=ALU.max, op1=ALU.min)
                w.step("vector", _q, reads=[R[f"hstg{gp}"], R["rxsx"]],
                       writes=[R["hint"]])
                w.step("scalar", lambda e, g=g, hb=hb: e.activation(
                    out=hqsb[:, hb, g, :], in_=hclip[:], func=AF.Copy,
                    scale=rxsx[:, 3:4]),
                    reads=[R["hint"], R["rxsx"]], writes=[R[f"hq{hb}_{g}"]])

        def dblock(ti):
            hb = ti % 2
            w.serial = SERIAL_P5 in (1, 3)
            for tq in range(4):
                qi = 4 * ti + tq
                pq = qi % 2
                w.step("gpsimd", lambda e, qi=qi, pq=pq: e.dma_start(
                    out=xbuf[:, pq, :], in_=x_d[128*qi:128*(qi+1), :]),
                    writes=[R[f"xbuf{pq}"]], sem=f"LB{pq}", n=16)
                w.step("gpsimd", lambda e, qi=qi, pq=pq: e.dma_start(
                    out=attst[:, pq, :], in_=attno_hbm[128*qi:128*(qi+1), :]),
                    reads=[R["d_attno"]], writes=[R[f"attst{pq}"]], sem=f"LC{pq}",
                    n=16)
                def _dn(e, tq=tq, pq=pq, hb=hb):
                    for half, bank in ((0, psA[pq]), (1, psB[pq])):
                        for g in range(NI):
                            i = e.matmul(out=bank[:],
                                         lhsT=hqsb[:, hb, g, 128*tq:128*(tq+1)],
                                         rhs=wdt[:, g, 512*half:512*(half+1)],
                                         start=(g == 0), stop=(g == NI - 1))
                    return i
                w.step("tensor", _dn,
                       reads=[R["wdt"]] + [R[f"hq{hb}_{g}"] for g in range(NI)],
                       writes=[R[f"psA{pq}"], R[f"psB{pq}"]])
                def _fin(e, pq=pq):
                    for half, bank in ((0, psA[pq]), (1, psB[pq])):
                        w.zs(e, e.tensor_tensor(out=tmpadd[:], in0=bank[:],
                                                in1=xbuf[:, pq, 512*half:512*(half+1)],
                                                op=ALU.add))
                        i = e.tensor_tensor(out=outsb[:, pq, 512*half:512*(half+1)],
                                            in0=tmpadd[:],
                                            in1=attst[:, pq, 512*half:512*(half+1)],
                                            op=ALU.add)
                    return i
                w.step("vector", _fin,
                       reads=[R[f"psA{pq}"], R[f"psB{pq}"], R[f"xbuf{pq}"],
                              R[f"attst{pq}"]],
                       writes=[R[f"outsb{pq}"], R["tmpadd"]])
                w.step("gpsimd", lambda e, qi=qi, pq=pq: e.dma_start(
                    out=out_d[128*qi:128*(qi+1), :], in_=outsb[:, pq, :]),
                    reads=[R[f"outsb{pq}"]], writes=[], sem=f"SA{pq}", n=16)

        for ti in range(NTI):
            qblock(ti)
            if ti > 0:
                dblock(ti - 1)
        dblock(NTI - 1)
        if debug:
            for g in range(NI):
                w.step("sync", lambda e, g=g: e.dma_start(
                    out=dbg_hq[:, g, :] if True else None, in_=hqsb[:, 1, g, :]),
                    reads=[R["hq1_" + str(g)]], sem="ST", n=16)
        if w.me == "sync":
            for s_ in ("ST", "SA0", "SA1"):
                if w.c[s_] > 0:
                    w.eng.wait_ge(semh[s_], w.c[s_])

    run_phase(ph_p5, barrier=True)
    p5ctx.close()
    pwctx.close()
    xctx.close()

    octx.close()
    return nc


_NC_CACHE = {}


def kernel(L0_attn_x, L0_attn_wq, L0_attn_bq, L0_attn_wk, L0_attn_bk, L0_attn_wv,
           L0_attn_bv, L0_attn_wo, L0_attn_bo, L0_attn_qn_w, L0_attn_kn_w,
           L0_ffn_clip_x, L0_ffn_clip_down, L0_ffn_w_gate, L0_ffn_w_up,
           L0_ffn_w_down):
    f32 = lambda a: np.ascontiguousarray(np.asarray(a, dtype=np.float32))
    x = f32(L0_attn_x)
    wq, wk, wv, wo = f32(L0_attn_wq), f32(L0_attn_wk), f32(L0_attn_wv), f32(L0_attn_wo)
    bq, bk = f32(L0_attn_bq).reshape(1, E), f32(L0_attn_bk).reshape(1, E)
    bv, bo = f32(L0_attn_bv).reshape(1, E), f32(L0_attn_bo).reshape(1, E)
    qn, kn = f32(L0_attn_qn_w).reshape(1, D), f32(L0_attn_kn_w).reshape(1, D)
    clipx = f32(L0_ffn_clip_x).reshape(1, 1)
    clipd = f32(L0_ffn_clip_down).reshape(1, 1)
    wg, wu, wd = f32(L0_ffn_w_gate), f32(L0_ffn_w_up), f32(L0_ffn_w_down)

    consts = _host_consts()
    if "nc" not in _NC_CACHE:
        _NC_CACHE["nc"] = build_nc()
    nc = _NC_CACHE["nc"]

    in_maps = _mk_inmaps(x, wq, wk, wv, wo, bq, bk, bv, bo, qn, kn, clipx, clipd,
                         wg, wu, wd, consts)
    res = run_bass_kernel_spmd(nc, in_maps, core_ids=list(range(8)))
    out = np.stack([res.results[c]["out"] for c in range(8)], axis=0)
    return out.astype(np.float32)


def _mk_inmaps(x, wq, wk, wv, wo, bq, bk, bv, bo, qn, kn, clipx, clipd,
               wg, wu, wd, consts):
    in_maps = []
    for c in range(8):
        lo = 3 * c
        gsh = np.ones((3, 128, E), np.float32)
        ush = np.ones((3, 128, E), np.float32)
        for j in range(3):
            g = lo + j
            if g < NI:
                gsh[j] = wg[128*g:128*(g+1), :]
                ush[j] = wu[128*g:128*(g+1), :]
        m = dict(x=np.ascontiguousarray(x[c]),
                 wq=wq, wk=wk, wv=wv, wo=wo, bq=bq, bk=bk, bv=bv, bo=bo,
                 qn=qn, kn=kn, clipx=clipx, clipd=clipd,
                 wg_sh=np.ascontiguousarray(gsh),
                 wu_sh=np.ascontiguousarray(ush),
                 wd_sh=np.ascontiguousarray(wd[128*c:128*(c+1), :]))
        m.update(consts)
        in_maps.append(m)
    return in_maps


# version 18
# speedup vs baseline: 1.0030x; 1.0030x over previous
import numpy as np
import ml_dtypes
from contextlib import ExitStack
import concourse.bass as bass
import concourse.mybir as mybir
from concourse.bass_utils import run_bass_kernel_spmd

F32 = mybir.dt.float32
BF16 = mybir.dt.bfloat16
I32 = mybir.dt.int32
AF = mybir.ActivationFunctionType
ALU = mybir.AluOpType
AX = mybir.AxisListType

T = 2048
E = 1024
NH = 16
D = 64
IF = 2816
NT = T // 128
NE = E // 128
NI = IF // 128
NTI = 4
NEG = -1e9
LN_EPS = 1e-5
RMS_EPS = float(np.finfo(np.float32).eps)
QMAX = 127.0
BF = ml_dtypes.bfloat16
SERIAL_P5 = 0
SERIAL_P4 = False
SERIAL_P1 = False
SERIAL_P0 = False
SERIAL_Q = False
SERIAL_P2 = False
SERIAL_P3 = False


def _host_consts():
    ident = np.eye(128, dtype=np.float32)
    dq = np.arange(128)[:, None]
    o = np.arange(128)[None, :]
    m0 = np.where(o <= dq, NEG, 0.0).astype(np.float32)
    m2 = np.where(o > dq, NEG, 0.0).astype(np.float32)
    maskrow = np.concatenate([m0, np.zeros((128, 128), np.float32), m2], axis=1)
    maskT = np.ascontiguousarray(
        np.concatenate([m0.T, np.zeros((128, 128), np.float32), m2.T], axis=1))
    qh = np.zeros((128, 128), np.float32)
    for ec in range(8):
        qh[0:64, 16 * ec + 2 * ec] = 1.0
        qh[64:128, 16 * ec + 2 * ec + 1] = 1.0
    rbc = np.zeros((16, 1024), np.float32)
    for ec in range(8):
        rbc[2 * ec, 128 * ec: 128 * ec + 64] = 1.0
        rbc[2 * ec + 1, 128 * ec + 64: 128 * ec + 128] = 1.0
    return {
        "identbf": ident.astype(BF),
        "identf": ident,
        "onesg": np.ones((128, 128), np.float32).astype(BF),
        "onesf": np.ones((1, 128), np.float32),
        "maskrow": maskrow.astype(BF),
        "maskTt": maskT.astype(BF),
        "qh_ind": qh.astype(BF),
        "rbc_ind": rbc.astype(BF),
    }


class RES:
    __slots__ = ("w", "r", "dram")
    def __init__(self, dram=False):
        self.w = None
        self.r = []
        self.dram = dram


class Walk:
    SEM_OF = {"sync": "LX", "gpsimd": "G", "tensor": "T", "vector": "V",
              "scalar": "S"}
    DMA_SEMS = ("LX", "LG", "ST", "LA0", "LA1", "LB0", "LB1", "LC0", "LC1",
                "LD0", "LD1", "SA0", "SA1")

    def __init__(self, semh, counts, res, eng, me, consts_res, zst):
        self.semh = semh
        self.c = counts
        self.res = res
        self.eng = eng
        self.me = me
        self.consts = consts_res
        self.fenced = False
        self.Z = zst
        self.serial = False

    def sync_all(self):
        for s_ in ("T", "V", "S", "G") + self.DMA_SEMS:
            if self.c[s_] > 0:
                self.eng.wait_ge(self.semh[s_], self.c[s_])

    def zs(self, e, instr):
        k = self.me
        self.Z[k] += 1
        instr.then_inc(self.semh["Z" + k], 1)
        e.wait_ge(self.semh["Z" + k], self.Z[k])
        return instr

    def step(self, who, emit, reads=(), writes=(), sem=None, n=1):
        sem = sem or self.SEM_OF[who]
        on = (who == self.me)
        instr = None
        rd = list(reads)
        if on:
            if self.serial:
                self.sync_all()
            if not self.fenced and self.consts is not None:
                rd.append(self.consts)
                self.fenced = True
            need_r = {}
            need_w = {}
            def _need(d, tok):
                if tok is not None:
                    s_, v_ = tok
                    if v_ > d.get(s_, -1):
                        d[s_] = v_
            for b in rd:
                _need(need_r, b.w)
            for b in writes:
                _need(need_w, b.w)
                for tok in b.r:
                    _need(need_w, tok)
            for s_, v_ in sorted(need_r.items()):
                self.eng.wait_ge(self.semh[s_], v_)
            for s_, v_ in sorted(need_w.items()):
                if need_r.get(s_, -1) >= v_:
                    continue
                if s_ == sem and s_ not in self.DMA_SEMS:
                    continue
                self.eng.wait_ge(self.semh[s_], v_)
            instr = emit(self.eng)
        self.c[sem] += n
        tok = (sem, self.c[sem])
        if on and instr is not None:
            instr.then_inc(self.semh[sem], n)
        for b in reads:
            b.r.append(tok)
        for b in writes:
            b.w = tok
            b.r = []
        return instr


def build_nc(debug=False):
    nc = bass.Bass("TRN2", target_bir_lowering=False)
    dti = lambda n, s, d: nc.dram_tensor(n, s, d, kind="ExternalInput").ap()
    x_d = dti("x", [T, E], F32)
    wq_d = dti("wq", [E, E], F32); wk_d = dti("wk", [E, E], F32)
    wv_d = dti("wv", [E, E], F32); wo_d = dti("wo", [E, E], F32)
    bq_d = dti("bq", [1, E], F32); bk_d = dti("bk", [1, E], F32)
    bv_d = dti("bv", [1, E], F32); bo_d = dti("bo", [1, E], F32)
    qn_d = dti("qn", [1, D], F32); kn_d = dti("kn", [1, D], F32)
    clipx_d = dti("clipx", [1, 1], F32); clipd_d = dti("clipd", [1, 1], F32)
    wg_d = dti("wg_sh", [3, 128, E], F32)
    wu_d = dti("wu_sh", [3, 128, E], F32)
    wd_d = dti("wd_sh", [128, IF], F32)
    identbf_d = dti("identbf", [128, 128], BF16)
    identf_d = dti("identf", [128, 128], F32)
    onesg_d = dti("onesg", [128, 128], BF16)
    onesf_d = dti("onesf", [1, 128], F32)
    maskrow_d = dti("maskrow", [128, 384], BF16)
    maskTt_d = dti("maskTt", [128, 384], BF16)
    qh_d = dti("qh_ind", [128, 128], BF16)
    rbc_d = dti("rbc_ind", [16, 1024], BF16)
    out_d = nc.dram_tensor("out", [T, E], F32, kind="ExternalOutput").ap()
    h_hbm = nc.dram_tensor("h_hbm", [NI, 128, T], F32).ap()
    attno_hbm = nc.dram_tensor("attno_hbm", [T, E], BF16).ap()
    cc1i = nc.dram_tensor("cc1i", [1, 1], F32).ap()
    cc1o = nc.dram_tensor("cc1o", [1, 1], F32).ap()
    cc2i = nc.dram_tensor("cc2i", [1, 1], F32).ap()
    cc2o = nc.dram_tensor("cc2o", [1, 1], F32).ap()
    agg_i = nc.dram_tensor("agg_i", [3, 128, 8, 128], BF16).ap()
    agg_o = nc.dram_tensor("agg_o", [8, 3, 128, 8, 128], BF16).ap()
    agu_i = nc.dram_tensor("agu_i", [3, 128, 8, 128], BF16).ap()
    agu_o = nc.dram_tensor("agu_o", [8, 3, 128, 8, 128], BF16).ap()
    agd_i = nc.dram_tensor("agd_i", [NI, 128, 128], BF16).ap()
    agd_o = nc.dram_tensor("agd_o", [8, NI, 128, 128], BF16).ap()
    scl_i = nc.dram_tensor("scl_i", [128, 8], F32).ap()
    scl_o = nc.dram_tensor("scl_o", [8, 128, 8], F32).ap()
    wdT_hbm = nc.dram_tensor("wdT_hbm", [NI, 128, E], BF16).ap()
    if debug:
        dbg_xn = nc.dram_tensor("dbg_xn", [128, 8, T], BF16, kind="ExternalOutput").ap()
        dbg_qh = nc.dram_tensor("dbg_qh", [128, 8, T], BF16, kind="ExternalOutput").ap()
        dbg_kh = nc.dram_tensor("dbg_kh", [128, 8, T], BF16, kind="ExternalOutput").ap()
        dbg_v = nc.dram_tensor("dbg_v", [128, 16, NH * 65], BF16, kind="ExternalOutput").ap()
        dbg_sc = nc.dram_tensor("dbg_sc", [1, 8], F32, kind="ExternalOutput").ap()
        dbg_rx = nc.dram_tensor("dbg_rx", [128, 4], F32, kind="ExternalOutput").ap()
        dbg_at = nc.dram_tensor("dbg_at", [T, E], BF16, kind="ExternalOutput").ap()
        dbg_h = nc.dram_tensor("dbg_h", [NI, 128, T], F32, kind="ExternalOutput").ap()
        dbg_st = nc.dram_tensor("dbg_st", [128, 96], F32, kind="ExternalOutput").ap()
        dbg_hq = nc.dram_tensor("dbg_hq", [128, NI, 512], BF16, kind="ExternalOutput").ap()
        dbg_ssw = nc.dram_tensor("dbg_ssw", [128, 24], F32, kind="ExternalOutput").ap()
        dbg_agg = nc.dram_tensor("dbg_agg", [8, 3, 128, 8, 128], BF16, kind="ExternalOutput").ap()
        dbg_sl = nc.dram_tensor("dbg_sl", [128, 4, 512], F32, kind="ExternalOutput").ap()
        dbg_wg0 = nc.dram_tensor("dbg_wg0", [128, 8, 128], BF16, kind="ExternalOutput").ap()
        dbg_ag0 = nc.dram_tensor("dbg_ag0", [128, 8, 128], BF16, kind="ExternalOutput").ap()
        dbg_wp = nc.dram_tensor("dbg_wp", [128, 8, E], BF16, kind="ExternalOutput").ap()

    octx = ExitStack()
    sb = lambda st, n, s, d: st.enter_context(nc.sbuf_tensor(n, s, d))
    pt = lambda n, s, d: octx.enter_context(nc.psum_tensor(n, s, d))

    identbf = sb(octx, "identbf_s", [128, 128], BF16)
    identf = sb(octx, "identf_s", [128, 128], F32)
    onesg = sb(octx, "onesg_s", [128, 128], BF16)
    onesf = sb(octx, "onesf_s", [1, 128], F32)
    maskrow = sb(octx, "maskrow_s", [128, 384], BF16)
    maskTt = sb(octx, "maskTt_s", [128, 384], BF16)
    qh_ind = sb(octx, "qh_ind_s", [128, 128], BF16)
    rbc_ind = sb(octx, "rbc_ind_s", [16, 1024], BF16)
    bqc = sb(octx, "bqc", [128, 8], F32); bkc = sb(octx, "bkc", [128, 8], F32)
    bvrow = sb(octx, "bvrow", [1, E], BF16); borow = sb(octx, "borow", [1, E], BF16)
    qncol = sb(octx, "qncol", [128, 1], F32); kncol = sb(octx, "kncol", [128, 1], F32)
    qn8 = sb(octx, "qn8", [128, 1], F32)
    bqq = sb(octx, "bqq", [128, 8], F32); bkk = sb(octx, "bkk", [128, 8], F32)
    clipx_s = sb(octx, "clipx_s", [1, 1], F32); clipd_s = sb(octx, "clipd_s", [1, 1], F32)
    stat = sb(octx, "stat", [128, 96], F32)
    sumsq = sb(octx, "sumsq", [128, 16], F32)
    rabs = sb(octx, "rabs", [128, 16], F32)
    habs = sb(octx, "habs", [128, 96], F32)
    sc11 = sb(octx, "sc11", [1, 8], F32)
    rxsx = sb(octx, "rxsx", [128, 4], F32)
    acol = sb(octx, "acol", [128, 16], F32); b2col = sb(octx, "b2col", [128, 16], F32)
    sclsb = sb(octx, "sclsb", [128, 8, 8], F32)
    sxswg = sb(octx, "sxswg", [128, 24], F32)
    sxswu = sb(octx, "sxswu", [128, 24], F32)
    vsmall = sb(octx, "vsmall", [128, 2], F32)
    gtr = sb(octx, "gtr", [128, 32], F32)
    epsln = sb(octx, "epsln", [128, 1], F32)
    epsrms = sb(octx, "epsrms", [128, 1], F32)
    xbuf = sb(octx, "xbuf", [128, 2, E], F32)
    mnegf = sb(octx, "mnegf", [128, 2], F32)
    recipz = sb(octx, "recipz", [128, 2], F32)
    ptsb = sb(octx, "ptsb", [128, 2, 384], BF16)
    prow = sb(octx, "prow", [128, 2, 384], BF16)
    attok = sb(octx, "attok", [128, E], BF16)
    afm = sb(octx, "afm", [128, 8, 128], BF16)
    attst = sb(octx, "attst", [128, 2, E], BF16)

    psA0 = pt("psA0", [128, 512], F32)
    psA1 = pt("psA1", [128, 512], F32)
    psB0 = pt("psB0", [128, 512], F32)
    psB1 = pt("psB1", [128, 512], F32)
    psC0 = pt("psC0", [128, 512], F32)
    psC1 = pt("psC1", [128, 512], F32)
    psT = pt("psT", [128, 1024], BF16)
    psU = pt("psU", [128, 512], BF16)
    psA = [psA0, psA1]; psB = [psB0, psB1]; psC = [psC0, psC1]
    psUT = [psU, psT]

    semh = {}
    for n_ in ("LX", "LG", "T", "V", "S", "G", "ST",
               "LA0", "LA1", "LB0", "LB1", "LC0", "LC1", "LD0", "LD1",
               "SA0", "SA1",
               "Zvector", "Zscalar", "Zgpsimd", "Ztensor", "Zsync"):
        semh[n_] = octx.enter_context(nc.semaphore(n_))
    block = octx.enter_context(nc.Block())

    RNAMES = ["psA0", "psA1", "psB0", "psB1", "psC0", "psC1", "psT", "psU",
              "xbuf0", "xbuf1", "xnbuf0", "xnbuf1", "sqscr0", "sqscr1",
              "stat", "sumsq", "rabs", "habs", "sc11", "rxsx", "acol",
              "sclsb", "sxsw", "small", "xnfm0", "xnfm1", "xnfm2", "xnfm3", "qhat", "khat", "vsb",
              "wv", "wo", "prel0", "prel1", "qsqb0", "qsqb1", "sdq", "rqf",
              "rqbf0", "rqbf1",
              "mneg0", "mneg1", "mrow0", "mrow1", "recipz0", "recipz1",
              "ptsb0", "ptsb1", "prow0", "prow1", "attok", "afm",
              "attst0", "attst1",
              "wshard", "wdshard", "wint", "stg", "sclpack",
              "xqi", "xqb0", "xqb1",
              "wgt0", "wgt1", "wut0", "wut1", "silub", "upsb",
              "hbuf0", "hbuf1", "wdt", "hstg0", "hstg1", "hint",
              "outsb0", "outsb1", "tmpadd", "biases", "consts", "tmpdbg"]
    RNAMES += [f"hq{hb}_{g}" for hb in range(2) for g in range(NI)]
    RNAMES += [f"wqk{s}" for s in range(4)]
    DRAM = ["d_hhbm", "d_attno", "d_cc1i", "d_cc1o", "d_cc2i", "d_cc2o",
            "d_aggi", "d_aggo", "d_agui", "d_aguo", "d_agdi", "d_agdo",
            "d_scli", "d_sclo", "d_wdT"]

    def make_res():
        r = {n_: RES() for n_ in RNAMES}
        r.update({n_: RES(dram=True) for n_ in DRAM})
        return r

    ENGS = ("sync", "gpsimd", "tensor", "vector", "scalar")
    EOBJ = {"sync": nc.sync, "gpsimd": nc.gpsimd, "tensor": nc.tensor,
            "vector": nc.vector, "scalar": nc.scalar}
    state = {"c": {n_: 0 for n_ in semh}, "res": make_res(),
             "z": {e_: 0 for e_ in ENGS}}

    def run_phase(phase_fn, barrier=False):
        final_c = final_r = None
        base = dict(state["c"])
        for me in ENGS:
            counts = dict(base)
            res = {k: _res_copy(v) for k, v in state["res"].items()}
            def body(e, me=me, counts=counts, res=res):
                if barrier:
                    for s_ in ("T", "V", "S") + Walk.DMA_SEMS:
                        if base[s_] > 0:
                            e.wait_ge(semh[s_], base[s_])
                w = Walk(semh, counts, res, e, me, res["consts"], state["z"])
                phase_fn(w, res)
            getattr(block, me)(body)
            final_c, final_r = counts, res
        state["c"] = final_c
        state["res"] = final_r

    def _res_copy(b):
        n = RES(b.dram)
        n.w = b.w
        n.r = list(b.r)
        return n

    def ph_loads(w, R):
        loads = [(identbf, identbf_d), (identf, identf_d),
                 (onesg, onesg_d), (maskrow, maskrow_d),
                 (maskTt, maskTt_d), (qh_ind, qh_d), (rbc_ind, rbc_d),
                 (onesf, onesf_d), (clipx_s, clipx_d), (clipd_s, clipd_d)]
        for dst, src in loads:
            w.step("sync", lambda e, dst=dst, src=src:
                   e.dma_start(out=dst[:], in_=src),
                   writes=[R["consts"]], sem="LX", n=16)
        w.step("sync", lambda e: e.dma_start(
            out=bqc[:], in_=bq_d.rearrange("a (c p) -> p (a c)", p=128),
            allow_slow_non_contiguous=True),
            writes=[R["biases"]], sem="LX", n=16)
        w.step("sync", lambda e: e.dma_start(
            out=bkc[:], in_=bk_d.rearrange("a (c p) -> p (a c)", p=128),
            allow_slow_non_contiguous=True),
            writes=[R["biases"]], sem="LX", n=16)
        for dst, src in ((qncol, qn_d), (kncol, kn_d)):
            w.step("sync", lambda e, dst=dst, src=src: e.dma_start(
                out=dst[0:64, :], in_=src.rearrange("a d -> d a"),
                allow_slow_non_contiguous=True),
                writes=[R["biases"]], sem="LX", n=16)
            w.step("sync", lambda e, dst=dst, src=src: e.dma_start(
                out=dst[64:128, :], in_=src.rearrange("a d -> d a"),
                allow_slow_non_contiguous=True),
                writes=[R["biases"]], sem="LX", n=16)
        w.step("gpsimd", lambda e: e.dma_start(out=bvrow[:], in_=bv_d),
               writes=[R["biases"]], sem="LG", n=16)
        w.step("gpsimd", lambda e: e.dma_start(out=borow[:], in_=bo_d),
               writes=[R["biases"]], sem="LG", n=16)

        def _pre(e):
            w.zs(e, e.tensor_scalar(out=qn8[:], in0=qncol[:], scalar1=0.125,
                                    scalar2=None, op0=ALU.mult))
            e.tensor_scalar(out=bqq[:], in0=bqc[:], scalar1=qn8[:], scalar2=None,
                            op0=ALU.mult)
            e.tensor_scalar(out=bkk[:], in0=bkc[:], scalar1=kncol[:], scalar2=None,
                            op0=ALU.mult)
            e.memset(epsln[:], LN_EPS)
            return e.memset(epsrms[:], RMS_EPS)
        w.step("vector", _pre, reads=[R["biases"]], writes=[R["biases"]])

    run_phase(ph_loads)

    xctx = ExitStack()
    xnfm = sb(xctx, "xnfm", [128, 8, T], BF16)
    lctx = ExitStack()
    qhat = sb(lctx, "qhat", [128, 8, T], BF16)
    khat = sb(lctx, "khat", [128, 8, T], BF16)
    vsb = sb(lctx, "vsb", [128, 16, NH * 65], BF16)

    qctx = ExitStack()
    wshard = sb(qctx, "wshard", [128, 3, E], F32)
    wdshard = sb(qctx, "wdshard", [128, IF], F32)
    wi32 = sb(qctx, "wi32", [128, E], I32)
    wbf = sb(qctx, "wbf", [128, E], BF16)
    stg = sb(qctx, "stg", [128, 8, 128], BF16)
    sclpack = sb(qctx, "sclpack", [128, 8], F32)
    absq = sb(qctx, "absq", [128, 8], F32)
    rq127 = sb(qctx, "rq127", [128, 8], F32)

    def ph_quant(w, R):
        w.serial = SERIAL_Q
        w.step("sync", lambda e: e.dma_start(
            out=wshard[:], in_=wg_d.rearrange("j (a p) e -> p j (a e)", p=128)),
            writes=[R["wshard"]], sem="LB0", n=16)
        w.step("sync", lambda e: e.dma_start(out=wdshard[:], in_=wd_d),
               writes=[R["wdshard"]], sem="LB0", n=16)
        for grp in range(2):
            agdst = agg_i if grp == 0 else agu_i
            dram_i = R["d_aggi"] if grp == 0 else R["d_agui"]
            if grp == 1:
                w.step("sync", lambda e: e.dma_start(
                    out=wshard[:],
                    in_=wu_d.rearrange("j (a p) e -> p j (a e)", p=128)),
                    writes=[R["wshard"]], sem="LB0", n=16)
            for j in range(3):
                k = grp * 3 + j
                def _q(e, j=j, k=k):
                    w.zs(e, e.tensor_reduce(out=absq[:, k:k+1], in_=wshard[:, j, :],
                                            axis=AX.X, op=ALU.max,
                                            apply_absolute_value=True))
                    w.zs(e, e.reciprocal(out=rq127[:, k:k+1], in_=absq[:, k:k+1]))
                    e.tensor_scalar(out=sclpack[:, k:k+1], in0=absq[:, k:k+1],
                                    scalar1=1.0 / QMAX, scalar2=None, op0=ALU.mult)
                    e.tensor_scalar(out=wi32[:], in0=wshard[:, j, :],
                                    scalar1=rq127[:, k:k+1], scalar2=QMAX,
                                    op0=ALU.mult, op1=ALU.mult)
                    return e.tensor_copy(out=wbf[:], in_=wi32[:])
                w.step("vector", _q, reads=[R["wshard"]],
                       writes=[R["wint"], R["sclpack"]])
                def _t(e):
                    for ec in range(8):
                        i = e.transpose(out=psT[:, 128*ec:128*(ec+1)],
                                        in_=wbf[:, 128*ec:128*(ec+1)],
                                        identity=identbf[:])
                    return i
                w.step("tensor", _t, reads=[R["wint"]], writes=[R["psT"]])
                w.step("scalar", lambda e: e.copy(
                    out=stg[:], in_=psT[:].rearrange("p (ec t) -> p ec t", ec=8)),
                    reads=[R["psT"]], writes=[R["stg"]])
                w.step("sync", lambda e, agdst=agdst, j=j: e.dma_start(
                    out=agdst[j], in_=stg[:]),
                    reads=[R["stg"]], writes=[dram_i], sem="SA0", n=16)
        def _qd(e):
            w.zs(e, e.tensor_reduce(out=absq[:, 6:7], in_=wdshard[:], axis=AX.X,
                                    op=ALU.max, apply_absolute_value=True))
            w.zs(e, e.reciprocal(out=rq127[:, 6:7], in_=absq[:, 6:7]))
            return e.tensor_scalar(out=sclpack[:, 6:7], in0=absq[:, 6:7],
                                   scalar1=1.0 / QMAX, scalar2=None, op0=ALU.mult)
        w.step("vector", _qd, reads=[R["wdshard"]], writes=[R["sclpack"]])
        for b in range(NI):
            def _qb(e, b=b):
                w.zs(e, e.tensor_scalar(out=wi32[:, 0:128],
                                        in0=wdshard[:, 128*b:128*(b+1)],
                                        scalar1=rq127[:, 6:7], scalar2=QMAX,
                                        op0=ALU.mult, op1=ALU.mult))
                return e.tensor_scalar(out=wbf[:, 0:128], in0=wi32[:, 0:128],
                                       scalar1=sclpack[:, 6:7], scalar2=None,
                                       op0=ALU.mult)
            w.step("vector", _qb, reads=[R["wdshard"], R["sclpack"]],
                   writes=[R["wint"]])
            w.step("tensor", lambda e: e.transpose(
                out=psT[:, 0:128], in_=wbf[:, 0:128], identity=identbf[:]),
                reads=[R["wint"]], writes=[R["psT"]])
            w.step("scalar", lambda e: e.copy(out=stg[:, 0, :], in_=psT[:, 0:128]),
                   reads=[R["psT"]], writes=[R["stg"]])
            w.step("sync", lambda e, b=b: e.dma_start(out=agd_i[b], in_=stg[:, 0, :]),
                   reads=[R["stg"]], writes=[R["d_agdi"]], sem="SA0", n=16)
        w.step("sync", lambda e: e.dma_start(out=scl_i, in_=sclpack[:]),
               reads=[R["sclpack"]], writes=[R["d_scli"]], sem="SA0", n=16)
        for ins, outs, ri, ro in ((agg_i, agg_o, "d_aggi", "d_aggo"),
                                  (agu_i, agu_o, "d_agui", "d_aguo"),
                                  (agd_i, agd_o, "d_agdi", "d_agdo"),
                                  (scl_i, scl_o, "d_scli", "d_sclo")):
            w.step("gpsimd", lambda e, ins=ins, outs=outs: e.collective_compute(
                "AllGather", ALU.bypass, replica_groups=[list(range(8))],
                ins=[ins.opt()], outs=[outs.opt()]),
                reads=[R[ri]], writes=[R[ro]], sem="G")
        for ro in ("d_aggo", "d_aguo", "d_agdo", "d_sclo"):
            R[ro].w = ("G", w.c["G"])
        for r8 in range(8):
            w.step("gpsimd", lambda e, r8=r8: e.dma_start(
                out=wdT_hbm[:, :, 128*r8:128*(r8+1)], in_=agd_o[r8]),
                reads=[R["d_agdo"]], writes=[R["d_wdT"]], sem="LG", n=16)
        w.step("gpsimd", lambda e: e.dma_start(
            out=sclsb[:], in_=scl_o.rearrange("r p c -> p r c")),
            reads=[R["d_sclo"]], writes=[R["sclsb"]], sem="LD1", n=16)

    run_phase(ph_quant)

    sqscr = sb(qctx, "sqscr", [128, 2, E], F32)
    xnbuf = sb(qctx, "xnbuf", [128, 2, E], BF16)

    def ph_p0(w, R):
        w.serial = SERIAL_P0
        w.step("vector", lambda e: e.memset(
            vsb[:].rearrange("p tc (h d) -> p tc h d", d=65)[:, :, :, 64:65], 1.0),
            writes=[R["vsb"]])
        for tc in range(NT):
            par = tc % 2
            xb = R[f"xbuf{par}"]
            w.step("sync", lambda e, tc=tc, par=par: e.dma_start(
                out=xbuf[:, par, :], in_=x_d[128*tc:128*(tc+1), :]),
                writes=[xb], sem=f"LA{par}", n=16)
            def _red(e, tc=tc, par=par):
                e.reduce_sum(out=stat[:, tc:tc+1], in_=xbuf[:, par, :], axis=AX.X)
                e.reduce_max(out=stat[:, 16+tc:17+tc], in_=xbuf[:, par, :], axis=AX.X)
                return e.tensor_reduce(out=stat[:, 32+tc:33+tc], in_=xbuf[:, par, :],
                                       axis=AX.X, op=ALU.min)
            w.step("vector", _red, reads=[xb], writes=[R["stat"]])
            w.step("scalar", lambda e, tc=tc, par=par: e.activation(
                out=sqscr[:, par, :], in_=xbuf[:, par, :], func=AF.Square,
                accum_out=sumsq[:, tc:tc+1]),
                reads=[xb], writes=[R[f"sqscr{par}"], R["sumsq"]])
            def _st(e, tc=tc):
                w.zs(e, e.tensor_scalar(out=stat[:, 48+tc:49+tc], in0=stat[:, tc:tc+1],
                                        scalar1=1.0/E, scalar2=None, op0=ALU.mult))
                w.zs(e, e.tensor_scalar(out=vsmall[:, 0:1], in0=sumsq[:, tc:tc+1],
                                        scalar1=1.0/E, scalar2=None, op0=ALU.mult))
                w.zs(e, e.tensor_tensor(out=vsmall[:, 1:2], in0=stat[:, 48+tc:49+tc],
                                        in1=stat[:, 48+tc:49+tc], op=ALU.mult))
                return e.tensor_tensor(out=vsmall[:, 0:1], in0=vsmall[:, 0:1],
                                       in1=vsmall[:, 1:2], op=ALU.subtract)
            w.step("vector", _st, reads=[R["sumsq"], R["stat"]],
                   writes=[R["small"], R["stat"]])
            w.step("scalar", lambda e: e.activation(
                out=vsmall[:, 1:2], in_=vsmall[:, 0:1], func=AF.Sqrt,
                bias=epsln[:], scale=1.0),
                reads=[R["small"]], writes=[R["small"]])
            def _rr(e, tc=tc):
                w.zs(e, e.reciprocal(out=stat[:, 64+tc:65+tc], in_=vsmall[:, 1:2]))
                return e.tensor_scalar(out=stat[:, 80+tc:81+tc],
                                       in0=stat[:, 48+tc:49+tc],
                                       scalar1=stat[:, 64+tc:65+tc], scalar2=-1.0,
                                       op0=ALU.mult, op1=ALU.mult)
            w.step("vector", _rr, reads=[R["small"]], writes=[R["stat"]])
            w.step("scalar", lambda e, tc=tc, par=par: e.activation(
                out=xnbuf[:, par, :], in_=xbuf[:, par, :], func=AF.Identity,
                scale=stat[:, 64+tc:65+tc], bias=stat[:, 80+tc:81+tc]),
                reads=[xb, R["stat"]], writes=[R[f"xnbuf{par}"]])
            def _ra(e, tc=tc):
                w.zs(e, e.tensor_tensor(out=vsmall[:, 0:1], in0=stat[:, 16+tc:17+tc],
                                        in1=stat[:, 48+tc:49+tc], op=ALU.subtract))
                w.zs(e, e.tensor_tensor(out=rabs[:, tc:tc+1], in0=stat[:, 48+tc:49+tc],
                                        in1=stat[:, 32+tc:33+tc], op=ALU.subtract))
                w.zs(e, e.tensor_tensor(out=rabs[:, tc:tc+1], in0=rabs[:, tc:tc+1],
                                        in1=vsmall[:, 0:1], op=ALU.max))
                return e.tensor_scalar(out=rabs[:, tc:tc+1], in0=rabs[:, tc:tc+1],
                                       scalar1=stat[:, 64+tc:65+tc], scalar2=None,
                                       op0=ALU.mult)
            w.step("vector", _ra, reads=[R["stat"]], writes=[R["rabs"], R["small"]])
            def _tr(e, par=par):
                for s8 in range(8):
                    i = e.transpose(out=psT[:, 128*s8:128*(s8+1)],
                                    in_=xnbuf[:, par, 128*s8:128*(s8+1)],
                                    identity=identbf[:])
                return i
            w.step("tensor", _tr, reads=[R[f"xnbuf{par}"]], writes=[R["psT"]])
            w.step("scalar", lambda e, tc=tc: e.copy(
                out=xnfm[:, :, 128*tc:128*(tc+1)],
                in_=psT[:].rearrange("p (ec t) -> p ec t", ec=8)),
                reads=[R["psT"]], writes=[R[f"xnfm{tc//4}"]])
        def _gr(e):
            e.memset(gtr[:], 0.0)
            return e.tensor_reduce(out=gtr[:, 0:1], in_=rabs[:], axis=AX.X,
                                   op=ALU.max)
        w.step("vector", _gr, reads=[R["rabs"]], writes=[R["small"]])
        w.step("tensor", lambda e: e.transpose(out=psC1[0:32, 0:128],
                                               in_=gtr[:, 0:32], identity=identf[:]),
               reads=[R["small"]], writes=[R["psC1"]])
        w.step("vector", lambda e: e.reduce_max(out=sc11[:, 0:1],
                                                in_=psC1[0:1, 0:128], axis=AX.X),
               reads=[R["psC1"]], writes=[R["sc11"]])
        w.step("gpsimd", lambda e: e.dma_start(out=cc1i, in_=sc11[:, 0:1]),
               reads=[R["sc11"]], writes=[R["d_cc1i"]], sem="LC1", n=16)
        w.step("gpsimd", lambda e: e.collective_compute(
            "AllReduce", ALU.max, replica_groups=[list(range(8))],
            ins=[cc1i.opt()], outs=[cc1o.opt()]),
            reads=[R["d_cc1i"]], writes=[R["d_cc1o"]], sem="G")
        w.step("gpsimd", lambda e: e.dma_start(out=sc11[:, 0:1], in_=cc1o),
               reads=[R["d_cc1o"]], writes=[R["sc11"]], sem="LD0", n=16)
        w.step("scalar", lambda e: e.activation(out=sc11[:, 1:2], in_=clipx_s[:],
                                                func=AF.Sigmoid),
               reads=[R["sc11"]], writes=[R["sc11"]])
        def _sx(e):
            w.zs(e, e.tensor_tensor(out=sc11[:, 2:3], in0=sc11[:, 1:2],
                                    in1=sc11[:, 0:1], op=ALU.mult))
            w.zs(e, e.reciprocal(out=sc11[:, 3:4], in_=sc11[:, 2:3]))
            w.zs(e, e.tensor_scalar(out=sc11[:, 3:4], in0=sc11[:, 3:4], scalar1=QMAX,
                                    scalar2=None, op0=ALU.mult))
            return e.tensor_scalar(out=sc11[:, 4:5], in0=sc11[:, 2:3],
                                   scalar1=1.0/QMAX, scalar2=None, op0=ALU.mult)
        w.step("vector", _sx, reads=[R["sc11"]], writes=[R["sc11"]])
        def _bc(e):
            e.matmul(out=psC0[:, 0:1], lhsT=onesf[:], rhs=sc11[:, 3:4],
                     start=True, stop=False)
            return e.matmul(out=psC0[:, 1:2], lhsT=onesf[:], rhs=sc11[:, 4:5],
                            start=False, stop=True)
        w.step("tensor", _bc, reads=[R["sc11"]], writes=[R["psC0"]])
        w.step("vector", lambda e: e.tensor_copy(out=rxsx[:, 0:2], in_=psC0[:, 0:2]),
               reads=[R["psC0"]], writes=[R["rxsx"]])
        def _ssw(e):
            e.tensor_scalar(out=sxswg[:].rearrange("p (r c) -> p r c", c=3),
                            in0=sclsb[:, :, 0:3],
                            scalar1=rxsx[:, 1:2], scalar2=None, op0=ALU.mult)
            return e.tensor_scalar(out=sxswu[:].rearrange("p (r c) -> p r c", c=3),
                                   in0=sclsb[:, :, 3:6],
                                   scalar1=rxsx[:, 1:2], scalar2=None, op0=ALU.mult)
        w.step("vector", _ssw, reads=[R["sclsb"], R["rxsx"]], writes=[R["sxsw"]])

    run_phase(ph_p0)
    qctx.close()

    p1ctx = ExitStack()
    wpbuf = sb(p1ctx, "wpbuf", [128, 8, E], BF16)
    prel = sb(p1ctx, "prel", [128, 8, 512], BF16)
    qsqb = sb(p1ctx, "qsqb", [128, 2, 512], BF16)
    sdq = sb(p1ctx, "sdq", [16, 512], F32)
    rqbf = sb(p1ctx, "rqbf", [16, 512], BF16)
    wosb = sb(p1ctx, "wosb", [128, 8, E], BF16)

    def ph_p1(w, R):
        w.serial = SERIAL_P1
        pend = []

        def flush_pend():
            while pend:
                pend.pop(0)()

        def unit(ti, pj):
            t0 = 512 * ti
            flush_pend()
            bcols = bqc if pj == 0 else bkc
            ncol = qn8 if pj == 0 else kncol
            bnn = bqq if pj == 0 else bkk
            pssum, sumR = (psC0, R["psC0"]) if pj == 0 else (psC1, R["psC1"])
            dst, dstR = (qhat, R["qhat"]) if pj == 0 else (khat, R["khat"])
            for ec in range(8):
                pa = ec % 2
                def _mm(e, ec=ec, pa=pa):
                    for ei in range(8):
                        i = e.matmul(out=psA[pa][:],
                                     lhsT=wpbuf[:, ei, 128*ec:128*(ec+1)],
                                     rhs=xnfm[:, ei, t0:t0+512],
                                     start=(ei == 0), stop=(ei == 7))
                    return i
                w.step("tensor", _mm, reads=[R["wv"], R[f"xnfm{ti}"]],
                       writes=[R[f"psA{pa}"]])
                def _ep(e, ec=ec, pa=pa):
                    e.activation(out=qsqb[:, pa, :], in_=psA[pa][:], func=AF.Square,
                                 bias=bcols[:, ec:ec+1], scale=1.0)
                    return e.activation(out=prel[:, ec, :], in_=psA[pa][:],
                                        func=AF.Identity, scale=ncol[:],
                                        bias=bnn[:, ec:ec+1])
                w.step("scalar", _ep, reads=[R[f"psA{pa}"], R["biases"]],
                       writes=[R[f"qsqb{pa}"], R["prel0"]])
                if ec >= 1:
                    hs(ti, pj, ec - 1, pssum, sumR)
            hs(ti, pj, 7, pssum, sumR)
            w.step("scalar", lambda e: e.activation(
                out=sdq[:], in_=pssum[0:16, :], func=AF.Sqrt,
                scale=1.0 / D, bias=epsrms[0:16, :]),
                reads=[sumR], writes=[R["sdq"]])
            def _rc(e):
                w.zs(e, e.reciprocal(out=sdq[:], in_=sdq[:]))
                return e.tensor_copy(out=rqbf[:], in_=sdq[:])
            w.step("vector", _rc, reads=[R["sdq"]], writes=[R["sdq"], R["rqbf0"]])
            for ec in range(8):
                pend.append(lambda ec=ec, dst=dst, dstR=dstR, t0=t0:
                            fa(ec, dst, dstR, t0))

        def hs(ti, pj, ec, pssum, sumR):
            pa = ec % 2
            w.step("tensor", lambda e: e.matmul(
                out=pssum[0:16, :], lhsT=qh_ind[:, 16*ec:16*(ec+1)],
                rhs=qsqb[:, pa, :], start=(ec == 0), stop=(ec == 7)),
                reads=[R[f"qsqb{pa}"]], writes=[sumR])

        def fa(ec, dst, dstR, t0):
            pb = ec % 2
            w.step("tensor", lambda e: e.matmul(
                out=psB[pb][:], lhsT=rbc_ind[:, 128*ec:128*(ec+1)],
                rhs=rqbf[:], start=True, stop=True),
                reads=[R["rqbf0"]], writes=[R[f"psB{pb}"]])
            w.step("vector", lambda e: e.scalar_tensor_tensor(
                out=dst[:, ec, t0:t0+512], in0=prel[:, ec, :],
                scalar=1.0, in1=psB[pb][:], op0=ALU.mult, op1=ALU.mult),
                reads=[R["prel0"], R[f"psB{pb}"]], writes=[dstR])

        for pj in range(2):
            wsrc = wq_d if pj == 0 else wk_d
            w.step("gpsimd", lambda e, wsrc=wsrc: e.dma_start(
                out=wpbuf[:], in_=wsrc.rearrange("(ei p) eo -> p ei eo", p=128)),
                writes=[R["wv"]], sem="LG", n=16)
            for ti in range(NTI):
                unit(ti, pj)
        flush_pend()
        w.step("gpsimd", lambda e: e.dma_start(
            out=wpbuf[:], in_=wv_d.rearrange("(ei p) eo -> p ei eo", p=128)),
            writes=[R["wv"]], sem="LG", n=16)
        for tc in range(NT):
            for half in range(2):
                pa = (2 * tc + half) % 2
                def _vm(e, tc=tc, half=half, pa=pa):
                    for ei in range(8):
                        e.matmul(out=psA[pa][:], lhsT=xnfm[:, ei, 128*tc:128*(tc+1)],
                                 rhs=wpbuf[:, ei, 512*half:512*(half+1)],
                                 start=(ei == 0), stop=False)
                    return e.matmul(out=psA[pa][:], lhsT=onesg[0:1, :],
                                    rhs=bvrow[:, 512*half:512*(half+1)],
                                    start=False, stop=True)
                w.step("tensor", _vm, reads=[R[f"xnfm{tc//4}"], R["wv"], R["biases"]],
                       writes=[R[f"psA{pa}"]])
                w.step("scalar", lambda e, tc=tc, half=half, pa=pa: e.copy(
                    out=vsb[:, tc, :].rearrange("p (h d) -> p h d", d=65)
                        [:, 8*half:8*(half+1), 0:64],
                    in_=psA[pa][:].rearrange("p (h d) -> p h d", d=64)),
                    reads=[R[f"psA{pa}"]], writes=[R["vsb"]])

    run_phase(ph_p1, barrier=True)
    if debug:
        def ph_dbg1b(w, R):
            w.step("sync", lambda e: e.dma_start(out=dbg_wp, in_=wpbuf[:]),
                   reads=[R["wv"]], sem="ST", n=16)
            w.step("sync", lambda e: e.dma_start(out=dbg_qh, in_=qhat[:]),
                   reads=[R["qhat"]], sem="ST", n=16)
            w.step("sync", lambda e: e.dma_start(out=dbg_kh, in_=khat[:]),
                   reads=[R["khat"]], sem="ST", n=16)
            w.step("sync", lambda e: e.dma_start(out=dbg_v, in_=vsb[:]),
                   reads=[R["vsb"]], sem="ST", n=16)
        run_phase(ph_dbg1b)


    def ph_p4(w, R):
        w.serial = SERIAL_P4
        w.step("gpsimd", lambda e: e.dma_start(
            out=wosb[:], in_=wo_d.rearrange("(ei p) eo -> p ei eo", p=128)),
            writes=[R["wo"]], sem="LC0", n=16)

        def geom(qi):
            nw = min(qi + 1, 3)
            return nw, qi - nw + 1, 128 * nw, 384 - 128 * nw

        def stA(qi, h):
            k = qi * NH + h
            nw, wc0, Wd, moff = geom(qi)
            ch, rb, pa = h // 2, 64 * (h % 2), k % 2
            def _row(e):
                e.matmul(out=psA[pa][:, 0:Wd],
                         lhsT=qhat[rb:rb+64, ch, 128*qi:128*(qi+1)],
                         rhs=khat[rb:rb+64, ch, 128*wc0:128*wc0+Wd],
                         start=True, stop=False)
                return e.matmul(out=psA[pa][:, 0:Wd], lhsT=identbf[:],
                                rhs=maskrow[:, moff:384], start=False, stop=True)
            w.step("tensor", _row, reads=[R["qhat"], R["khat"]],
                   writes=[R[f"psA{pa}"]])

        def stM(qi, h):
            k = qi * NH + h
            nw, wc0, Wd, moff = geom(qi)
            pa = k % 2
            w.step("vector", lambda e: e.reduce_max(
                out=mnegf[:, pa:pa+1], in_=psA[pa][:, 0:Wd], axis=AX.X, negate=True),
                reads=[R[f"psA{pa}"]], writes=[R[f"mneg{pa}"]])

        def stE(qi, h):
            k = qi * NH + h
            nw, wc0, Wd, moff = geom(qi)
            pa = k % 2
            w.step("scalar", lambda e: e.activation(
                out=prow[:, pa, 0:Wd], in_=psA[pa][:, 0:Wd], func=AF.Exp,
                bias=mnegf[:, pa:pa+1], scale=1.0),
                reads=[R[f"psA{pa}"], R[f"mneg{pa}"]], writes=[R[f"prow{pa}"]])

        def stT(qi, h):
            k = qi * NH + h
            nw, wc0, Wd, moff = geom(qi)
            pa = k % 2
            def _tt(e):
                for c in range(nw):
                    i = e.transpose(out=psUT[pa][:, 128*c:128*(c+1)],
                                    in_=prow[:, pa, 128*c:128*(c+1)],
                                    identity=identbf[:])
                return i
            w.step("tensor", _tt, reads=[R[f"prow{pa}"]],
                   writes=[R["psU"] if pa == 0 else R["psT"]])

        def stPC(qi, h):
            k = qi * NH + h
            nw, wc0, Wd, moff = geom(qi)
            pa = k % 2
            w.step("scalar", lambda e: e.copy(
                out=ptsb[:, pa, 0:Wd], in_=psUT[pa][:, 0:Wd]),
                reads=[R["psU"] if pa == 0 else R["psT"]], writes=[R[f"ptsb{pa}"]])

        def stPV(qi, h):
            k = qi * NH + h
            nw, wc0, Wd, moff = geom(qi)
            pa = k % 2
            def _pv(e):
                for c in range(nw):
                    i = e.matmul(out=psC[pa][:, 0:65],
                                 lhsT=ptsb[:, pa, 128*c:128*(c+1)],
                                 rhs=vsb[:, wc0+c, 65*h:65*(h+1)],
                                 start=(c == 0), stop=(c == nw - 1))
                return i
            w.step("tensor", _pv, reads=[R[f"ptsb{pa}"], R["vsb"]],
                   writes=[R[f"psC{pa}"]])

        def stRZ(qi, h):
            k = qi * NH + h
            pa = k % 2
            w.step("vector", lambda e: e.reciprocal(
                out=recipz[:, pa:pa+1], in_=psC[pa][:, 64:65]),
                reads=[R[f"psC{pa}"]], writes=[R[f"recipz{pa}"]])

        def stEV(qi, h):
            k = qi * NH + h
            pa = k % 2
            w.step("scalar", lambda e: e.activation(
                out=attok[:, 64*h:64*(h+1)], in_=psC[pa][:, 0:64],
                func=AF.Copy, scale=recipz[:, pa:pa+1]),
                reads=[R[f"psC{pa}"], R[f"recipz{pa}"]], writes=[R["attok"]])

        def tail(qi):
            def _tr(e):
                for s8 in range(8):
                    i = e.transpose(out=psT[:, 128*s8:128*(s8+1)],
                                    in_=attok[:, 128*s8:128*(s8+1)],
                                    identity=identbf[:])
                return i
            w.step("tensor", _tr, reads=[R["attok"]], writes=[R["psT"]])
            w.step("scalar", lambda e: e.copy(
                out=afm[:], in_=psT[:].rearrange("p (ec t) -> p ec t", ec=8)),
                reads=[R["psT"]], writes=[R["afm"]])
            def _wo(e):
                for half in range(2):
                    for s8 in range(8):
                        e.matmul(out=psB[half][:], lhsT=afm[:, s8, :],
                                 rhs=wosb[:, s8, 512*half:512*(half+1)],
                                 start=(s8 == 0), stop=False)
                    i = e.matmul(out=psB[half][:], lhsT=onesg[0:1, :],
                                 rhs=borow[:, 512*half:512*(half+1)],
                                 start=False, stop=True)
                return i
            w.step("tensor", _wo, reads=[R["afm"], R["wo"], R["biases"]],
                   writes=[R["psB0"], R["psB1"]])
            pq = qi % 2
            def _ev(e):
                e.activation(out=attst[:, pq, 0:512], in_=psB[0][:], func=AF.Copy)
                return e.activation(out=attst[:, pq, 512:1024], in_=psB[1][:],
                                    func=AF.Copy)
            w.step("scalar", _ev, reads=[R["psB0"], R["psB1"]],
                   writes=[R[f"attst{pq}"]])
            w.step("sync", lambda e: e.dma_start(
                out=attno_hbm[128*qi:128*(qi+1), :], in_=attst[:, pq, :]),
                reads=[R[f"attst{pq}"]], writes=[R["d_attno"]], sem=f"SA{pq}",
                n=16)

        KK = NT * NH
        def hk(k):
            return k // NH, k % NH
        for k in range(KK + 3):
            if k < KK:
                qi, h = hk(k)
                stA(qi, h)
                stM(qi, h)
                stE(qi, h)
            if 1 <= k < KK + 1:
                qi, h = hk(k - 1)
                stT(qi, h)
                stPC(qi, h)
            if 2 <= k < KK + 2:
                qi, h = hk(k - 2)
                stPV(qi, h)
                stRZ(qi, h)
            if 3 <= k < KK + 3:
                qi, h = hk(k - 3)
                stEV(qi, h)
                if h == NH - 1:
                    tail(qi)

    run_phase(ph_p4)
    if debug:
        def ph_dbg4(w, R):
            w.step("sync", lambda e: e.dma_start(out=dbg_at, in_=attno_hbm),
                   reads=[R["d_attno"]], sem="ST", n=16)
        run_phase(ph_dbg4)
    p1ctx.close()
    lctx.close()

    pwctx = ExitStack()
    wdt = sb(pwctx, "wdt", [128, NI, E], BF16)
    p2ctx = ExitStack()
    xqi = sb(p2ctx, "xqi", [128, E], I32)
    xqb = sb(p2ctx, "xqb", [128, 2, E], BF16)

    def ph_p2(w, R):
        w.serial = SERIAL_P2
        def _ab(e):
            e.tensor_scalar(out=acol[:], in0=stat[:, 64:80], scalar1=rxsx[:, 0:1],
                            scalar2=None, op0=ALU.mult)
            return e.tensor_scalar(out=b2col[:], in0=stat[:, 80:96],
                                   scalar1=rxsx[:, 0:1], scalar2=None, op0=ALU.mult)
        w.step("vector", _ab, reads=[R["stat"], R["rxsx"]], writes=[R["acol"]])
        for tc in range(NT):
            par = tc % 2
            xb = R[f"xbuf{par}"]
            w.step("sync", lambda e, tc=tc, par=par: e.dma_start(
                out=xbuf[:, par, :], in_=x_d[128*tc:128*(tc+1), :]),
                writes=[xb], sem=f"LA{par}", n=16)
            def _q(e, tc=tc, par=par):
                w.zs(e, e.tensor_scalar(out=xqi[:], in0=xbuf[:, par, :],
                                        scalar1=acol[:, tc:tc+1],
                                        scalar2=b2col[:, tc:tc+1],
                                        op0=ALU.mult, op1=ALU.add))
                return e.tensor_scalar(out=xqb[:, par, :], in0=xqi[:],
                                       scalar1=-128.0, scalar2=127.0,
                                       op0=ALU.max, op1=ALU.min)
            w.step("vector", _q, reads=[xb, R["acol"]],
                   writes=[R["xqi"], R[f"xqb{par}"]])
            def _tr(e, par=par):
                for s8 in range(8):
                    i = e.transpose(out=psT[:, 128*s8:128*(s8+1)],
                                    in_=xqb[:, par, 128*s8:128*(s8+1)],
                                    identity=identbf[:])
                return i
            w.step("tensor", _tr, reads=[R[f"xqb{par}"]], writes=[R["psT"]])
            w.step("scalar", lambda e, tc=tc: e.copy(
                out=xnfm[:, :, 128*tc:128*(tc+1)],
                in_=psT[:].rearrange("p (ec t) -> p ec t", ec=8)),
                reads=[R["psT"]], writes=[R[f"xnfm{tc//4}"]])

    run_phase(ph_p2, barrier=True)

    p3ctx = ExitStack()
    wgt = sb(p3ctx, "wgt", [128, 2, 8, 128], BF16)
    wut = sb(p3ctx, "wut", [128, 2, 8, 128], BF16)
    silub = sb(p3ctx, "silub", [128, 512], F32)
    dbgslb = sb(p3ctx, "dbgslb", [128, 4, 512], F32) if debug else None
    upsb = sb(p3ctx, "upsb", [128, 512], F32)
    hbuf = sb(p3ctx, "hbuf", [128, 2, 512], F32)

    def ph_p3(w, R):
        w.serial = SERIAL_P3
        for c4 in range(4):
            w.step("gpsimd", lambda e, c4=c4: e.dma_start(
                out=wdt[:, :, 256*c4:256*(c4+1)],
                in_=wdT_hbm[:, :, 256*c4:256*(c4+1)].rearrange("g p e -> p g e")),
                reads=[R["d_wdT"]], writes=[R["wdt"]], sem="LG", n=16)
        for g in range(NI):
            r8, j = g // 3, g % 3
            gp = g % 2
            w.step("gpsimd", lambda e, r8=r8, j=j, gp=gp: e.dma_start(
                out=wgt[:, gp, :, :], in_=agg_o[r8, j]),
                reads=[R["d_aggo"]], writes=[R[f"wgt{gp}"]], sem=f"LB{gp}", n=16)
            w.step("gpsimd", lambda e, r8=r8, j=j, gp=gp: e.dma_start(
                out=wut[:, gp, :, :], in_=agu_o[r8, j]),
                reads=[R["d_aguo"]], writes=[R[f"wut{gp}"]], sem=f"LB{gp}", n=16)
            for ti in range(NTI):
                t0 = 512 * ti
                pg = ti % 2
                def _mm(e, gp=gp, pg=pg, t0=t0):
                    for ei in range(8):
                        e.matmul(out=psA[pg][:], lhsT=wgt[:, gp, ei, :],
                                 rhs=xnfm[:, ei, t0:t0+512],
                                 start=(ei == 0), stop=(ei == 7))
                    for ei in range(8):
                        i = e.matmul(out=psB[pg][:], lhsT=wut[:, gp, ei, :],
                                     rhs=xnfm[:, ei, t0:t0+512],
                                     start=(ei == 0), stop=(ei == 7))
                    return i
                w.step("tensor", _mm,
                       reads=[R[f"wgt{gp}"], R[f"wut{gp}"], R[f"xnfm{ti}"]],
                       writes=[R[f"psA{pg}"], R[f"psB{pg}"]])
                def _ep(e, g=g, pg=pg):
                    e.activation(out=silub[:], in_=psA[pg][:], func=AF.Silu,
                                 scale=sxswg[:, g:g+1])
                    return e.activation(out=upsb[:], in_=psB[pg][:], func=AF.Copy,
                                        scale=sxswu[:, g:g+1])
                w.step("scalar", _ep,
                       reads=[R[f"psA{pg}"], R[f"psB{pg}"], R["sxsw"]],
                       writes=[R["silub"], R["upsb"]])
                def _h(e, g=g, ti=ti, pg=pg):
                    w.zs(e, e.tensor_tensor(out=hbuf[:, pg, :], in0=silub[:],
                                            in1=upsb[:], op=ALU.mult))
                    return e.tensor_reduce(out=habs[:, 4*g+ti:4*g+ti+1],
                                           in_=hbuf[:, pg, :], axis=AX.X, op=ALU.max,
                                           apply_absolute_value=True)
                if debug and g == 0 and ti == 0:
                    def _dsl(e):
                        e.tensor_copy(out=dbgslb[:, 0, :], in_=silub[:])
                        e.tensor_copy(out=dbgslb[:, 1, :], in_=upsb[:])
                        e.tensor_copy(out=dbgslb[:, 2, :], in_=psA[0][:])
                        return e.tensor_copy(out=dbgslb[:, 3, :], in_=psB[0][:])
                    w.step("vector", _dsl,
                           reads=[R["silub"], R["upsb"], R["psA0"], R["psB0"]],
                           writes=[R["tmpdbg"]])
                    w.step("sync", lambda e: e.dma_start(out=dbg_sl, in_=dbgslb[:]),
                           reads=[R["tmpdbg"]], sem="ST", n=16)
                    w.step("sync", lambda e: e.dma_start(out=dbg_wg0, in_=wgt[:, 0, :, :]),
                           reads=[R["wgt0"]], sem="ST", n=16)
                    w.step("sync", lambda e: e.dma_start(out=dbg_ag0, in_=agg_o[0, 0]),
                           reads=[R["d_aggo"]], sem="ST", n=16)
                w.step("vector", _h, reads=[R["silub"], R["upsb"]],
                       writes=[R[f"hbuf{pg}"], R["habs"]])
                w.step("sync", lambda e, g=g, t0=t0, pg=pg: e.dma_start(
                    out=h_hbm[g, :, t0:t0+512], in_=hbuf[:, pg, :]),
                    reads=[R[f"hbuf{pg}"]], writes=[R["d_hhbm"]], sem=f"SA{pg}",
                    n=16)
        def _hr(e):
            e.memset(gtr[:], 0.0)
            return e.tensor_reduce(out=gtr[:, 0:1], in_=habs[:, 0:88], axis=AX.X,
                                   op=ALU.max)
        w.step("vector", _hr, reads=[R["habs"]], writes=[R["small"]])
        w.step("tensor", lambda e: e.transpose(out=psC1[0:32, 0:128],
                                               in_=gtr[:, 0:32], identity=identf[:]),
               reads=[R["small"]], writes=[R["psC1"]])
        w.step("vector", lambda e: e.reduce_max(out=sc11[:, 5:6],
                                                in_=psC1[0:1, 0:128], axis=AX.X),
               reads=[R["psC1"]], writes=[R["sc11"]])
        w.step("gpsimd", lambda e: e.dma_start(out=cc2i, in_=sc11[:, 5:6]),
               reads=[R["sc11"]], writes=[R["d_cc2i"]], sem="LC1", n=16)
        w.step("gpsimd", lambda e: e.collective_compute(
            "AllReduce", ALU.max, replica_groups=[list(range(8))],
            ins=[cc2i.opt()], outs=[cc2o.opt()]),
            reads=[R["d_cc2i"]], writes=[R["d_cc2o"]], sem="G")
        w.step("gpsimd", lambda e: e.dma_start(out=sc11[:, 5:6], in_=cc2o),
               reads=[R["d_cc2o"]], writes=[R["sc11"]], sem="LD0", n=16)
        w.step("scalar", lambda e: e.activation(out=sc11[:, 6:7], in_=clipd_s[:],
                                                func=AF.Sigmoid),
               reads=[R["sc11"]], writes=[R["sc11"]])
        def _sh(e):
            w.zs(e, e.tensor_tensor(out=sc11[:, 7:8], in0=sc11[:, 6:7],
                                    in1=sc11[:, 5:6], op=ALU.mult))
            w.zs(e, e.reciprocal(out=sc11[:, 5:6], in_=sc11[:, 7:8]))
            w.zs(e, e.tensor_scalar(out=sc11[:, 5:6], in0=sc11[:, 5:6], scalar1=QMAX,
                                    scalar2=None, op0=ALU.mult))
            return e.tensor_scalar(out=sc11[:, 6:7], in0=sc11[:, 7:8],
                                   scalar1=1.0/QMAX, scalar2=None, op0=ALU.mult)
        w.step("vector", _sh, reads=[R["sc11"]], writes=[R["sc11"]])
        def _bc(e):
            e.matmul(out=psC0[:, 2:3], lhsT=onesf[:], rhs=sc11[:, 5:6],
                     start=True, stop=False)
            return e.matmul(out=psC0[:, 3:4], lhsT=onesf[:], rhs=sc11[:, 6:7],
                            start=False, stop=True)
        w.step("tensor", _bc, reads=[R["sc11"]], writes=[R["psC0"]])
        w.step("vector", lambda e: e.tensor_copy(out=rxsx[:, 2:4], in_=psC0[:, 2:4]),
               reads=[R["psC0"]], writes=[R["rxsx"]])

    run_phase(ph_p3)
    if debug:
        def ph_dbg3(w, R):
            w.step("sync", lambda e: e.dma_start(out=dbg_h, in_=h_hbm),
                   reads=[R["d_hhbm"]], sem="ST", n=16)
            w.step("sync", lambda e: e.dma_start(out=dbg_sc, in_=sc11[:]),
                   reads=[R["sc11"]], sem="ST", n=16)
            w.step("sync", lambda e: e.dma_start(out=dbg_rx, in_=rxsx[:]),
                   reads=[R["rxsx"]], sem="ST", n=16)
            w.step("sync", lambda e: e.dma_start(out=dbg_ssw, in_=sxswg[:]),
                   reads=[R["sxsw"]], sem="ST", n=16)
            w.step("sync", lambda e: e.dma_start(out=dbg_agg, in_=agg_o),
                   reads=[R["d_aggo"]], sem="ST", n=16)
        run_phase(ph_dbg3)
    p3ctx.close()
    p2ctx.close()

    p5ctx = ExitStack()
    hqsb = sb(p5ctx, "hqsb", [128, 2, NI, 512], BF16)
    hstg = sb(p5ctx, "hstg", [128, 2, 512], F32)
    hi32 = sb(p5ctx, "hi32", [128, 512], I32)
    hclip = sb(p5ctx, "hclip", [128, 512], BF16)
    outsb = sb(p5ctx, "outsb", [128, 2, E], F32)
    tmpadd = sb(p5ctx, "tmpadd", [128, 512], F32)

    def ph_p5(w, R):
        def qblock(ti):
            t0 = 512 * ti
            hb = ti % 2
            w.serial = SERIAL_P5 in (1, 2)
            for g in range(NI):
                gp = g % 2
                w.step("sync", lambda e, g=g, t0=t0, gp=gp: e.dma_start(
                    out=hstg[:, gp, :], in_=h_hbm[g, :, t0:t0+512]),
                    reads=[R["d_hhbm"]], writes=[R[f"hstg{gp}"]], sem=f"LA{gp}",
                    n=16)
                def _q(e, gp=gp):
                    w.zs(e, e.tensor_scalar(out=hi32[:], in0=hstg[:, gp, :],
                                            scalar1=rxsx[:, 2:3], scalar2=None,
                                            op0=ALU.mult))
                    return e.tensor_scalar(out=hclip[:], in0=hi32[:],
                                           scalar1=-128.0, scalar2=127.0,
                                           op0=ALU.max, op1=ALU.min)
                w.step("vector", _q, reads=[R[f"hstg{gp}"], R["rxsx"]],
                       writes=[R["hint"]])
                w.step("scalar", lambda e, g=g, hb=hb: e.activation(
                    out=hqsb[:, hb, g, :], in_=hclip[:], func=AF.Copy,
                    scale=rxsx[:, 3:4]),
                    reads=[R["hint"], R["rxsx"]], writes=[R[f"hq{hb}_{g}"]])

        def dblock(ti):
            hb = ti % 2
            w.serial = SERIAL_P5 in (1, 3)
            for tq in range(4):
                qi = 4 * ti + tq
                pq = qi % 2
                qb = qi % 3
                bank0 = (psA0, psB0, psC0)[qb]
                bank1 = (psA1, psB1, psC1)[qb]
                n0 = ("psA0", "psB0", "psC0")[qb]
                n1 = ("psA1", "psB1", "psC1")[qb]
                w.step("gpsimd", lambda e, qi=qi, pq=pq: e.dma_start(
                    out=xbuf[:, pq, :], in_=x_d[128*qi:128*(qi+1), :]),
                    writes=[R[f"xbuf{pq}"]], sem=f"LB{pq}", n=16)
                w.step("gpsimd", lambda e, qi=qi, pq=pq: e.dma_start(
                    out=attst[:, pq, :], in_=attno_hbm[128*qi:128*(qi+1), :]),
                    reads=[R["d_attno"]], writes=[R[f"attst{pq}"]], sem=f"LC{pq}",
                    n=16)
                def _dn(e, tq=tq, hb=hb, bank0=bank0, bank1=bank1):
                    for half, bank in ((0, bank0), (1, bank1)):
                        for g in range(NI):
                            i = e.matmul(out=bank[:],
                                         lhsT=hqsb[:, hb, g, 128*tq:128*(tq+1)],
                                         rhs=wdt[:, g, 512*half:512*(half+1)],
                                         start=(g == 0), stop=(g == NI - 1))
                    return i
                w.step("tensor", _dn,
                       reads=[R["wdt"]] + [R[f"hq{hb}_{g}"] for g in range(NI)],
                       writes=[R[n0], R[n1]])
                def _fin(e, pq=pq, bank0=bank0, bank1=bank1):
                    for half, bank in ((0, bank0), (1, bank1)):
                        w.zs(e, e.tensor_tensor(out=tmpadd[:], in0=bank[:],
                                                in1=xbuf[:, pq, 512*half:512*(half+1)],
                                                op=ALU.add))
                        i = e.tensor_tensor(out=outsb[:, pq, 512*half:512*(half+1)],
                                            in0=tmpadd[:],
                                            in1=attst[:, pq, 512*half:512*(half+1)],
                                            op=ALU.add)
                    return i
                w.step("vector", _fin,
                       reads=[R[n0], R[n1], R[f"xbuf{pq}"],
                              R[f"attst{pq}"]],
                       writes=[R[f"outsb{pq}"], R["tmpadd"]])
                w.step("gpsimd", lambda e, qi=qi, pq=pq: e.dma_start(
                    out=out_d[128*qi:128*(qi+1), :], in_=outsb[:, pq, :]),
                    reads=[R[f"outsb{pq}"]], writes=[], sem=f"SA{pq}", n=16)

        for ti in range(NTI):
            qblock(ti)
            if ti > 0:
                dblock(ti - 1)
        dblock(NTI - 1)
        if debug:
            for g in range(NI):
                w.step("sync", lambda e, g=g: e.dma_start(
                    out=dbg_hq[:, g, :] if True else None, in_=hqsb[:, 1, g, :]),
                    reads=[R["hq1_" + str(g)]], sem="ST", n=16)
        if w.me == "sync":
            for s_ in ("ST", "SA0", "SA1"):
                if w.c[s_] > 0:
                    w.eng.wait_ge(semh[s_], w.c[s_])

    run_phase(ph_p5, barrier=True)
    p5ctx.close()
    pwctx.close()
    xctx.close()

    octx.close()
    return nc


_NC_CACHE = {}


def kernel(L0_attn_x, L0_attn_wq, L0_attn_bq, L0_attn_wk, L0_attn_bk, L0_attn_wv,
           L0_attn_bv, L0_attn_wo, L0_attn_bo, L0_attn_qn_w, L0_attn_kn_w,
           L0_ffn_clip_x, L0_ffn_clip_down, L0_ffn_w_gate, L0_ffn_w_up,
           L0_ffn_w_down):
    f32 = lambda a: np.ascontiguousarray(np.asarray(a, dtype=np.float32))
    x = f32(L0_attn_x)
    wq, wk, wv, wo = f32(L0_attn_wq), f32(L0_attn_wk), f32(L0_attn_wv), f32(L0_attn_wo)
    bq, bk = f32(L0_attn_bq).reshape(1, E), f32(L0_attn_bk).reshape(1, E)
    bv, bo = f32(L0_attn_bv).reshape(1, E), f32(L0_attn_bo).reshape(1, E)
    qn, kn = f32(L0_attn_qn_w).reshape(1, D), f32(L0_attn_kn_w).reshape(1, D)
    clipx = f32(L0_ffn_clip_x).reshape(1, 1)
    clipd = f32(L0_ffn_clip_down).reshape(1, 1)
    wg, wu, wd = f32(L0_ffn_w_gate), f32(L0_ffn_w_up), f32(L0_ffn_w_down)

    consts = _host_consts()
    if "nc" not in _NC_CACHE:
        _NC_CACHE["nc"] = build_nc()
    nc = _NC_CACHE["nc"]

    in_maps = _mk_inmaps(x, wq, wk, wv, wo, bq, bk, bv, bo, qn, kn, clipx, clipd,
                         wg, wu, wd, consts)
    res = run_bass_kernel_spmd(nc, in_maps, core_ids=list(range(8)))
    out = np.stack([res.results[c]["out"] for c in range(8)], axis=0)
    return out.astype(np.float32)


def _mk_inmaps(x, wq, wk, wv, wo, bq, bk, bv, bo, qn, kn, clipx, clipd,
               wg, wu, wd, consts):
    in_maps = []
    for c in range(8):
        lo = 3 * c
        gsh = np.ones((3, 128, E), np.float32)
        ush = np.ones((3, 128, E), np.float32)
        for j in range(3):
            g = lo + j
            if g < NI:
                gsh[j] = wg[128*g:128*(g+1), :]
                ush[j] = wu[128*g:128*(g+1), :]
        m = dict(x=np.ascontiguousarray(x[c]),
                 wq=wq, wk=wk, wv=wv, wo=wo, bq=bq, bk=bk, bv=bv, bo=bo,
                 qn=qn, kn=kn, clipx=clipx, clipd=clipd,
                 wg_sh=np.ascontiguousarray(gsh),
                 wu_sh=np.ascontiguousarray(ush),
                 wd_sh=np.ascontiguousarray(wd[128*c:128*(c+1), :]))
        m.update(consts)
        in_maps.append(m)
    return in_maps


# version 19 (final)
# speedup vs baseline: 1.0286x; 1.0255x over previous
import numpy as np
import ml_dtypes
from contextlib import ExitStack
import concourse.bass as bass
import concourse.mybir as mybir
from concourse.bass_utils import run_bass_kernel_spmd

F32 = mybir.dt.float32
BF16 = mybir.dt.bfloat16
I32 = mybir.dt.int32
AF = mybir.ActivationFunctionType
ALU = mybir.AluOpType
AX = mybir.AxisListType

T = 2048
E = 1024
NH = 16
D = 64
IF = 2816
NT = T // 128
NE = E // 128
NI = IF // 128
NTI = 4
NEG = -1e9
LN_EPS = 1e-5
RMS_EPS = float(np.finfo(np.float32).eps)
QMAX = 127.0
BF = ml_dtypes.bfloat16
SERIAL_P5 = 0
SERIAL_P4 = False
SERIAL_P1 = False
SERIAL_P0 = False
SERIAL_Q = False
SERIAL_P2 = False
SERIAL_P3 = False


def _host_consts():
    ident = np.eye(128, dtype=np.float32)
    dq = np.arange(128)[:, None]
    o = np.arange(128)[None, :]
    m0 = np.where(o <= dq, NEG, 0.0).astype(np.float32)
    m2 = np.where(o > dq, NEG, 0.0).astype(np.float32)
    maskrow = np.concatenate([m0, np.zeros((128, 128), np.float32), m2], axis=1)
    maskT = np.ascontiguousarray(
        np.concatenate([m0.T, np.zeros((128, 128), np.float32), m2.T], axis=1))
    qh = np.zeros((128, 128), np.float32)
    for ec in range(8):
        qh[0:64, 16 * ec + 2 * ec] = 1.0
        qh[64:128, 16 * ec + 2 * ec + 1] = 1.0
    rbc = np.zeros((16, 1024), np.float32)
    for ec in range(8):
        rbc[2 * ec, 128 * ec: 128 * ec + 64] = 1.0
        rbc[2 * ec + 1, 128 * ec + 64: 128 * ec + 128] = 1.0
    return {
        "identbf": ident.astype(BF),
        "identf": ident,
        "onesg": np.ones((128, 128), np.float32).astype(BF),
        "onesf": np.ones((1, 128), np.float32),
        "maskrow": maskrow.astype(BF),
        "maskTt": maskT.astype(BF),
        "qh_ind": qh.astype(BF),
        "rbc_ind": rbc.astype(BF),
    }


class RES:
    __slots__ = ("w", "r", "dram")
    def __init__(self, dram=False):
        self.w = None
        self.r = []
        self.dram = dram


class Walk:
    SEM_OF = {"sync": "LX", "gpsimd": "G", "tensor": "T", "vector": "V",
              "scalar": "S"}
    DMA_SEMS = ("LX", "LG", "ST", "LA0", "LA1", "LB0", "LB1", "LC0", "LC1",
                "LD0", "LD1", "SA0", "SA1")

    def __init__(self, semh, counts, res, eng, me, consts_res, zst):
        self.semh = semh
        self.c = counts
        self.res = res
        self.eng = eng
        self.me = me
        self.consts = consts_res
        self.fenced = False
        self.Z = zst
        self.serial = False

    def sync_all(self):
        for s_ in ("T", "V", "S", "G") + self.DMA_SEMS:
            if self.c[s_] > 0:
                self.eng.wait_ge(self.semh[s_], self.c[s_])

    def zs(self, e, instr):
        k = self.me
        self.Z[k] += 1
        instr.then_inc(self.semh["Z" + k], 1)
        e.wait_ge(self.semh["Z" + k], self.Z[k])
        return instr

    def step(self, who, emit, reads=(), writes=(), sem=None, n=1):
        sem = sem or self.SEM_OF[who]
        on = (who == self.me)
        instr = None
        rd = list(reads)
        if on:
            if self.serial:
                self.sync_all()
            if not self.fenced and self.consts is not None:
                rd.append(self.consts)
                self.fenced = True
            need_r = {}
            need_w = {}
            def _need(d, tok):
                if tok is not None:
                    s_, v_ = tok
                    if v_ > d.get(s_, -1):
                        d[s_] = v_
            for b in rd:
                _need(need_r, b.w)
            for b in writes:
                _need(need_w, b.w)
                for tok in b.r:
                    _need(need_w, tok)
            for s_, v_ in sorted(need_r.items()):
                self.eng.wait_ge(self.semh[s_], v_)
            for s_, v_ in sorted(need_w.items()):
                if need_r.get(s_, -1) >= v_:
                    continue
                if s_ == sem and s_ not in self.DMA_SEMS:
                    continue
                self.eng.wait_ge(self.semh[s_], v_)
            instr = emit(self.eng)
        self.c[sem] += n
        tok = (sem, self.c[sem])
        if on and instr is not None:
            instr.then_inc(self.semh[sem], n)
        for b in reads:
            b.r.append(tok)
        for b in writes:
            b.w = tok
            b.r = []
        return instr


def build_nc(debug=False):
    nc = bass.Bass("TRN2", target_bir_lowering=False)
    dti = lambda n, s, d: nc.dram_tensor(n, s, d, kind="ExternalInput").ap()
    x_d = dti("x", [T, E], F32)
    wq_d = dti("wq", [E, E], F32); wk_d = dti("wk", [E, E], F32)
    wv_d = dti("wv", [E, E], F32); wo_d = dti("wo", [E, E], F32)
    bq_d = dti("bq", [1, E], F32); bk_d = dti("bk", [1, E], F32)
    bv_d = dti("bv", [1, E], F32); bo_d = dti("bo", [1, E], F32)
    qn_d = dti("qn", [1, D], F32); kn_d = dti("kn", [1, D], F32)
    clipx_d = dti("clipx", [1, 1], F32); clipd_d = dti("clipd", [1, 1], F32)
    wg_d = dti("wg_sh", [3, 128, E], F32)
    wu_d = dti("wu_sh", [3, 128, E], F32)
    wd_d = dti("wd_sh", [128, IF], F32)
    identbf_d = dti("identbf", [128, 128], BF16)
    identf_d = dti("identf", [128, 128], F32)
    onesg_d = dti("onesg", [128, 128], BF16)
    onesf_d = dti("onesf", [1, 128], F32)
    maskrow_d = dti("maskrow", [128, 384], BF16)
    maskTt_d = dti("maskTt", [128, 384], BF16)
    qh_d = dti("qh_ind", [128, 128], BF16)
    rbc_d = dti("rbc_ind", [16, 1024], BF16)
    out_d = nc.dram_tensor("out", [T, E], F32, kind="ExternalOutput").ap()
    h_hbm = nc.dram_tensor("h_hbm", [NI, 128, T], F32).ap()
    attno_hbm = nc.dram_tensor("attno_hbm", [T, E], BF16).ap()
    cc1i = nc.dram_tensor("cc1i", [1, 1], F32).ap()
    cc1o = nc.dram_tensor("cc1o", [1, 1], F32).ap()
    cc2i = nc.dram_tensor("cc2i", [1, 1], F32).ap()
    cc2o = nc.dram_tensor("cc2o", [1, 1], F32).ap()
    agg_i = nc.dram_tensor("agg_i", [3, 128, 8, 128], BF16).ap()
    agg_o = nc.dram_tensor("agg_o", [8, 3, 128, 8, 128], BF16).ap()
    agu_i = nc.dram_tensor("agu_i", [3, 128, 8, 128], BF16).ap()
    agu_o = nc.dram_tensor("agu_o", [8, 3, 128, 8, 128], BF16).ap()
    agd_i = nc.dram_tensor("agd_i", [NI, 128, 128], BF16).ap()
    agd_o = nc.dram_tensor("agd_o", [8, NI, 128, 128], BF16).ap()
    scl_i = nc.dram_tensor("scl_i", [128, 8], F32).ap()
    scl_o = nc.dram_tensor("scl_o", [8, 128, 8], F32).ap()
    wdT_hbm = nc.dram_tensor("wdT_hbm", [NI, 128, E], BF16).ap()
    if debug:
        dbg_xn = nc.dram_tensor("dbg_xn", [128, 8, T], BF16, kind="ExternalOutput").ap()
        dbg_qh = nc.dram_tensor("dbg_qh", [128, 8, T], BF16, kind="ExternalOutput").ap()
        dbg_kh = nc.dram_tensor("dbg_kh", [128, 8, T], BF16, kind="ExternalOutput").ap()
        dbg_v = nc.dram_tensor("dbg_v", [128, 16, NH * 65], BF16, kind="ExternalOutput").ap()
        dbg_sc = nc.dram_tensor("dbg_sc", [1, 8], F32, kind="ExternalOutput").ap()
        dbg_rx = nc.dram_tensor("dbg_rx", [128, 4], F32, kind="ExternalOutput").ap()
        dbg_at = nc.dram_tensor("dbg_at", [T, E], BF16, kind="ExternalOutput").ap()
        dbg_h = nc.dram_tensor("dbg_h", [NI, 128, T], F32, kind="ExternalOutput").ap()
        dbg_st = nc.dram_tensor("dbg_st", [128, 96], F32, kind="ExternalOutput").ap()
        dbg_hq = nc.dram_tensor("dbg_hq", [128, NI, 512], BF16, kind="ExternalOutput").ap()
        dbg_ssw = nc.dram_tensor("dbg_ssw", [128, 24], F32, kind="ExternalOutput").ap()
        dbg_agg = nc.dram_tensor("dbg_agg", [8, 3, 128, 8, 128], BF16, kind="ExternalOutput").ap()
        dbg_sl = nc.dram_tensor("dbg_sl", [128, 4, 512], F32, kind="ExternalOutput").ap()
        dbg_wg0 = nc.dram_tensor("dbg_wg0", [128, 8, 128], BF16, kind="ExternalOutput").ap()
        dbg_ag0 = nc.dram_tensor("dbg_ag0", [128, 8, 128], BF16, kind="ExternalOutput").ap()
        dbg_wp = nc.dram_tensor("dbg_wp", [128, 8, E], BF16, kind="ExternalOutput").ap()

    octx = ExitStack()
    sb = lambda st, n, s, d: st.enter_context(nc.sbuf_tensor(n, s, d))
    pt = lambda n, s, d: octx.enter_context(nc.psum_tensor(n, s, d))

    identbf = sb(octx, "identbf_s", [128, 128], BF16)
    identf = sb(octx, "identf_s", [128, 128], F32)
    onesg = sb(octx, "onesg_s", [128, 128], BF16)
    onesf = sb(octx, "onesf_s", [1, 128], F32)
    maskrow = sb(octx, "maskrow_s", [128, 384], BF16)
    maskTt = sb(octx, "maskTt_s", [128, 384], BF16)
    qh_ind = sb(octx, "qh_ind_s", [128, 128], BF16)
    rbc_ind = sb(octx, "rbc_ind_s", [16, 1024], BF16)
    bqc = sb(octx, "bqc", [128, 8], F32); bkc = sb(octx, "bkc", [128, 8], F32)
    bvrow = sb(octx, "bvrow", [1, E], BF16); borow = sb(octx, "borow", [1, E], BF16)
    qncol = sb(octx, "qncol", [128, 1], F32); kncol = sb(octx, "kncol", [128, 1], F32)
    qn8 = sb(octx, "qn8", [128, 1], F32)
    bqq = sb(octx, "bqq", [128, 8], F32); bkk = sb(octx, "bkk", [128, 8], F32)
    clipx_s = sb(octx, "clipx_s", [1, 1], F32); clipd_s = sb(octx, "clipd_s", [1, 1], F32)
    stat = sb(octx, "stat", [128, 96], F32)
    sumsq = sb(octx, "sumsq", [128, 16], F32)
    rabs = sb(octx, "rabs", [128, 16], F32)
    habs = sb(octx, "habs", [128, 96], F32)
    sc11 = sb(octx, "sc11", [1, 8], F32)
    rxsx = sb(octx, "rxsx", [128, 4], F32)
    acol = sb(octx, "acol", [128, 16], F32); b2col = sb(octx, "b2col", [128, 16], F32)
    sclsb = sb(octx, "sclsb", [128, 8, 8], F32)
    sxswg = sb(octx, "sxswg", [128, 24], F32)
    sxswu = sb(octx, "sxswu", [128, 24], F32)
    vsmall = sb(octx, "vsmall", [128, 2], F32)
    gtr = sb(octx, "gtr", [128, 32], F32)
    epsln = sb(octx, "epsln", [128, 1], F32)
    epsrms = sb(octx, "epsrms", [128, 1], F32)
    xbuf = sb(octx, "xbuf", [128, 2, E], F32)
    mnegf = sb(octx, "mnegf", [128, 2], F32)
    recipz = sb(octx, "recipz", [128, 2], F32)
    ptsb = sb(octx, "ptsb", [128, 2, 384], BF16)
    prow = sb(octx, "prow", [128, 2, 384], BF16)
    attok = sb(octx, "attok", [128, E], BF16)
    afm = sb(octx, "afm", [128, 8, 128], BF16)
    attst = sb(octx, "attst", [128, 2, E], BF16)

    psA0 = pt("psA0", [128, 512], F32)
    psA1 = pt("psA1", [128, 512], F32)
    psB0 = pt("psB0", [128, 512], F32)
    psB1 = pt("psB1", [128, 512], F32)
    psC0 = pt("psC0", [128, 512], F32)
    psC1 = pt("psC1", [128, 512], F32)
    psT = pt("psT", [128, 1024], BF16)
    psU = pt("psU", [128, 512], BF16)
    psA = [psA0, psA1]; psB = [psB0, psB1]; psC = [psC0, psC1]
    psUT = [psU, psT]

    semh = {}
    for n_ in ("LX", "LG", "T", "V", "S", "G", "ST",
               "LA0", "LA1", "LB0", "LB1", "LC0", "LC1", "LD0", "LD1",
               "SA0", "SA1",
               "Zvector", "Zscalar", "Zgpsimd", "Ztensor", "Zsync"):
        semh[n_] = octx.enter_context(nc.semaphore(n_))
    block = octx.enter_context(nc.Block())

    RNAMES = ["psA0", "psA1", "psB0", "psB1", "psC0", "psC1", "psT", "psU",
              "xbuf0", "xbuf1", "xnbuf0", "xnbuf1", "sqscr0", "sqscr1",
              "stat", "sumsq", "rabs", "habs", "sc11", "rxsx", "acol",
              "sclsb", "sxsw", "small", "xnfm0", "xnfm1", "xnfm2", "xnfm3", "qhat", "khat", "vsb",
              "wv", "wo", "prel0", "prel1", "qsqb0", "qsqb1", "sdq", "rqf",
              "rqbf0", "rqbf1",
              "mneg0", "mneg1", "mrow0", "mrow1", "recipz0", "recipz1",
              "ptsb0", "ptsb1", "prow0", "prow1", "attok", "afm",
              "attst0", "attst1",
              "wshard", "wdshard", "wint", "stg", "sclpack",
              "xqi", "xqb0", "xqb1",
              "wgt0", "wgt1", "wut0", "wut1", "silub", "upsb",
              "hbuf0", "hbuf1", "wdt", "hstg0", "hstg1", "hint",
              "outsb0", "outsb1", "tmpadd", "biases", "consts", "tmpdbg"]
    RNAMES += [f"hq{hb}_{g}" for hb in range(2) for g in range(NI)]
    RNAMES += [f"wqk{s}" for s in range(4)]
    DRAM = ["d_hhbm", "d_attno", "d_cc1i", "d_cc1o", "d_cc2i", "d_cc2o",
            "d_aggi", "d_aggo", "d_agui", "d_aguo", "d_agdi", "d_agdo",
            "d_scli", "d_sclo", "d_wdT"]

    def make_res():
        r = {n_: RES() for n_ in RNAMES}
        r.update({n_: RES(dram=True) for n_ in DRAM})
        return r

    ENGS = ("sync", "gpsimd", "tensor", "vector", "scalar")
    EOBJ = {"sync": nc.sync, "gpsimd": nc.gpsimd, "tensor": nc.tensor,
            "vector": nc.vector, "scalar": nc.scalar}
    state = {"c": {n_: 0 for n_ in semh}, "res": make_res(),
             "z": {e_: 0 for e_ in ENGS}}

    def run_phase(phase_fn, barrier=False):
        final_c = final_r = None
        base = dict(state["c"])
        for me in ENGS:
            counts = dict(base)
            res = {k: _res_copy(v) for k, v in state["res"].items()}
            def body(e, me=me, counts=counts, res=res):
                if barrier:
                    for s_ in ("T", "V", "S") + Walk.DMA_SEMS:
                        if base[s_] > 0:
                            e.wait_ge(semh[s_], base[s_])
                w = Walk(semh, counts, res, e, me, res["consts"], state["z"])
                phase_fn(w, res)
            getattr(block, me)(body)
            final_c, final_r = counts, res
        state["c"] = final_c
        state["res"] = final_r

    def _res_copy(b):
        n = RES(b.dram)
        n.w = b.w
        n.r = list(b.r)
        return n

    def ph_loads(w, R):
        loads = [(identbf, identbf_d), (identf, identf_d),
                 (onesg, onesg_d), (maskrow, maskrow_d),
                 (maskTt, maskTt_d), (qh_ind, qh_d), (rbc_ind, rbc_d),
                 (onesf, onesf_d), (clipx_s, clipx_d), (clipd_s, clipd_d)]
        for dst, src in loads:
            w.step("sync", lambda e, dst=dst, src=src:
                   e.dma_start(out=dst[:], in_=src),
                   writes=[R["consts"]], sem="LX", n=16)
        w.step("sync", lambda e: e.dma_start(
            out=bqc[:], in_=bq_d.rearrange("a (c p) -> p (a c)", p=128),
            allow_slow_non_contiguous=True),
            writes=[R["biases"]], sem="LX", n=16)
        w.step("sync", lambda e: e.dma_start(
            out=bkc[:], in_=bk_d.rearrange("a (c p) -> p (a c)", p=128),
            allow_slow_non_contiguous=True),
            writes=[R["biases"]], sem="LX", n=16)
        for dst, src in ((qncol, qn_d), (kncol, kn_d)):
            w.step("sync", lambda e, dst=dst, src=src: e.dma_start(
                out=dst[0:64, :], in_=src.rearrange("a d -> d a"),
                allow_slow_non_contiguous=True),
                writes=[R["biases"]], sem="LX", n=16)
            w.step("sync", lambda e, dst=dst, src=src: e.dma_start(
                out=dst[64:128, :], in_=src.rearrange("a d -> d a"),
                allow_slow_non_contiguous=True),
                writes=[R["biases"]], sem="LX", n=16)
        w.step("gpsimd", lambda e: e.dma_start(out=bvrow[:], in_=bv_d),
               writes=[R["biases"]], sem="LG", n=16)
        w.step("gpsimd", lambda e: e.dma_start(out=borow[:], in_=bo_d),
               writes=[R["biases"]], sem="LG", n=16)

        def _pre(e):
            w.zs(e, e.tensor_scalar(out=qn8[:], in0=qncol[:], scalar1=0.125,
                                    scalar2=None, op0=ALU.mult))
            e.tensor_scalar(out=bqq[:], in0=bqc[:], scalar1=qn8[:], scalar2=None,
                            op0=ALU.mult)
            e.tensor_scalar(out=bkk[:], in0=bkc[:], scalar1=kncol[:], scalar2=None,
                            op0=ALU.mult)
            e.memset(epsln[:], LN_EPS)
            return e.memset(epsrms[:], RMS_EPS)
        w.step("vector", _pre, reads=[R["biases"]], writes=[R["biases"]])

    run_phase(ph_loads)

    xctx = ExitStack()
    xnfm = sb(xctx, "xnfm", [128, 8, T], BF16)
    lctx = ExitStack()
    qhat = sb(lctx, "qhat", [128, 8, T], BF16)
    khat = sb(lctx, "khat", [128, 8, T], BF16)
    vsb = sb(lctx, "vsb", [128, 16, NH * 65], BF16)

    qctx = ExitStack()
    wshard = sb(qctx, "wshard", [128, 3, E], F32)
    wdshard = sb(qctx, "wdshard", [128, IF], F32)
    wi32 = sb(qctx, "wi32", [128, E], I32)
    wbf = sb(qctx, "wbf", [128, E], BF16)
    stg = sb(qctx, "stg", [128, 8, 128], BF16)
    sclpack = sb(qctx, "sclpack", [128, 8], F32)
    absq = sb(qctx, "absq", [128, 8], F32)
    rq127 = sb(qctx, "rq127", [128, 8], F32)

    def ph_quant(w, R):
        w.serial = SERIAL_Q
        w.step("sync", lambda e: e.dma_start(
            out=wshard[:], in_=wg_d.rearrange("j (a p) e -> p j (a e)", p=128)),
            writes=[R["wshard"]], sem="LB0", n=16)
        w.step("sync", lambda e: e.dma_start(out=wdshard[:], in_=wd_d),
               writes=[R["wdshard"]], sem="LB0", n=16)
        for grp in range(2):
            agdst = agg_i if grp == 0 else agu_i
            dram_i = R["d_aggi"] if grp == 0 else R["d_agui"]
            if grp == 1:
                w.step("sync", lambda e: e.dma_start(
                    out=wshard[:],
                    in_=wu_d.rearrange("j (a p) e -> p j (a e)", p=128)),
                    writes=[R["wshard"]], sem="LB0", n=16)
            for j in range(3):
                k = grp * 3 + j
                def _q(e, j=j, k=k):
                    w.zs(e, e.tensor_reduce(out=absq[:, k:k+1], in_=wshard[:, j, :],
                                            axis=AX.X, op=ALU.max,
                                            apply_absolute_value=True))
                    w.zs(e, e.reciprocal(out=rq127[:, k:k+1], in_=absq[:, k:k+1]))
                    e.tensor_scalar(out=sclpack[:, k:k+1], in0=absq[:, k:k+1],
                                    scalar1=1.0 / QMAX, scalar2=None, op0=ALU.mult)
                    e.tensor_scalar(out=wi32[:], in0=wshard[:, j, :],
                                    scalar1=rq127[:, k:k+1], scalar2=QMAX,
                                    op0=ALU.mult, op1=ALU.mult)
                    return e.tensor_copy(out=wbf[:], in_=wi32[:])
                w.step("vector", _q, reads=[R["wshard"]],
                       writes=[R["wint"], R["sclpack"]])
                def _t(e):
                    for ec in range(8):
                        i = e.transpose(out=psT[:, 128*ec:128*(ec+1)],
                                        in_=wbf[:, 128*ec:128*(ec+1)],
                                        identity=identbf[:])
                    return i
                w.step("tensor", _t, reads=[R["wint"]], writes=[R["psT"]])
                w.step("scalar", lambda e: e.copy(
                    out=stg[:], in_=psT[:].rearrange("p (ec t) -> p ec t", ec=8)),
                    reads=[R["psT"]], writes=[R["stg"]])
                w.step("sync", lambda e, agdst=agdst, j=j: e.dma_start(
                    out=agdst[j], in_=stg[:]),
                    reads=[R["stg"]], writes=[dram_i], sem="SA0", n=16)
        def _qd(e):
            w.zs(e, e.tensor_reduce(out=absq[:, 6:7], in_=wdshard[:], axis=AX.X,
                                    op=ALU.max, apply_absolute_value=True))
            w.zs(e, e.reciprocal(out=rq127[:, 6:7], in_=absq[:, 6:7]))
            return e.tensor_scalar(out=sclpack[:, 6:7], in0=absq[:, 6:7],
                                   scalar1=1.0 / QMAX, scalar2=None, op0=ALU.mult)
        w.step("vector", _qd, reads=[R["wdshard"]], writes=[R["sclpack"]])
        for b in range(NI):
            def _qb(e, b=b):
                w.zs(e, e.tensor_scalar(out=wi32[:, 0:128],
                                        in0=wdshard[:, 128*b:128*(b+1)],
                                        scalar1=rq127[:, 6:7], scalar2=QMAX,
                                        op0=ALU.mult, op1=ALU.mult))
                return e.tensor_scalar(out=wbf[:, 0:128], in0=wi32[:, 0:128],
                                       scalar1=sclpack[:, 6:7], scalar2=None,
                                       op0=ALU.mult)
            w.step("vector", _qb, reads=[R["wdshard"], R["sclpack"]],
                   writes=[R["wint"]])
            w.step("tensor", lambda e: e.transpose(
                out=psT[:, 0:128], in_=wbf[:, 0:128], identity=identbf[:]),
                reads=[R["wint"]], writes=[R["psT"]])
            w.step("scalar", lambda e: e.copy(out=stg[:, 0, :], in_=psT[:, 0:128]),
                   reads=[R["psT"]], writes=[R["stg"]])
            w.step("sync", lambda e, b=b: e.dma_start(out=agd_i[b], in_=stg[:, 0, :]),
                   reads=[R["stg"]], writes=[R["d_agdi"]], sem="SA0", n=16)
        w.step("sync", lambda e: e.dma_start(out=scl_i, in_=sclpack[:]),
               reads=[R["sclpack"]], writes=[R["d_scli"]], sem="SA0", n=16)
        for ins, outs, ri, ro in ((agg_i, agg_o, "d_aggi", "d_aggo"),
                                  (agu_i, agu_o, "d_agui", "d_aguo"),
                                  (agd_i, agd_o, "d_agdi", "d_agdo"),
                                  (scl_i, scl_o, "d_scli", "d_sclo")):
            w.step("gpsimd", lambda e, ins=ins, outs=outs: e.collective_compute(
                "AllGather", ALU.bypass, replica_groups=[list(range(8))],
                ins=[ins.opt()], outs=[outs.opt()]),
                reads=[R[ri]], writes=[R[ro]], sem="G")
        for ro in ("d_aggo", "d_aguo", "d_agdo", "d_sclo"):
            R[ro].w = ("G", w.c["G"])
        for r8 in range(8):
            w.step("gpsimd", lambda e, r8=r8: e.dma_start(
                out=wdT_hbm[:, :, 128*r8:128*(r8+1)], in_=agd_o[r8]),
                reads=[R["d_agdo"]], writes=[R["d_wdT"]], sem="LG", n=16)
        w.step("gpsimd", lambda e: e.dma_start(
            out=sclsb[:], in_=scl_o.rearrange("r p c -> p r c")),
            reads=[R["d_sclo"]], writes=[R["sclsb"]], sem="LD1", n=16)

    run_phase(ph_quant)

    sqscr = sb(qctx, "sqscr", [128, 2, E], F32)
    xnbuf = sb(qctx, "xnbuf", [128, 2, E], BF16)

    def ph_p0(w, R):
        w.serial = SERIAL_P0
        w.step("vector", lambda e: e.memset(
            vsb[:].rearrange("p tc (h d) -> p tc h d", d=65)[:, :, :, 64:65], 1.0),
            writes=[R["vsb"]])
        for tc in range(NT):
            par = tc % 2
            xb = R[f"xbuf{par}"]
            w.step("sync", lambda e, tc=tc, par=par: e.dma_start(
                out=xbuf[:, par, :], in_=x_d[128*tc:128*(tc+1), :]),
                writes=[xb], sem=f"LA{par}", n=16)
            def _red(e, tc=tc, par=par):
                e.reduce_sum(out=stat[:, tc:tc+1], in_=xbuf[:, par, :], axis=AX.X)
                e.reduce_max(out=stat[:, 16+tc:17+tc], in_=xbuf[:, par, :], axis=AX.X)
                return e.tensor_reduce(out=stat[:, 32+tc:33+tc], in_=xbuf[:, par, :],
                                       axis=AX.X, op=ALU.min)
            w.step("vector", _red, reads=[xb], writes=[R["stat"]])
            w.step("scalar", lambda e, tc=tc, par=par: e.activation(
                out=sqscr[:, par, :], in_=xbuf[:, par, :], func=AF.Square,
                accum_out=sumsq[:, tc:tc+1]),
                reads=[xb], writes=[R[f"sqscr{par}"], R["sumsq"]])
            def _st(e, tc=tc):
                w.zs(e, e.tensor_scalar(out=stat[:, 48+tc:49+tc], in0=stat[:, tc:tc+1],
                                        scalar1=1.0/E, scalar2=None, op0=ALU.mult))
                w.zs(e, e.tensor_scalar(out=vsmall[:, 0:1], in0=sumsq[:, tc:tc+1],
                                        scalar1=1.0/E, scalar2=None, op0=ALU.mult))
                w.zs(e, e.tensor_tensor(out=vsmall[:, 1:2], in0=stat[:, 48+tc:49+tc],
                                        in1=stat[:, 48+tc:49+tc], op=ALU.mult))
                return e.tensor_tensor(out=vsmall[:, 0:1], in0=vsmall[:, 0:1],
                                       in1=vsmall[:, 1:2], op=ALU.subtract)
            w.step("vector", _st, reads=[R["sumsq"], R["stat"]],
                   writes=[R["small"], R["stat"]])
            w.step("scalar", lambda e: e.activation(
                out=vsmall[:, 1:2], in_=vsmall[:, 0:1], func=AF.Sqrt,
                bias=epsln[:], scale=1.0),
                reads=[R["small"]], writes=[R["small"]])
            def _rr(e, tc=tc):
                w.zs(e, e.reciprocal(out=stat[:, 64+tc:65+tc], in_=vsmall[:, 1:2]))
                return e.tensor_scalar(out=stat[:, 80+tc:81+tc],
                                       in0=stat[:, 48+tc:49+tc],
                                       scalar1=stat[:, 64+tc:65+tc], scalar2=-1.0,
                                       op0=ALU.mult, op1=ALU.mult)
            w.step("vector", _rr, reads=[R["small"]], writes=[R["stat"]])
            w.step("scalar", lambda e, tc=tc, par=par: e.activation(
                out=xnbuf[:, par, :], in_=xbuf[:, par, :], func=AF.Identity,
                scale=stat[:, 64+tc:65+tc], bias=stat[:, 80+tc:81+tc]),
                reads=[xb, R["stat"]], writes=[R[f"xnbuf{par}"]])
            def _ra(e, tc=tc):
                w.zs(e, e.tensor_tensor(out=vsmall[:, 0:1], in0=stat[:, 16+tc:17+tc],
                                        in1=stat[:, 48+tc:49+tc], op=ALU.subtract))
                w.zs(e, e.tensor_tensor(out=rabs[:, tc:tc+1], in0=stat[:, 48+tc:49+tc],
                                        in1=stat[:, 32+tc:33+tc], op=ALU.subtract))
                w.zs(e, e.tensor_tensor(out=rabs[:, tc:tc+1], in0=rabs[:, tc:tc+1],
                                        in1=vsmall[:, 0:1], op=ALU.max))
                return e.tensor_scalar(out=rabs[:, tc:tc+1], in0=rabs[:, tc:tc+1],
                                       scalar1=stat[:, 64+tc:65+tc], scalar2=None,
                                       op0=ALU.mult)
            w.step("vector", _ra, reads=[R["stat"]], writes=[R["rabs"], R["small"]])
            def _tr(e, par=par):
                for s8 in range(8):
                    i = e.transpose(out=psT[:, 128*s8:128*(s8+1)],
                                    in_=xnbuf[:, par, 128*s8:128*(s8+1)],
                                    identity=identbf[:])
                return i
            w.step("tensor", _tr, reads=[R[f"xnbuf{par}"]], writes=[R["psT"]])
            w.step("scalar", lambda e, tc=tc: e.copy(
                out=xnfm[:, :, 128*tc:128*(tc+1)],
                in_=psT[:].rearrange("p (ec t) -> p ec t", ec=8)),
                reads=[R["psT"]], writes=[R[f"xnfm{tc//4}"]])
        def _gr(e):
            e.memset(gtr[:], 0.0)
            return e.tensor_reduce(out=gtr[:, 0:1], in_=rabs[:], axis=AX.X,
                                   op=ALU.max)
        w.step("vector", _gr, reads=[R["rabs"]], writes=[R["small"]])
        w.step("tensor", lambda e: e.transpose(out=psC1[0:32, 0:128],
                                               in_=gtr[:, 0:32], identity=identf[:]),
               reads=[R["small"]], writes=[R["psC1"]])
        w.step("vector", lambda e: e.reduce_max(out=sc11[:, 0:1],
                                                in_=psC1[0:1, 0:128], axis=AX.X),
               reads=[R["psC1"]], writes=[R["sc11"]])
        w.step("gpsimd", lambda e: e.dma_start(out=cc1i, in_=sc11[:, 0:1]),
               reads=[R["sc11"]], writes=[R["d_cc1i"]], sem="LC1", n=16)
        w.step("gpsimd", lambda e: e.collective_compute(
            "AllReduce", ALU.max, replica_groups=[list(range(8))],
            ins=[cc1i.opt()], outs=[cc1o.opt()]),
            reads=[R["d_cc1i"]], writes=[R["d_cc1o"]], sem="G")
        w.step("gpsimd", lambda e: e.dma_start(out=sc11[:, 0:1], in_=cc1o),
               reads=[R["d_cc1o"]], writes=[R["sc11"]], sem="LD0", n=16)
        w.step("scalar", lambda e: e.activation(out=sc11[:, 1:2], in_=clipx_s[:],
                                                func=AF.Sigmoid),
               reads=[R["sc11"]], writes=[R["sc11"]])
        def _sx(e):
            w.zs(e, e.tensor_tensor(out=sc11[:, 2:3], in0=sc11[:, 1:2],
                                    in1=sc11[:, 0:1], op=ALU.mult))
            w.zs(e, e.reciprocal(out=sc11[:, 3:4], in_=sc11[:, 2:3]))
            w.zs(e, e.tensor_scalar(out=sc11[:, 3:4], in0=sc11[:, 3:4], scalar1=QMAX,
                                    scalar2=None, op0=ALU.mult))
            return e.tensor_scalar(out=sc11[:, 4:5], in0=sc11[:, 2:3],
                                   scalar1=1.0/QMAX, scalar2=None, op0=ALU.mult)
        w.step("vector", _sx, reads=[R["sc11"]], writes=[R["sc11"]])
        def _bc(e):
            e.matmul(out=psC0[:, 0:1], lhsT=onesf[:], rhs=sc11[:, 3:4],
                     start=True, stop=False)
            return e.matmul(out=psC0[:, 1:2], lhsT=onesf[:], rhs=sc11[:, 4:5],
                            start=False, stop=True)
        w.step("tensor", _bc, reads=[R["sc11"]], writes=[R["psC0"]])
        w.step("vector", lambda e: e.tensor_copy(out=rxsx[:, 0:2], in_=psC0[:, 0:2]),
               reads=[R["psC0"]], writes=[R["rxsx"]])
        def _ssw(e):
            e.tensor_scalar(out=sxswg[:].rearrange("p (r c) -> p r c", c=3),
                            in0=sclsb[:, :, 0:3],
                            scalar1=rxsx[:, 1:2], scalar2=None, op0=ALU.mult)
            return e.tensor_scalar(out=sxswu[:].rearrange("p (r c) -> p r c", c=3),
                                   in0=sclsb[:, :, 3:6],
                                   scalar1=rxsx[:, 1:2], scalar2=None, op0=ALU.mult)
        w.step("vector", _ssw, reads=[R["sclsb"], R["rxsx"]], writes=[R["sxsw"]])

    run_phase(ph_p0)
    qctx.close()

    p1ctx = ExitStack()
    wpbuf = sb(p1ctx, "wpbuf", [128, 8, E], BF16)
    prel = sb(p1ctx, "prel", [128, 8, 512], BF16)
    qsqb = sb(p1ctx, "qsqb", [128, 2, 512], BF16)
    sdq = sb(p1ctx, "sdq", [16, 512], F32)
    rqbf = sb(p1ctx, "rqbf", [16, 512], BF16)
    wosb = sb(p1ctx, "wosb", [128, 8, E], BF16)

    def ph_p1(w, R):
        w.serial = SERIAL_P1
        pend = []

        def flush_pend():
            while pend:
                pend.pop(0)()

        def unit(ti, pj):
            t0 = 512 * ti
            flush_pend()
            bcols = bqc if pj == 0 else bkc
            ncol = qn8 if pj == 0 else kncol
            bnn = bqq if pj == 0 else bkk
            pssum, sumR = (psC0, R["psC0"]) if pj == 0 else (psC1, R["psC1"])
            dst, dstR = (qhat, R["qhat"]) if pj == 0 else (khat, R["khat"])
            for ec in range(8):
                pa = ec % 2
                def _mm(e, ec=ec, pa=pa):
                    for ei in range(8):
                        i = e.matmul(out=psA[pa][:],
                                     lhsT=wpbuf[:, ei, 128*ec:128*(ec+1)],
                                     rhs=xnfm[:, ei, t0:t0+512],
                                     start=(ei == 0), stop=(ei == 7))
                    return i
                w.step("tensor", _mm, reads=[R["wv"], R[f"xnfm{ti}"]],
                       writes=[R[f"psA{pa}"]])
                def _ep(e, ec=ec, pa=pa):
                    e.activation(out=qsqb[:, pa, :], in_=psA[pa][:], func=AF.Square,
                                 bias=bcols[:, ec:ec+1], scale=1.0)
                    return e.activation(out=prel[:, ec, :], in_=psA[pa][:],
                                        func=AF.Identity, scale=ncol[:],
                                        bias=bnn[:, ec:ec+1])
                w.step("scalar", _ep, reads=[R[f"psA{pa}"], R["biases"]],
                       writes=[R[f"qsqb{pa}"], R["prel0"]])
                if ec >= 1:
                    hs(ti, pj, ec - 1, pssum, sumR)
            hs(ti, pj, 7, pssum, sumR)
            w.step("scalar", lambda e: e.activation(
                out=sdq[:], in_=pssum[0:16, :], func=AF.Sqrt,
                scale=1.0 / D, bias=epsrms[0:16, :]),
                reads=[sumR], writes=[R["sdq"]])
            def _rc(e):
                w.zs(e, e.reciprocal(out=sdq[:], in_=sdq[:]))
                return e.tensor_copy(out=rqbf[:], in_=sdq[:])
            w.step("vector", _rc, reads=[R["sdq"]], writes=[R["sdq"], R["rqbf0"]])
            for ec in range(8):
                pend.append(lambda ec=ec, dst=dst, dstR=dstR, t0=t0:
                            fa(ec, dst, dstR, t0))

        def hs(ti, pj, ec, pssum, sumR):
            pa = ec % 2
            w.step("tensor", lambda e: e.matmul(
                out=pssum[0:16, :], lhsT=qh_ind[:, 16*ec:16*(ec+1)],
                rhs=qsqb[:, pa, :], start=(ec == 0), stop=(ec == 7)),
                reads=[R[f"qsqb{pa}"]], writes=[sumR])

        def fa(ec, dst, dstR, t0):
            pb = ec % 2
            w.step("tensor", lambda e: e.matmul(
                out=psB[pb][:], lhsT=rbc_ind[:, 128*ec:128*(ec+1)],
                rhs=rqbf[:], start=True, stop=True),
                reads=[R["rqbf0"]], writes=[R[f"psB{pb}"]])
            w.step("vector", lambda e: e.scalar_tensor_tensor(
                out=dst[:, ec, t0:t0+512], in0=prel[:, ec, :],
                scalar=1.0, in1=psB[pb][:], op0=ALU.mult, op1=ALU.mult),
                reads=[R["prel0"], R[f"psB{pb}"]], writes=[dstR])

        for pj in range(2):
            wsrc = wq_d if pj == 0 else wk_d
            w.step("gpsimd", lambda e, wsrc=wsrc: e.dma_start(
                out=wpbuf[:], in_=wsrc.rearrange("(ei p) eo -> p ei eo", p=128)),
                writes=[R["wv"]], sem="LG", n=16)
            for ti in range(NTI):
                unit(ti, pj)
        flush_pend()
        w.step("gpsimd", lambda e: e.dma_start(
            out=wpbuf[:], in_=wv_d.rearrange("(ei p) eo -> p ei eo", p=128)),
            writes=[R["wv"]], sem="LG", n=16)
        for tc in range(NT):
            for half in range(2):
                pa = (2 * tc + half) % 2
                def _vm(e, tc=tc, half=half, pa=pa):
                    for ei in range(8):
                        e.matmul(out=psA[pa][:], lhsT=xnfm[:, ei, 128*tc:128*(tc+1)],
                                 rhs=wpbuf[:, ei, 512*half:512*(half+1)],
                                 start=(ei == 0), stop=False)
                    return e.matmul(out=psA[pa][:], lhsT=onesg[0:1, :],
                                    rhs=bvrow[:, 512*half:512*(half+1)],
                                    start=False, stop=True)
                w.step("tensor", _vm, reads=[R[f"xnfm{tc//4}"], R["wv"], R["biases"]],
                       writes=[R[f"psA{pa}"]])
                w.step("scalar", lambda e, tc=tc, half=half, pa=pa: e.copy(
                    out=vsb[:, tc, :].rearrange("p (h d) -> p h d", d=65)
                        [:, 8*half:8*(half+1), 0:64],
                    in_=psA[pa][:].rearrange("p (h d) -> p h d", d=64)),
                    reads=[R[f"psA{pa}"]], writes=[R["vsb"]])

    run_phase(ph_p1, barrier=True)
    if debug:
        def ph_dbg1b(w, R):
            w.step("sync", lambda e: e.dma_start(out=dbg_wp, in_=wpbuf[:]),
                   reads=[R["wv"]], sem="ST", n=16)
            w.step("sync", lambda e: e.dma_start(out=dbg_qh, in_=qhat[:]),
                   reads=[R["qhat"]], sem="ST", n=16)
            w.step("sync", lambda e: e.dma_start(out=dbg_kh, in_=khat[:]),
                   reads=[R["khat"]], sem="ST", n=16)
            w.step("sync", lambda e: e.dma_start(out=dbg_v, in_=vsb[:]),
                   reads=[R["vsb"]], sem="ST", n=16)
        run_phase(ph_dbg1b)


    def ph_p4(w, R):
        w.serial = SERIAL_P4
        w.step("gpsimd", lambda e: e.dma_start(
            out=wosb[:], in_=wo_d.rearrange("(ei p) eo -> p ei eo", p=128)),
            writes=[R["wo"]], sem="LC0", n=16)

        def geom(qi):
            nw = min(qi + 1, 3)
            return nw, qi - nw + 1, 128 * nw, 384 - 128 * nw

        def stA(qi, h):
            k = qi * NH + h
            nw, wc0, Wd, moff = geom(qi)
            ch, rb, pa = h // 2, 64 * (h % 2), k % 2
            def _row(e):
                e.matmul(out=psA[pa][:, 0:Wd],
                         lhsT=qhat[rb:rb+64, ch, 128*qi:128*(qi+1)],
                         rhs=khat[rb:rb+64, ch, 128*wc0:128*wc0+Wd],
                         start=True, stop=False)
                return e.matmul(out=psA[pa][:, 0:Wd], lhsT=identbf[:],
                                rhs=maskrow[:, moff:384], start=False, stop=True)
            w.step("tensor", _row, reads=[R["qhat"], R["khat"]],
                   writes=[R[f"psA{pa}"]])

        def stM(qi, h):
            k = qi * NH + h
            nw, wc0, Wd, moff = geom(qi)
            pa = k % 2
            w.step("vector", lambda e: e.reduce_max(
                out=mnegf[:, pa:pa+1], in_=psA[pa][:, 0:Wd], axis=AX.X, negate=True),
                reads=[R[f"psA{pa}"]], writes=[R[f"mneg{pa}"]])

        def stE(qi, h):
            k = qi * NH + h
            nw, wc0, Wd, moff = geom(qi)
            pa = k % 2
            w.step("scalar", lambda e: e.activation(
                out=prow[:, pa, 0:Wd], in_=psA[pa][:, 0:Wd], func=AF.Exp,
                bias=mnegf[:, pa:pa+1], scale=1.0),
                reads=[R[f"psA{pa}"], R[f"mneg{pa}"]], writes=[R[f"prow{pa}"]])

        def stT(qi, h):
            k = qi * NH + h
            nw, wc0, Wd, moff = geom(qi)
            pa = k % 2
            def _tt(e):
                for c in range(nw):
                    i = e.transpose(out=psUT[pa][:, 128*c:128*(c+1)],
                                    in_=prow[:, pa, 128*c:128*(c+1)],
                                    identity=identbf[:])
                return i
            w.step("tensor", _tt, reads=[R[f"prow{pa}"]],
                   writes=[R["psU"] if pa == 0 else R["psT"]])

        def stPC(qi, h):
            k = qi * NH + h
            nw, wc0, Wd, moff = geom(qi)
            pa = k % 2
            w.step("vector", lambda e: e.tensor_copy(
                out=ptsb[:, pa, 0:Wd], in_=psUT[pa][:, 0:Wd]),
                reads=[R["psU"] if pa == 0 else R["psT"]], writes=[R[f"ptsb{pa}"]])

        def stPV(qi, h):
            k = qi * NH + h
            nw, wc0, Wd, moff = geom(qi)
            pa = k % 2
            def _pv(e):
                for c in range(nw):
                    i = e.matmul(out=psC[pa][:, 0:65],
                                 lhsT=ptsb[:, pa, 128*c:128*(c+1)],
                                 rhs=vsb[:, wc0+c, 65*h:65*(h+1)],
                                 start=(c == 0), stop=(c == nw - 1))
                return i
            w.step("tensor", _pv, reads=[R[f"ptsb{pa}"], R["vsb"]],
                   writes=[R[f"psC{pa}"]])

        def stRZ(qi, h):
            k = qi * NH + h
            pa = k % 2
            w.step("vector", lambda e: e.reciprocal(
                out=recipz[:, pa:pa+1], in_=psC[pa][:, 64:65]),
                reads=[R[f"psC{pa}"]], writes=[R[f"recipz{pa}"]])

        def stEV(qi, h):
            k = qi * NH + h
            pa = k % 2
            w.step("scalar", lambda e: e.activation(
                out=attok[:, 64*h:64*(h+1)], in_=psC[pa][:, 0:64],
                func=AF.Copy, scale=recipz[:, pa:pa+1]),
                reads=[R[f"psC{pa}"], R[f"recipz{pa}"]], writes=[R["attok"]])

        def tail(qi):
            def _tr(e):
                for s8 in range(8):
                    i = e.transpose(out=psT[:, 128*s8:128*(s8+1)],
                                    in_=attok[:, 128*s8:128*(s8+1)],
                                    identity=identbf[:])
                return i
            w.step("tensor", _tr, reads=[R["attok"]], writes=[R["psT"]])
            w.step("scalar", lambda e: e.copy(
                out=afm[:], in_=psT[:].rearrange("p (ec t) -> p ec t", ec=8)),
                reads=[R["psT"]], writes=[R["afm"]])
            def _wo(e):
                for half in range(2):
                    for s8 in range(8):
                        e.matmul(out=psB[half][:], lhsT=afm[:, s8, :],
                                 rhs=wosb[:, s8, 512*half:512*(half+1)],
                                 start=(s8 == 0), stop=False)
                    i = e.matmul(out=psB[half][:], lhsT=onesg[0:1, :],
                                 rhs=borow[:, 512*half:512*(half+1)],
                                 start=False, stop=True)
                return i
            w.step("tensor", _wo, reads=[R["afm"], R["wo"], R["biases"]],
                   writes=[R["psB0"], R["psB1"]])
            pq = qi % 2
            def _ev(e):
                e.activation(out=attst[:, pq, 0:512], in_=psB[0][:], func=AF.Copy)
                return e.activation(out=attst[:, pq, 512:1024], in_=psB[1][:],
                                    func=AF.Copy)
            w.step("scalar", _ev, reads=[R["psB0"], R["psB1"]],
                   writes=[R[f"attst{pq}"]])
            w.step("sync", lambda e: e.dma_start(
                out=attno_hbm[128*qi:128*(qi+1), :], in_=attst[:, pq, :]),
                reads=[R[f"attst{pq}"]], writes=[R["d_attno"]], sem=f"SA{pq}",
                n=16)

        KK = NT * NH
        def hk(k):
            return k // NH, k % NH
        for k in range(KK + 3):
            if k < KK:
                qi, h = hk(k)
                stA(qi, h)
                stM(qi, h)
                stE(qi, h)
            if 1 <= k < KK + 1:
                qi, h = hk(k - 1)
                stT(qi, h)
                stPC(qi, h)
            if 2 <= k < KK + 2:
                qi, h = hk(k - 2)
                stPV(qi, h)
                stRZ(qi, h)
            if 3 <= k < KK + 3:
                qi, h = hk(k - 3)
                stEV(qi, h)
                if h == NH - 1:
                    tail(qi)

    run_phase(ph_p4)
    if debug:
        def ph_dbg4(w, R):
            w.step("sync", lambda e: e.dma_start(out=dbg_at, in_=attno_hbm),
                   reads=[R["d_attno"]], sem="ST", n=16)
        run_phase(ph_dbg4)
    p1ctx.close()
    lctx.close()

    pwctx = ExitStack()
    wdt = sb(pwctx, "wdt", [128, NI, E], BF16)
    p2ctx = ExitStack()
    xqi = sb(p2ctx, "xqi", [128, E], I32)
    xqb = sb(p2ctx, "xqb", [128, 2, E], BF16)

    def ph_p2(w, R):
        w.serial = SERIAL_P2
        def _ab(e):
            e.tensor_scalar(out=acol[:], in0=stat[:, 64:80], scalar1=rxsx[:, 0:1],
                            scalar2=None, op0=ALU.mult)
            return e.tensor_scalar(out=b2col[:], in0=stat[:, 80:96],
                                   scalar1=rxsx[:, 0:1], scalar2=None, op0=ALU.mult)
        w.step("vector", _ab, reads=[R["stat"], R["rxsx"]], writes=[R["acol"]])
        for tc in range(NT):
            par = tc % 2
            xb = R[f"xbuf{par}"]
            w.step("sync", lambda e, tc=tc, par=par: e.dma_start(
                out=xbuf[:, par, :], in_=x_d[128*tc:128*(tc+1), :]),
                writes=[xb], sem=f"LA{par}", n=16)
            def _q(e, tc=tc, par=par):
                w.zs(e, e.tensor_scalar(out=xqi[:], in0=xbuf[:, par, :],
                                        scalar1=acol[:, tc:tc+1],
                                        scalar2=b2col[:, tc:tc+1],
                                        op0=ALU.mult, op1=ALU.add))
                return e.tensor_scalar(out=xqb[:, par, :], in0=xqi[:],
                                       scalar1=-128.0, scalar2=127.0,
                                       op0=ALU.max, op1=ALU.min)
            w.step("vector", _q, reads=[xb, R["acol"]],
                   writes=[R["xqi"], R[f"xqb{par}"]])
            def _tr(e, par=par):
                for s8 in range(8):
                    i = e.transpose(out=psT[:, 128*s8:128*(s8+1)],
                                    in_=xqb[:, par, 128*s8:128*(s8+1)],
                                    identity=identbf[:])
                return i
            w.step("tensor", _tr, reads=[R[f"xqb{par}"]], writes=[R["psT"]])
            w.step("scalar", lambda e, tc=tc: e.copy(
                out=xnfm[:, :, 128*tc:128*(tc+1)],
                in_=psT[:].rearrange("p (ec t) -> p ec t", ec=8)),
                reads=[R["psT"]], writes=[R[f"xnfm{tc//4}"]])

    run_phase(ph_p2, barrier=True)

    p3ctx = ExitStack()
    wgt = sb(p3ctx, "wgt", [128, 2, 8, 128], BF16)
    wut = sb(p3ctx, "wut", [128, 2, 8, 128], BF16)
    silub = sb(p3ctx, "silub", [128, 512], F32)
    dbgslb = sb(p3ctx, "dbgslb", [128, 4, 512], F32) if debug else None
    upsb = sb(p3ctx, "upsb", [128, 512], F32)
    hbuf = sb(p3ctx, "hbuf", [128, 2, 512], F32)

    def ph_p3(w, R):
        w.serial = SERIAL_P3
        for c4 in range(4):
            w.step("gpsimd", lambda e, c4=c4: e.dma_start(
                out=wdt[:, :, 256*c4:256*(c4+1)],
                in_=wdT_hbm[:, :, 256*c4:256*(c4+1)].rearrange("g p e -> p g e")),
                reads=[R["d_wdT"]], writes=[R["wdt"]], sem="LG", n=16)
        for g in range(NI):
            r8, j = g // 3, g % 3
            gp = g % 2
            w.step("gpsimd", lambda e, r8=r8, j=j, gp=gp: e.dma_start(
                out=wgt[:, gp, :, :], in_=agg_o[r8, j]),
                reads=[R["d_aggo"]], writes=[R[f"wgt{gp}"]], sem=f"LB{gp}", n=16)
            w.step("gpsimd", lambda e, r8=r8, j=j, gp=gp: e.dma_start(
                out=wut[:, gp, :, :], in_=agu_o[r8, j]),
                reads=[R["d_aguo"]], writes=[R[f"wut{gp}"]], sem=f"LB{gp}", n=16)
            for ti in range(NTI):
                t0 = 512 * ti
                pg = ti % 2
                def _mm(e, gp=gp, pg=pg, t0=t0):
                    for ei in range(8):
                        e.matmul(out=psA[pg][:], lhsT=wgt[:, gp, ei, :],
                                 rhs=xnfm[:, ei, t0:t0+512],
                                 start=(ei == 0), stop=(ei == 7))
                    for ei in range(8):
                        i = e.matmul(out=psB[pg][:], lhsT=wut[:, gp, ei, :],
                                     rhs=xnfm[:, ei, t0:t0+512],
                                     start=(ei == 0), stop=(ei == 7))
                    return i
                w.step("tensor", _mm,
                       reads=[R[f"wgt{gp}"], R[f"wut{gp}"], R[f"xnfm{ti}"]],
                       writes=[R[f"psA{pg}"], R[f"psB{pg}"]])
                def _ep(e, g=g, pg=pg):
                    e.activation(out=silub[:], in_=psA[pg][:], func=AF.Silu,
                                 scale=sxswg[:, g:g+1])
                    return e.activation(out=upsb[:], in_=psB[pg][:], func=AF.Copy,
                                        scale=sxswu[:, g:g+1])
                w.step("scalar", _ep,
                       reads=[R[f"psA{pg}"], R[f"psB{pg}"], R["sxsw"]],
                       writes=[R["silub"], R["upsb"]])
                def _h(e, g=g, ti=ti, pg=pg):
                    w.zs(e, e.tensor_tensor(out=hbuf[:, pg, :], in0=silub[:],
                                            in1=upsb[:], op=ALU.mult))
                    return e.tensor_reduce(out=habs[:, 4*g+ti:4*g+ti+1],
                                           in_=hbuf[:, pg, :], axis=AX.X, op=ALU.max,
                                           apply_absolute_value=True)
                if debug and g == 0 and ti == 0:
                    def _dsl(e):
                        e.tensor_copy(out=dbgslb[:, 0, :], in_=silub[:])
                        e.tensor_copy(out=dbgslb[:, 1, :], in_=upsb[:])
                        e.tensor_copy(out=dbgslb[:, 2, :], in_=psA[0][:])
                        return e.tensor_copy(out=dbgslb[:, 3, :], in_=psB[0][:])
                    w.step("vector", _dsl,
                           reads=[R["silub"], R["upsb"], R["psA0"], R["psB0"]],
                           writes=[R["tmpdbg"]])
                    w.step("sync", lambda e: e.dma_start(out=dbg_sl, in_=dbgslb[:]),
                           reads=[R["tmpdbg"]], sem="ST", n=16)
                    w.step("sync", lambda e: e.dma_start(out=dbg_wg0, in_=wgt[:, 0, :, :]),
                           reads=[R["wgt0"]], sem="ST", n=16)
                    w.step("sync", lambda e: e.dma_start(out=dbg_ag0, in_=agg_o[0, 0]),
                           reads=[R["d_aggo"]], sem="ST", n=16)
                w.step("vector", _h, reads=[R["silub"], R["upsb"]],
                       writes=[R[f"hbuf{pg}"], R["habs"]])
                w.step("sync", lambda e, g=g, t0=t0, pg=pg: e.dma_start(
                    out=h_hbm[g, :, t0:t0+512], in_=hbuf[:, pg, :]),
                    reads=[R[f"hbuf{pg}"]], writes=[R["d_hhbm"]], sem=f"SA{pg}",
                    n=16)
        def _hr(e):
            e.memset(gtr[:], 0.0)
            return e.tensor_reduce(out=gtr[:, 0:1], in_=habs[:, 0:88], axis=AX.X,
                                   op=ALU.max)
        w.step("vector", _hr, reads=[R["habs"]], writes=[R["small"]])
        w.step("tensor", lambda e: e.transpose(out=psC1[0:32, 0:128],
                                               in_=gtr[:, 0:32], identity=identf[:]),
               reads=[R["small"]], writes=[R["psC1"]])
        w.step("vector", lambda e: e.reduce_max(out=sc11[:, 5:6],
                                                in_=psC1[0:1, 0:128], axis=AX.X),
               reads=[R["psC1"]], writes=[R["sc11"]])
        w.step("gpsimd", lambda e: e.dma_start(out=cc2i, in_=sc11[:, 5:6]),
               reads=[R["sc11"]], writes=[R["d_cc2i"]], sem="LC1", n=16)
        w.step("gpsimd", lambda e: e.collective_compute(
            "AllReduce", ALU.max, replica_groups=[list(range(8))],
            ins=[cc2i.opt()], outs=[cc2o.opt()]),
            reads=[R["d_cc2i"]], writes=[R["d_cc2o"]], sem="G")
        w.step("gpsimd", lambda e: e.dma_start(out=sc11[:, 5:6], in_=cc2o),
               reads=[R["d_cc2o"]], writes=[R["sc11"]], sem="LD0", n=16)
        w.step("scalar", lambda e: e.activation(out=sc11[:, 6:7], in_=clipd_s[:],
                                                func=AF.Sigmoid),
               reads=[R["sc11"]], writes=[R["sc11"]])
        def _sh(e):
            w.zs(e, e.tensor_tensor(out=sc11[:, 7:8], in0=sc11[:, 6:7],
                                    in1=sc11[:, 5:6], op=ALU.mult))
            w.zs(e, e.reciprocal(out=sc11[:, 5:6], in_=sc11[:, 7:8]))
            w.zs(e, e.tensor_scalar(out=sc11[:, 5:6], in0=sc11[:, 5:6], scalar1=QMAX,
                                    scalar2=None, op0=ALU.mult))
            return e.tensor_scalar(out=sc11[:, 6:7], in0=sc11[:, 7:8],
                                   scalar1=1.0/QMAX, scalar2=None, op0=ALU.mult)
        w.step("vector", _sh, reads=[R["sc11"]], writes=[R["sc11"]])
        def _bc(e):
            e.matmul(out=psC0[:, 2:3], lhsT=onesf[:], rhs=sc11[:, 5:6],
                     start=True, stop=False)
            return e.matmul(out=psC0[:, 3:4], lhsT=onesf[:], rhs=sc11[:, 6:7],
                            start=False, stop=True)
        w.step("tensor", _bc, reads=[R["sc11"]], writes=[R["psC0"]])
        w.step("vector", lambda e: e.tensor_copy(out=rxsx[:, 2:4], in_=psC0[:, 2:4]),
               reads=[R["psC0"]], writes=[R["rxsx"]])

    run_phase(ph_p3)
    if debug:
        def ph_dbg3(w, R):
            w.step("sync", lambda e: e.dma_start(out=dbg_h, in_=h_hbm),
                   reads=[R["d_hhbm"]], sem="ST", n=16)
            w.step("sync", lambda e: e.dma_start(out=dbg_sc, in_=sc11[:]),
                   reads=[R["sc11"]], sem="ST", n=16)
            w.step("sync", lambda e: e.dma_start(out=dbg_rx, in_=rxsx[:]),
                   reads=[R["rxsx"]], sem="ST", n=16)
            w.step("sync", lambda e: e.dma_start(out=dbg_ssw, in_=sxswg[:]),
                   reads=[R["sxsw"]], sem="ST", n=16)
            w.step("sync", lambda e: e.dma_start(out=dbg_agg, in_=agg_o),
                   reads=[R["d_aggo"]], sem="ST", n=16)
        run_phase(ph_dbg3)
    p3ctx.close()
    p2ctx.close()

    p5ctx = ExitStack()
    hqsb = sb(p5ctx, "hqsb", [128, 2, NI, 512], BF16)
    hstg = sb(p5ctx, "hstg", [128, 2, 512], F32)
    hi32 = sb(p5ctx, "hi32", [128, 512], I32)
    hclip = sb(p5ctx, "hclip", [128, 512], BF16)
    outsb = sb(p5ctx, "outsb", [128, 2, E], F32)
    tmpadd = sb(p5ctx, "tmpadd", [128, 512], F32)

    def ph_p5(w, R):
        def qblock(ti):
            t0 = 512 * ti
            hb = ti % 2
            w.serial = SERIAL_P5 in (1, 2)
            for g in range(NI):
                gp = g % 2
                w.step("sync", lambda e, g=g, t0=t0, gp=gp: e.dma_start(
                    out=hstg[:, gp, :], in_=h_hbm[g, :, t0:t0+512]),
                    reads=[R["d_hhbm"]], writes=[R[f"hstg{gp}"]], sem=f"LA{gp}",
                    n=16)
                def _q(e, gp=gp):
                    w.zs(e, e.tensor_scalar(out=hi32[:], in0=hstg[:, gp, :],
                                            scalar1=rxsx[:, 2:3], scalar2=None,
                                            op0=ALU.mult))
                    return e.tensor_scalar(out=hclip[:], in0=hi32[:],
                                           scalar1=-128.0, scalar2=127.0,
                                           op0=ALU.max, op1=ALU.min)
                w.step("vector", _q, reads=[R[f"hstg{gp}"], R["rxsx"]],
                       writes=[R["hint"]])
                w.step("scalar", lambda e, g=g, hb=hb: e.activation(
                    out=hqsb[:, hb, g, :], in_=hclip[:], func=AF.Copy,
                    scale=rxsx[:, 3:4]),
                    reads=[R["hint"], R["rxsx"]], writes=[R[f"hq{hb}_{g}"]])

        def dblock(ti):
            hb = ti % 2
            w.serial = SERIAL_P5 in (1, 3)
            for tq in range(4):
                qi = 4 * ti + tq
                pq = qi % 2
                w.step("gpsimd", lambda e, qi=qi, pq=pq: e.dma_start(
                    out=xbuf[:, pq, :], in_=x_d[128*qi:128*(qi+1), :]),
                    writes=[R[f"xbuf{pq}"]], sem=f"LB{pq}", n=16)
                w.step("gpsimd", lambda e, qi=qi, pq=pq: e.dma_start(
                    out=attst[:, pq, :], in_=attno_hbm[128*qi:128*(qi+1), :]),
                    reads=[R["d_attno"]], writes=[R[f"attst{pq}"]], sem=f"LC{pq}",
                    n=16)
                def _dn(e, tq=tq, pq=pq, hb=hb):
                    for half, bank in ((0, psA[pq]), (1, psB[pq])):
                        for g in range(NI):
                            i = e.matmul(out=bank[:],
                                         lhsT=hqsb[:, hb, g, 128*tq:128*(tq+1)],
                                         rhs=wdt[:, g, 512*half:512*(half+1)],
                                         start=(g == 0), stop=(g == NI - 1))
                    return i
                w.step("tensor", _dn,
                       reads=[R["wdt"]] + [R[f"hq{hb}_{g}"] for g in range(NI)],
                       writes=[R[f"psA{pq}"], R[f"psB{pq}"]])
                def _fin(e, pq=pq):
                    for half, bank in ((0, psA[pq]), (1, psB[pq])):
                        w.zs(e, e.tensor_tensor(out=tmpadd[:], in0=bank[:],
                                                in1=xbuf[:, pq, 512*half:512*(half+1)],
                                                op=ALU.add))
                        i = e.tensor_tensor(out=outsb[:, pq, 512*half:512*(half+1)],
                                            in0=tmpadd[:],
                                            in1=attst[:, pq, 512*half:512*(half+1)],
                                            op=ALU.add)
                    return i
                w.step("vector", _fin,
                       reads=[R[f"psA{pq}"], R[f"psB{pq}"], R[f"xbuf{pq}"],
                              R[f"attst{pq}"]],
                       writes=[R[f"outsb{pq}"], R["tmpadd"]])
                w.step("gpsimd", lambda e, qi=qi, pq=pq: e.dma_start(
                    out=out_d[128*qi:128*(qi+1), :], in_=outsb[:, pq, :]),
                    reads=[R[f"outsb{pq}"]], writes=[], sem=f"SA{pq}", n=16)

        for ti in range(NTI):
            qblock(ti)
            if ti > 0:
                dblock(ti - 1)
        dblock(NTI - 1)
        if debug:
            for g in range(NI):
                w.step("sync", lambda e, g=g: e.dma_start(
                    out=dbg_hq[:, g, :] if True else None, in_=hqsb[:, 1, g, :]),
                    reads=[R["hq1_" + str(g)]], sem="ST", n=16)
        if w.me == "sync":
            for s_ in ("ST", "SA0", "SA1"):
                if w.c[s_] > 0:
                    w.eng.wait_ge(semh[s_], w.c[s_])

    run_phase(ph_p5, barrier=True)
    p5ctx.close()
    pwctx.close()
    xctx.close()

    octx.close()
    return nc


_NC_CACHE = {}


def kernel(L0_attn_x, L0_attn_wq, L0_attn_bq, L0_attn_wk, L0_attn_bk, L0_attn_wv,
           L0_attn_bv, L0_attn_wo, L0_attn_bo, L0_attn_qn_w, L0_attn_kn_w,
           L0_ffn_clip_x, L0_ffn_clip_down, L0_ffn_w_gate, L0_ffn_w_up,
           L0_ffn_w_down):
    f32 = lambda a: np.ascontiguousarray(np.asarray(a, dtype=np.float32))
    x = f32(L0_attn_x)
    wq, wk, wv, wo = f32(L0_attn_wq), f32(L0_attn_wk), f32(L0_attn_wv), f32(L0_attn_wo)
    bq, bk = f32(L0_attn_bq).reshape(1, E), f32(L0_attn_bk).reshape(1, E)
    bv, bo = f32(L0_attn_bv).reshape(1, E), f32(L0_attn_bo).reshape(1, E)
    qn, kn = f32(L0_attn_qn_w).reshape(1, D), f32(L0_attn_kn_w).reshape(1, D)
    clipx = f32(L0_ffn_clip_x).reshape(1, 1)
    clipd = f32(L0_ffn_clip_down).reshape(1, 1)
    wg, wu, wd = f32(L0_ffn_w_gate), f32(L0_ffn_w_up), f32(L0_ffn_w_down)

    consts = _host_consts()
    if "nc" not in _NC_CACHE:
        _NC_CACHE["nc"] = build_nc()
    nc = _NC_CACHE["nc"]

    in_maps = _mk_inmaps(x, wq, wk, wv, wo, bq, bk, bv, bo, qn, kn, clipx, clipd,
                         wg, wu, wd, consts)
    res = run_bass_kernel_spmd(nc, in_maps, core_ids=list(range(8)))
    out = np.stack([res.results[c]["out"] for c in range(8)], axis=0)
    return out.astype(np.float32)


def _mk_inmaps(x, wq, wk, wv, wo, bq, bk, bv, bo, qn, kn, clipx, clipd,
               wg, wu, wd, consts):
    in_maps = []
    for c in range(8):
        lo = 3 * c
        gsh = np.ones((3, 128, E), np.float32)
        ush = np.ones((3, 128, E), np.float32)
        for j in range(3):
            g = lo + j
            if g < NI:
                gsh[j] = wg[128*g:128*(g+1), :]
                ush[j] = wu[128*g:128*(g+1), :]
        m = dict(x=np.ascontiguousarray(x[c]),
                 wq=wq, wk=wk, wv=wv, wo=wo, bq=bq, bk=bk, bv=bv, bo=bo,
                 qn=qn, kn=kn, clipx=clipx, clipd=clipd,
                 wg_sh=np.ascontiguousarray(gsh),
                 wu_sh=np.ascontiguousarray(ush),
                 wd_sh=np.ascontiguousarray(wd[128*c:128*(c+1), :]))
        m.update(consts)
        in_maps.append(m)
    return in_maps


# version 20
# speedup vs baseline: 1.0322x; 1.0035x over previous
import numpy as np
import ml_dtypes
from contextlib import ExitStack
import concourse.bass as bass
import concourse.mybir as mybir
from concourse.bass_utils import run_bass_kernel_spmd

F32 = mybir.dt.float32
BF16 = mybir.dt.bfloat16
I32 = mybir.dt.int32
AF = mybir.ActivationFunctionType
ALU = mybir.AluOpType
AX = mybir.AxisListType

T = 2048
E = 1024
NH = 16
D = 64
IF = 2816
NT = T // 128
NE = E // 128
NI = IF // 128
NTI = 4
NEG = -1e9
LN_EPS = 1e-5
RMS_EPS = float(np.finfo(np.float32).eps)
QMAX = 127.0
BF = ml_dtypes.bfloat16
SERIAL_P5 = 0
SERIAL_P4 = False
SERIAL_P1 = False
SERIAL_P0 = False
SERIAL_Q = False
SERIAL_P2 = False
SERIAL_P3 = False


def _host_consts():
    ident = np.eye(128, dtype=np.float32)
    dq = np.arange(128)[:, None]
    o = np.arange(128)[None, :]
    m0 = np.where(o <= dq, NEG, 0.0).astype(np.float32)
    m2 = np.where(o > dq, NEG, 0.0).astype(np.float32)
    maskrow = np.concatenate([m0, np.zeros((128, 128), np.float32), m2], axis=1)
    maskT = np.ascontiguousarray(
        np.concatenate([m0.T, np.zeros((128, 128), np.float32), m2.T], axis=1))
    qh = np.zeros((128, 128), np.float32)
    for ec in range(8):
        qh[0:64, 16 * ec + 2 * ec] = 1.0
        qh[64:128, 16 * ec + 2 * ec + 1] = 1.0
    rbc = np.zeros((16, 1024), np.float32)
    for ec in range(8):
        rbc[2 * ec, 128 * ec: 128 * ec + 64] = 1.0
        rbc[2 * ec + 1, 128 * ec + 64: 128 * ec + 128] = 1.0
    return {
        "identbf": ident.astype(BF),
        "identf": ident,
        "onesg": np.ones((128, 128), np.float32).astype(BF),
        "onesf": np.ones((1, 128), np.float32),
        "maskrow": maskrow.astype(BF),
        "maskTt": maskT.astype(BF),
        "qh_ind": qh.astype(BF),
        "rbc_ind": rbc.astype(BF),
    }


class RES:
    __slots__ = ("w", "r", "dram")
    def __init__(self, dram=False):
        self.w = None
        self.r = []
        self.dram = dram


class Walk:
    SEM_OF = {"sync": "LX", "gpsimd": "G", "tensor": "T", "vector": "V",
              "scalar": "S"}
    DMA_SEMS = ("LX", "LG", "ST", "LA0", "LA1", "LB0", "LB1", "LC0", "LC1",
                "LD0", "LD1", "SA0", "SA1")

    def __init__(self, semh, counts, res, eng, me, consts_res, zst):
        self.semh = semh
        self.c = counts
        self.res = res
        self.eng = eng
        self.me = me
        self.consts = consts_res
        self.fenced = False
        self.Z = zst
        self.serial = False

    def sync_all(self):
        for s_ in ("T", "V", "S", "G") + self.DMA_SEMS:
            if self.c[s_] > 0:
                self.eng.wait_ge(self.semh[s_], self.c[s_])

    def zs(self, e, instr):
        k = self.me
        self.Z[k] += 1
        instr.then_inc(self.semh["Z" + k], 1)
        e.wait_ge(self.semh["Z" + k], self.Z[k])
        return instr

    def step(self, who, emit, reads=(), writes=(), sem=None, n=1):
        sem = sem or self.SEM_OF[who]
        on = (who == self.me)
        instr = None
        rd = list(reads)
        if on:
            if self.serial:
                self.sync_all()
            if not self.fenced and self.consts is not None:
                rd.append(self.consts)
                self.fenced = True
            need_r = {}
            need_w = {}
            def _need(d, tok):
                if tok is not None:
                    s_, v_ = tok
                    if v_ > d.get(s_, -1):
                        d[s_] = v_
            for b in rd:
                _need(need_r, b.w)
            for b in writes:
                _need(need_w, b.w)
                for tok in b.r:
                    _need(need_w, tok)
            for s_, v_ in sorted(need_r.items()):
                self.eng.wait_ge(self.semh[s_], v_)
            for s_, v_ in sorted(need_w.items()):
                if need_r.get(s_, -1) >= v_:
                    continue
                if s_ == sem and s_ not in self.DMA_SEMS:
                    continue
                self.eng.wait_ge(self.semh[s_], v_)
            instr = emit(self.eng)
        self.c[sem] += n
        tok = (sem, self.c[sem])
        if on and instr is not None:
            instr.then_inc(self.semh[sem], n)
        for b in reads:
            b.r.append(tok)
        for b in writes:
            b.w = tok
            b.r = []
        return instr


def build_nc(debug=False):
    nc = bass.Bass("TRN2", target_bir_lowering=False)
    dti = lambda n, s, d: nc.dram_tensor(n, s, d, kind="ExternalInput").ap()
    x_d = dti("x", [T, E], F32)
    wq_d = dti("wq", [E, E], F32); wk_d = dti("wk", [E, E], F32)
    wv_d = dti("wv", [E, E], F32); wo_d = dti("wo", [E, E], F32)
    bq_d = dti("bq", [1, E], F32); bk_d = dti("bk", [1, E], F32)
    bv_d = dti("bv", [1, E], F32); bo_d = dti("bo", [1, E], F32)
    qn_d = dti("qn", [1, D], F32); kn_d = dti("kn", [1, D], F32)
    clipx_d = dti("clipx", [1, 1], F32); clipd_d = dti("clipd", [1, 1], F32)
    wg_d = dti("wg_sh", [3, 128, E], F32)
    wu_d = dti("wu_sh", [3, 128, E], F32)
    wd_d = dti("wd_sh", [128, IF], F32)
    identbf_d = dti("identbf", [128, 128], BF16)
    identf_d = dti("identf", [128, 128], F32)
    onesg_d = dti("onesg", [128, 128], BF16)
    onesf_d = dti("onesf", [1, 128], F32)
    maskrow_d = dti("maskrow", [128, 384], BF16)
    maskTt_d = dti("maskTt", [128, 384], BF16)
    qh_d = dti("qh_ind", [128, 128], BF16)
    rbc_d = dti("rbc_ind", [16, 1024], BF16)
    out_d = nc.dram_tensor("out", [T, E], F32, kind="ExternalOutput").ap()
    h_hbm = nc.dram_tensor("h_hbm", [NI, 128, T], F32).ap()
    attno_hbm = nc.dram_tensor("attno_hbm", [T, E], BF16).ap()
    cc1i = nc.dram_tensor("cc1i", [1, 1], F32).ap()
    cc1o = nc.dram_tensor("cc1o", [1, 1], F32).ap()
    cc2i = nc.dram_tensor("cc2i", [1, 1], F32).ap()
    cc2o = nc.dram_tensor("cc2o", [1, 1], F32).ap()
    agg_i = nc.dram_tensor("agg_i", [3, 128, 8, 128], BF16).ap()
    agg_o = nc.dram_tensor("agg_o", [8, 3, 128, 8, 128], BF16).ap()
    agu_i = nc.dram_tensor("agu_i", [3, 128, 8, 128], BF16).ap()
    agu_o = nc.dram_tensor("agu_o", [8, 3, 128, 8, 128], BF16).ap()
    agd_i = nc.dram_tensor("agd_i", [NI, 128, 128], BF16).ap()
    agd_o = nc.dram_tensor("agd_o", [8, NI, 128, 128], BF16).ap()
    scl_i = nc.dram_tensor("scl_i", [128, 8], F32).ap()
    scl_o = nc.dram_tensor("scl_o", [8, 128, 8], F32).ap()
    wdT_hbm = nc.dram_tensor("wdT_hbm", [NI, 128, E], BF16).ap()
    if debug:
        dbg_xn = nc.dram_tensor("dbg_xn", [128, 8, T], BF16, kind="ExternalOutput").ap()
        dbg_qh = nc.dram_tensor("dbg_qh", [128, 8, T], BF16, kind="ExternalOutput").ap()
        dbg_kh = nc.dram_tensor("dbg_kh", [128, 8, T], BF16, kind="ExternalOutput").ap()
        dbg_v = nc.dram_tensor("dbg_v", [128, 16, NH * 65], BF16, kind="ExternalOutput").ap()
        dbg_sc = nc.dram_tensor("dbg_sc", [1, 8], F32, kind="ExternalOutput").ap()
        dbg_rx = nc.dram_tensor("dbg_rx", [128, 4], F32, kind="ExternalOutput").ap()
        dbg_at = nc.dram_tensor("dbg_at", [T, E], BF16, kind="ExternalOutput").ap()
        dbg_h = nc.dram_tensor("dbg_h", [NI, 128, T], F32, kind="ExternalOutput").ap()
        dbg_st = nc.dram_tensor("dbg_st", [128, 96], F32, kind="ExternalOutput").ap()
        dbg_hq = nc.dram_tensor("dbg_hq", [128, NI, 512], BF16, kind="ExternalOutput").ap()
        dbg_ssw = nc.dram_tensor("dbg_ssw", [128, 24], F32, kind="ExternalOutput").ap()
        dbg_agg = nc.dram_tensor("dbg_agg", [8, 3, 128, 8, 128], BF16, kind="ExternalOutput").ap()
        dbg_sl = nc.dram_tensor("dbg_sl", [128, 4, 512], F32, kind="ExternalOutput").ap()
        dbg_wg0 = nc.dram_tensor("dbg_wg0", [128, 8, 128], BF16, kind="ExternalOutput").ap()
        dbg_ag0 = nc.dram_tensor("dbg_ag0", [128, 8, 128], BF16, kind="ExternalOutput").ap()
        dbg_wp = nc.dram_tensor("dbg_wp", [128, 8, E], BF16, kind="ExternalOutput").ap()

    octx = ExitStack()
    sb = lambda st, n, s, d: st.enter_context(nc.sbuf_tensor(n, s, d))
    pt = lambda n, s, d: octx.enter_context(nc.psum_tensor(n, s, d))

    identbf = sb(octx, "identbf_s", [128, 128], BF16)
    identf = sb(octx, "identf_s", [128, 128], F32)
    onesg = sb(octx, "onesg_s", [128, 128], BF16)
    onesf = sb(octx, "onesf_s", [1, 128], F32)
    maskrow = sb(octx, "maskrow_s", [128, 384], BF16)
    maskTt = sb(octx, "maskTt_s", [128, 384], BF16)
    qh_ind = sb(octx, "qh_ind_s", [128, 128], BF16)
    rbc_ind = sb(octx, "rbc_ind_s", [16, 1024], BF16)
    bqc = sb(octx, "bqc", [128, 8], F32); bkc = sb(octx, "bkc", [128, 8], F32)
    bvrow = sb(octx, "bvrow", [1, E], BF16); borow = sb(octx, "borow", [1, E], BF16)
    qncol = sb(octx, "qncol", [128, 1], F32); kncol = sb(octx, "kncol", [128, 1], F32)
    qn8 = sb(octx, "qn8", [128, 1], F32)
    bqq = sb(octx, "bqq", [128, 8], F32); bkk = sb(octx, "bkk", [128, 8], F32)
    clipx_s = sb(octx, "clipx_s", [1, 1], F32); clipd_s = sb(octx, "clipd_s", [1, 1], F32)
    stat = sb(octx, "stat", [128, 96], F32)
    sumsq = sb(octx, "sumsq", [128, 16], F32)
    rabs = sb(octx, "rabs", [128, 16], F32)
    habs = sb(octx, "habs", [128, 96], F32)
    sc11 = sb(octx, "sc11", [1, 8], F32)
    rxsx = sb(octx, "rxsx", [128, 4], F32)
    acol = sb(octx, "acol", [128, 16], F32); b2col = sb(octx, "b2col", [128, 16], F32)
    sclsb = sb(octx, "sclsb", [128, 8, 8], F32)
    sxswg = sb(octx, "sxswg", [128, 24], F32)
    sxswu = sb(octx, "sxswu", [128, 24], F32)
    vsmall = sb(octx, "vsmall", [128, 2], F32)
    gtr = sb(octx, "gtr", [128, 32], F32)
    epsln = sb(octx, "epsln", [128, 1], F32)
    epsrms = sb(octx, "epsrms", [128, 1], F32)
    xbuf = sb(octx, "xbuf", [128, 2, E], F32)
    mnegf = sb(octx, "mnegf", [128, 2], F32)
    recipz = sb(octx, "recipz", [128, 2], F32)
    ptsb = sb(octx, "ptsb", [128, 2, 384], BF16)
    prow = sb(octx, "prow", [128, 2, 384], BF16)
    attok = sb(octx, "attok", [128, E], BF16)
    afm = sb(octx, "afm", [128, 8, 128], BF16)
    attst = sb(octx, "attst", [128, 2, E], BF16)

    psA0 = pt("psA0", [128, 512], F32)
    psA1 = pt("psA1", [128, 512], F32)
    psB0 = pt("psB0", [128, 512], F32)
    psB1 = pt("psB1", [128, 512], F32)
    psC0 = pt("psC0", [128, 512], F32)
    psC1 = pt("psC1", [128, 512], F32)
    psT = pt("psT", [128, 1024], BF16)
    psU = pt("psU", [128, 512], BF16)
    psA = [psA0, psA1]; psB = [psB0, psB1]; psC = [psC0, psC1]
    psUT = [psU, psT]

    semh = {}
    for n_ in ("LX", "LG", "T", "V", "S", "G", "ST",
               "LA0", "LA1", "LB0", "LB1", "LC0", "LC1", "LD0", "LD1",
               "SA0", "SA1",
               "Zvector", "Zscalar", "Zgpsimd", "Ztensor", "Zsync"):
        semh[n_] = octx.enter_context(nc.semaphore(n_))
    block = octx.enter_context(nc.Block())

    RNAMES = ["psA0", "psA1", "psB0", "psB1", "psC0", "psC1", "psT", "psU",
              "xbuf0", "xbuf1", "xnbuf0", "xnbuf1", "sqscr0", "sqscr1",
              "stat", "sumsq", "rabs", "habs", "sc11", "rxsx", "acol",
              "sclsb", "sxsw", "small", "xnfm0", "xnfm1", "xnfm2", "xnfm3", "qhat", "khat", "vsb",
              "wv", "wo", "prel0", "prel1", "qsqb0", "qsqb1", "sdq", "rqf",
              "rqbf0", "rqbf1",
              "mneg0", "mneg1", "mrow0", "mrow1", "recipz0", "recipz1",
              "ptsb0", "ptsb1", "prow0", "prow1", "attok", "afm",
              "attst0", "attst1",
              "wshard", "wdshard", "wint", "stg", "sclpack",
              "xqi", "xqb0", "xqb1",
              "wgt0", "wgt1", "wut0", "wut1", "silub", "upsb",
              "hbuf0", "hbuf1", "wdt", "hstg0", "hstg1", "hint",
              "outsb0", "outsb1", "tmpadd", "biases", "consts", "tmpdbg"]
    RNAMES += [f"hq{hb}_{g}" for hb in range(2) for g in range(NI)]
    RNAMES += [f"wqk{s}" for s in range(4)]
    DRAM = ["d_hhbm", "d_attno", "d_cc1i", "d_cc1o", "d_cc2i", "d_cc2o",
            "d_aggi", "d_aggo", "d_agui", "d_aguo", "d_agdi", "d_agdo",
            "d_scli", "d_sclo", "d_wdT"]

    def make_res():
        r = {n_: RES() for n_ in RNAMES}
        r.update({n_: RES(dram=True) for n_ in DRAM})
        return r

    ENGS = ("sync", "gpsimd", "tensor", "vector", "scalar")
    EOBJ = {"sync": nc.sync, "gpsimd": nc.gpsimd, "tensor": nc.tensor,
            "vector": nc.vector, "scalar": nc.scalar}
    state = {"c": {n_: 0 for n_ in semh}, "res": make_res(),
             "z": {e_: 0 for e_ in ENGS}}

    def run_phase(phase_fn, barrier=False):
        final_c = final_r = None
        base = dict(state["c"])
        for me in ENGS:
            counts = dict(base)
            res = {k: _res_copy(v) for k, v in state["res"].items()}
            def body(e, me=me, counts=counts, res=res):
                if barrier:
                    for s_ in ("T", "V", "S") + Walk.DMA_SEMS:
                        if base[s_] > 0:
                            e.wait_ge(semh[s_], base[s_])
                w = Walk(semh, counts, res, e, me, res["consts"], state["z"])
                phase_fn(w, res)
            getattr(block, me)(body)
            final_c, final_r = counts, res
        state["c"] = final_c
        state["res"] = final_r

    def _res_copy(b):
        n = RES(b.dram)
        n.w = b.w
        n.r = list(b.r)
        return n

    def ph_loads(w, R):
        loads = [(identbf, identbf_d), (identf, identf_d),
                 (onesg, onesg_d), (maskrow, maskrow_d),
                 (maskTt, maskTt_d), (qh_ind, qh_d), (rbc_ind, rbc_d),
                 (onesf, onesf_d), (clipx_s, clipx_d), (clipd_s, clipd_d)]
        for dst, src in loads:
            w.step("sync", lambda e, dst=dst, src=src:
                   e.dma_start(out=dst[:], in_=src),
                   writes=[R["consts"]], sem="LX", n=16)
        w.step("sync", lambda e: e.dma_start(
            out=bqc[:], in_=bq_d.rearrange("a (c p) -> p (a c)", p=128),
            allow_slow_non_contiguous=True),
            writes=[R["biases"]], sem="LX", n=16)
        w.step("sync", lambda e: e.dma_start(
            out=bkc[:], in_=bk_d.rearrange("a (c p) -> p (a c)", p=128),
            allow_slow_non_contiguous=True),
            writes=[R["biases"]], sem="LX", n=16)
        for dst, src in ((qncol, qn_d), (kncol, kn_d)):
            w.step("sync", lambda e, dst=dst, src=src: e.dma_start(
                out=dst[0:64, :], in_=src.rearrange("a d -> d a"),
                allow_slow_non_contiguous=True),
                writes=[R["biases"]], sem="LX", n=16)
            w.step("sync", lambda e, dst=dst, src=src: e.dma_start(
                out=dst[64:128, :], in_=src.rearrange("a d -> d a"),
                allow_slow_non_contiguous=True),
                writes=[R["biases"]], sem="LX", n=16)
        w.step("gpsimd", lambda e: e.dma_start(out=bvrow[:], in_=bv_d),
               writes=[R["biases"]], sem="LG", n=16)
        w.step("gpsimd", lambda e: e.dma_start(out=borow[:], in_=bo_d),
               writes=[R["biases"]], sem="LG", n=16)

        def _pre(e):
            w.zs(e, e.tensor_scalar(out=qn8[:], in0=qncol[:], scalar1=0.125,
                                    scalar2=None, op0=ALU.mult))
            e.tensor_scalar(out=bqq[:], in0=bqc[:], scalar1=qn8[:], scalar2=None,
                            op0=ALU.mult)
            e.tensor_scalar(out=bkk[:], in0=bkc[:], scalar1=kncol[:], scalar2=None,
                            op0=ALU.mult)
            e.memset(epsln[:], LN_EPS)
            return e.memset(epsrms[:], RMS_EPS)
        w.step("vector", _pre, reads=[R["biases"]], writes=[R["biases"]])

    run_phase(ph_loads)

    xctx = ExitStack()
    xnfm = sb(xctx, "xnfm", [128, 8, T], BF16)
    lctx = ExitStack()
    qhat = sb(lctx, "qhat", [128, 8, T], BF16)
    khat = sb(lctx, "khat", [128, 8, T], BF16)
    vsb = sb(lctx, "vsb", [128, 16, NH * 65], BF16)

    qctx = ExitStack()
    wshard = sb(qctx, "wshard", [128, 3, E], F32)
    wdshard = sb(qctx, "wdshard", [128, IF], F32)
    wi32 = sb(qctx, "wi32", [128, E], I32)
    wbf = sb(qctx, "wbf", [128, E], BF16)
    stg = sb(qctx, "stg", [128, 8, 128], BF16)
    sclpack = sb(qctx, "sclpack", [128, 8], F32)
    absq = sb(qctx, "absq", [128, 8], F32)
    rq127 = sb(qctx, "rq127", [128, 8], F32)

    def ph_quant(w, R):
        w.serial = SERIAL_Q
        w.step("sync", lambda e: e.dma_start(
            out=wshard[:], in_=wg_d.rearrange("j (a p) e -> p j (a e)", p=128)),
            writes=[R["wshard"]], sem="LB0", n=16)
        w.step("sync", lambda e: e.dma_start(out=wdshard[:], in_=wd_d),
               writes=[R["wdshard"]], sem="LB0", n=16)
        for grp in range(2):
            agdst = agg_i if grp == 0 else agu_i
            dram_i = R["d_aggi"] if grp == 0 else R["d_agui"]
            if grp == 1:
                w.step("sync", lambda e: e.dma_start(
                    out=wshard[:],
                    in_=wu_d.rearrange("j (a p) e -> p j (a e)", p=128)),
                    writes=[R["wshard"]], sem="LB0", n=16)
            for j in range(3):
                k = grp * 3 + j
                def _q(e, j=j, k=k):
                    w.zs(e, e.tensor_reduce(out=absq[:, k:k+1], in_=wshard[:, j, :],
                                            axis=AX.X, op=ALU.max,
                                            apply_absolute_value=True))
                    w.zs(e, e.reciprocal(out=rq127[:, k:k+1], in_=absq[:, k:k+1]))
                    e.tensor_scalar(out=sclpack[:, k:k+1], in0=absq[:, k:k+1],
                                    scalar1=1.0 / QMAX, scalar2=None, op0=ALU.mult)
                    e.tensor_scalar(out=wi32[:], in0=wshard[:, j, :],
                                    scalar1=rq127[:, k:k+1], scalar2=QMAX,
                                    op0=ALU.mult, op1=ALU.mult)
                    return e.tensor_copy(out=wbf[:], in_=wi32[:])
                w.step("vector", _q, reads=[R["wshard"]],
                       writes=[R["wint"], R["sclpack"]])
                def _t(e):
                    for ec in range(8):
                        i = e.transpose(out=psT[:, 128*ec:128*(ec+1)],
                                        in_=wbf[:, 128*ec:128*(ec+1)],
                                        identity=identbf[:])
                    return i
                w.step("tensor", _t, reads=[R["wint"]], writes=[R["psT"]])
                w.step("scalar", lambda e: e.copy(
                    out=stg[:], in_=psT[:].rearrange("p (ec t) -> p ec t", ec=8)),
                    reads=[R["psT"]], writes=[R["stg"]])
                w.step("sync", lambda e, agdst=agdst, j=j: e.dma_start(
                    out=agdst[j], in_=stg[:]),
                    reads=[R["stg"]], writes=[dram_i], sem="SA0", n=16)
        def _qd(e):
            w.zs(e, e.tensor_reduce(out=absq[:, 6:7], in_=wdshard[:], axis=AX.X,
                                    op=ALU.max, apply_absolute_value=True))
            w.zs(e, e.reciprocal(out=rq127[:, 6:7], in_=absq[:, 6:7]))
            return e.tensor_scalar(out=sclpack[:, 6:7], in0=absq[:, 6:7],
                                   scalar1=1.0 / QMAX, scalar2=None, op0=ALU.mult)
        w.step("vector", _qd, reads=[R["wdshard"]], writes=[R["sclpack"]])
        for b in range(NI):
            def _qb(e, b=b):
                w.zs(e, e.tensor_scalar(out=wi32[:, 0:128],
                                        in0=wdshard[:, 128*b:128*(b+1)],
                                        scalar1=rq127[:, 6:7], scalar2=QMAX,
                                        op0=ALU.mult, op1=ALU.mult))
                return e.tensor_scalar(out=wbf[:, 0:128], in0=wi32[:, 0:128],
                                       scalar1=sclpack[:, 6:7], scalar2=None,
                                       op0=ALU.mult)
            w.step("vector", _qb, reads=[R["wdshard"], R["sclpack"]],
                   writes=[R["wint"]])
            w.step("tensor", lambda e: e.transpose(
                out=psT[:, 0:128], in_=wbf[:, 0:128], identity=identbf[:]),
                reads=[R["wint"]], writes=[R["psT"]])
            w.step("scalar", lambda e: e.copy(out=stg[:, 0, :], in_=psT[:, 0:128]),
                   reads=[R["psT"]], writes=[R["stg"]])
            w.step("sync", lambda e, b=b: e.dma_start(out=agd_i[b], in_=stg[:, 0, :]),
                   reads=[R["stg"]], writes=[R["d_agdi"]], sem="SA0", n=16)
        w.step("sync", lambda e: e.dma_start(out=scl_i, in_=sclpack[:]),
               reads=[R["sclpack"]], writes=[R["d_scli"]], sem="SA0", n=16)
        for ins, outs, ri, ro in ((agg_i, agg_o, "d_aggi", "d_aggo"),
                                  (agu_i, agu_o, "d_agui", "d_aguo"),
                                  (agd_i, agd_o, "d_agdi", "d_agdo"),
                                  (scl_i, scl_o, "d_scli", "d_sclo")):
            w.step("gpsimd", lambda e, ins=ins, outs=outs: e.collective_compute(
                "AllGather", ALU.bypass, replica_groups=[list(range(8))],
                ins=[ins.opt()], outs=[outs.opt()]),
                reads=[R[ri]], writes=[R[ro]], sem="G")
        for ro in ("d_aggo", "d_aguo", "d_agdo", "d_sclo"):
            R[ro].w = ("G", w.c["G"])
        for r8 in range(8):
            w.step("gpsimd", lambda e, r8=r8: e.dma_start(
                out=wdT_hbm[:, :, 128*r8:128*(r8+1)], in_=agd_o[r8]),
                reads=[R["d_agdo"]], writes=[R["d_wdT"]], sem="LG", n=16)
        w.step("gpsimd", lambda e: e.dma_start(
            out=sclsb[:], in_=scl_o.rearrange("r p c -> p r c")),
            reads=[R["d_sclo"]], writes=[R["sclsb"]], sem="LD1", n=16)

    run_phase(ph_quant)

    sqscr = sb(qctx, "sqscr", [128, 2, E], F32)
    xnbuf = sb(qctx, "xnbuf", [128, 2, E], BF16)

    def ph_p0(w, R):
        w.serial = SERIAL_P0
        w.step("vector", lambda e: e.memset(
            vsb[:].rearrange("p tc (h d) -> p tc h d", d=65)[:, :, :, 64:65], 1.0),
            writes=[R["vsb"]])
        for tc in range(NT):
            par = tc % 2
            xb = R[f"xbuf{par}"]
            w.step("sync", lambda e, tc=tc, par=par: e.dma_start(
                out=xbuf[:, par, :], in_=x_d[128*tc:128*(tc+1), :]),
                writes=[xb], sem=f"LA{par}", n=16)
            def _red(e, tc=tc, par=par):
                e.reduce_sum(out=stat[:, tc:tc+1], in_=xbuf[:, par, :], axis=AX.X)
                e.reduce_max(out=stat[:, 16+tc:17+tc], in_=xbuf[:, par, :], axis=AX.X)
                return e.tensor_reduce(out=stat[:, 32+tc:33+tc], in_=xbuf[:, par, :],
                                       axis=AX.X, op=ALU.min)
            w.step("vector", _red, reads=[xb], writes=[R["stat"]])
            w.step("scalar", lambda e, tc=tc, par=par: e.activation(
                out=sqscr[:, par, :], in_=xbuf[:, par, :], func=AF.Square,
                accum_out=sumsq[:, tc:tc+1]),
                reads=[xb], writes=[R[f"sqscr{par}"], R["sumsq"]])
            def _st(e, tc=tc):
                w.zs(e, e.tensor_scalar(out=stat[:, 48+tc:49+tc], in0=stat[:, tc:tc+1],
                                        scalar1=1.0/E, scalar2=None, op0=ALU.mult))
                w.zs(e, e.tensor_scalar(out=vsmall[:, 0:1], in0=sumsq[:, tc:tc+1],
                                        scalar1=1.0/E, scalar2=None, op0=ALU.mult))
                w.zs(e, e.tensor_tensor(out=vsmall[:, 1:2], in0=stat[:, 48+tc:49+tc],
                                        in1=stat[:, 48+tc:49+tc], op=ALU.mult))
                return e.tensor_tensor(out=vsmall[:, 0:1], in0=vsmall[:, 0:1],
                                       in1=vsmall[:, 1:2], op=ALU.subtract)
            w.step("vector", _st, reads=[R["sumsq"], R["stat"]],
                   writes=[R["small"], R["stat"]])
            w.step("scalar", lambda e: e.activation(
                out=vsmall[:, 1:2], in_=vsmall[:, 0:1], func=AF.Sqrt,
                bias=epsln[:], scale=1.0),
                reads=[R["small"]], writes=[R["small"]])
            def _rr(e, tc=tc):
                w.zs(e, e.reciprocal(out=stat[:, 64+tc:65+tc], in_=vsmall[:, 1:2]))
                return e.tensor_scalar(out=stat[:, 80+tc:81+tc],
                                       in0=stat[:, 48+tc:49+tc],
                                       scalar1=stat[:, 64+tc:65+tc], scalar2=-1.0,
                                       op0=ALU.mult, op1=ALU.mult)
            w.step("vector", _rr, reads=[R["small"]], writes=[R["stat"]])
            w.step("scalar", lambda e, tc=tc, par=par: e.activation(
                out=xnbuf[:, par, :], in_=xbuf[:, par, :], func=AF.Identity,
                scale=stat[:, 64+tc:65+tc], bias=stat[:, 80+tc:81+tc]),
                reads=[xb, R["stat"]], writes=[R[f"xnbuf{par}"]])
            def _ra(e, tc=tc):
                w.zs(e, e.tensor_tensor(out=vsmall[:, 0:1], in0=stat[:, 16+tc:17+tc],
                                        in1=stat[:, 48+tc:49+tc], op=ALU.subtract))
                w.zs(e, e.tensor_tensor(out=rabs[:, tc:tc+1], in0=stat[:, 48+tc:49+tc],
                                        in1=stat[:, 32+tc:33+tc], op=ALU.subtract))
                w.zs(e, e.tensor_tensor(out=rabs[:, tc:tc+1], in0=rabs[:, tc:tc+1],
                                        in1=vsmall[:, 0:1], op=ALU.max))
                return e.tensor_scalar(out=rabs[:, tc:tc+1], in0=rabs[:, tc:tc+1],
                                       scalar1=stat[:, 64+tc:65+tc], scalar2=None,
                                       op0=ALU.mult)
            w.step("vector", _ra, reads=[R["stat"]], writes=[R["rabs"], R["small"]])
            def _tr(e, par=par):
                for s8 in range(8):
                    i = e.transpose(out=psT[:, 128*s8:128*(s8+1)],
                                    in_=xnbuf[:, par, 128*s8:128*(s8+1)],
                                    identity=identbf[:])
                return i
            w.step("tensor", _tr, reads=[R[f"xnbuf{par}"]], writes=[R["psT"]])
            w.step("scalar", lambda e, tc=tc: e.copy(
                out=xnfm[:, :, 128*tc:128*(tc+1)],
                in_=psT[:].rearrange("p (ec t) -> p ec t", ec=8)),
                reads=[R["psT"]], writes=[R[f"xnfm{tc//4}"]])
        def _gr(e):
            e.memset(gtr[:], 0.0)
            return e.tensor_reduce(out=gtr[:, 0:1], in_=rabs[:], axis=AX.X,
                                   op=ALU.max)
        w.step("vector", _gr, reads=[R["rabs"]], writes=[R["small"]])
        w.step("tensor", lambda e: e.transpose(out=psC1[0:32, 0:128],
                                               in_=gtr[:, 0:32], identity=identf[:]),
               reads=[R["small"]], writes=[R["psC1"]])
        w.step("vector", lambda e: e.reduce_max(out=sc11[:, 0:1],
                                                in_=psC1[0:1, 0:128], axis=AX.X),
               reads=[R["psC1"]], writes=[R["sc11"]])
        w.step("gpsimd", lambda e: e.dma_start(out=cc1i, in_=sc11[:, 0:1]),
               reads=[R["sc11"]], writes=[R["d_cc1i"]], sem="LC1", n=16)
        w.step("gpsimd", lambda e: e.collective_compute(
            "AllReduce", ALU.max, replica_groups=[list(range(8))],
            ins=[cc1i.opt()], outs=[cc1o.opt()]),
            reads=[R["d_cc1i"]], writes=[R["d_cc1o"]], sem="G")
        w.step("gpsimd", lambda e: e.dma_start(out=sc11[:, 0:1], in_=cc1o),
               reads=[R["d_cc1o"]], writes=[R["sc11"]], sem="LD0", n=16)
        w.step("scalar", lambda e: e.activation(out=sc11[:, 1:2], in_=clipx_s[:],
                                                func=AF.Sigmoid),
               reads=[R["sc11"]], writes=[R["sc11"]])
        def _sx(e):
            w.zs(e, e.tensor_tensor(out=sc11[:, 2:3], in0=sc11[:, 1:2],
                                    in1=sc11[:, 0:1], op=ALU.mult))
            w.zs(e, e.reciprocal(out=sc11[:, 3:4], in_=sc11[:, 2:3]))
            w.zs(e, e.tensor_scalar(out=sc11[:, 3:4], in0=sc11[:, 3:4], scalar1=QMAX,
                                    scalar2=None, op0=ALU.mult))
            return e.tensor_scalar(out=sc11[:, 4:5], in0=sc11[:, 2:3],
                                   scalar1=1.0/QMAX, scalar2=None, op0=ALU.mult)
        w.step("vector", _sx, reads=[R["sc11"]], writes=[R["sc11"]])
        def _bc(e):
            e.matmul(out=psC0[:, 0:1], lhsT=onesf[:], rhs=sc11[:, 3:4],
                     start=True, stop=False)
            return e.matmul(out=psC0[:, 1:2], lhsT=onesf[:], rhs=sc11[:, 4:5],
                            start=False, stop=True)
        w.step("tensor", _bc, reads=[R["sc11"]], writes=[R["psC0"]])
        w.step("vector", lambda e: e.tensor_copy(out=rxsx[:, 0:2], in_=psC0[:, 0:2]),
               reads=[R["psC0"]], writes=[R["rxsx"]])
        def _ssw(e):
            e.tensor_scalar(out=sxswg[:].rearrange("p (r c) -> p r c", c=3),
                            in0=sclsb[:, :, 0:3],
                            scalar1=rxsx[:, 1:2], scalar2=None, op0=ALU.mult)
            return e.tensor_scalar(out=sxswu[:].rearrange("p (r c) -> p r c", c=3),
                                   in0=sclsb[:, :, 3:6],
                                   scalar1=rxsx[:, 1:2], scalar2=None, op0=ALU.mult)
        w.step("vector", _ssw, reads=[R["sclsb"], R["rxsx"]], writes=[R["sxsw"]])

    run_phase(ph_p0)
    qctx.close()

    p1ctx = ExitStack()
    wpbuf = sb(p1ctx, "wpbuf", [128, 8, E], BF16)
    prel = sb(p1ctx, "prel", [128, 8, 512], BF16)
    qsqb = sb(p1ctx, "qsqb", [128, 2, 512], BF16)
    sdq = sb(p1ctx, "sdq", [16, 512], F32)
    rqbf = sb(p1ctx, "rqbf", [16, 512], BF16)
    wosb = sb(p1ctx, "wosb", [128, 8, E], BF16)

    def ph_p1(w, R):
        w.serial = SERIAL_P1
        pend = []

        def flush_pend():
            while pend:
                pend.pop(0)()

        def unit(ti, pj):
            t0 = 512 * ti
            flush_pend()
            bcols = bqc if pj == 0 else bkc
            ncol = qn8 if pj == 0 else kncol
            bnn = bqq if pj == 0 else bkk
            pssum, sumR = (psC0, R["psC0"]) if pj == 0 else (psC1, R["psC1"])
            dst, dstR = (qhat, R["qhat"]) if pj == 0 else (khat, R["khat"])
            for ec in range(8):
                pa = ec % 2
                def _mm(e, ec=ec, pa=pa):
                    for ei in range(8):
                        i = e.matmul(out=psA[pa][:],
                                     lhsT=wpbuf[:, ei, 128*ec:128*(ec+1)],
                                     rhs=xnfm[:, ei, t0:t0+512],
                                     start=(ei == 0), stop=(ei == 7))
                    return i
                w.step("tensor", _mm, reads=[R["wv"], R[f"xnfm{ti}"]],
                       writes=[R[f"psA{pa}"]])
                def _ep(e, ec=ec, pa=pa):
                    e.activation(out=qsqb[:, pa, :], in_=psA[pa][:], func=AF.Square,
                                 bias=bcols[:, ec:ec+1], scale=1.0)
                    return e.activation(out=prel[:, ec, :], in_=psA[pa][:],
                                        func=AF.Identity, scale=ncol[:],
                                        bias=bnn[:, ec:ec+1])
                w.step("scalar", _ep, reads=[R[f"psA{pa}"], R["biases"]],
                       writes=[R[f"qsqb{pa}"], R["prel0"]])
                if ec >= 1:
                    hs(ti, pj, ec - 1, pssum, sumR)
            hs(ti, pj, 7, pssum, sumR)
            w.step("scalar", lambda e: e.activation(
                out=sdq[:], in_=pssum[0:16, :], func=AF.Sqrt,
                scale=1.0 / D, bias=epsrms[0:16, :]),
                reads=[sumR], writes=[R["sdq"]])
            def _rc(e):
                w.zs(e, e.reciprocal(out=sdq[:], in_=sdq[:]))
                return e.tensor_copy(out=rqbf[:], in_=sdq[:])
            w.step("vector", _rc, reads=[R["sdq"]], writes=[R["sdq"], R["rqbf0"]])
            for ec in range(8):
                pend.append(lambda ec=ec, dst=dst, dstR=dstR, t0=t0:
                            fa(ec, dst, dstR, t0))

        def hs(ti, pj, ec, pssum, sumR):
            pa = ec % 2
            w.step("tensor", lambda e: e.matmul(
                out=pssum[0:16, :], lhsT=qh_ind[:, 16*ec:16*(ec+1)],
                rhs=qsqb[:, pa, :], start=(ec == 0), stop=(ec == 7)),
                reads=[R[f"qsqb{pa}"]], writes=[sumR])

        def fa(ec, dst, dstR, t0):
            pb = ec % 2
            w.step("tensor", lambda e: e.matmul(
                out=psB[pb][:], lhsT=rbc_ind[:, 128*ec:128*(ec+1)],
                rhs=rqbf[:], start=True, stop=True),
                reads=[R["rqbf0"]], writes=[R[f"psB{pb}"]])
            w.step("vector", lambda e: e.scalar_tensor_tensor(
                out=dst[:, ec, t0:t0+512], in0=prel[:, ec, :],
                scalar=1.0, in1=psB[pb][:], op0=ALU.mult, op1=ALU.mult),
                reads=[R["prel0"], R[f"psB{pb}"]], writes=[dstR])

        for pj in range(2):
            wsrc = wq_d if pj == 0 else wk_d
            w.step("gpsimd", lambda e, wsrc=wsrc: e.dma_start(
                out=wpbuf[:], in_=wsrc.rearrange("(ei p) eo -> p ei eo", p=128)),
                writes=[R["wv"]], sem="LG", n=16)
            for ti in range(NTI):
                unit(ti, pj)
        flush_pend()
        w.step("gpsimd", lambda e: e.dma_start(
            out=wpbuf[:], in_=wv_d.rearrange("(ei p) eo -> p ei eo", p=128)),
            writes=[R["wv"]], sem="LG", n=16)
        for tc in range(NT):
            for half in range(2):
                pa = (2 * tc + half) % 2
                def _vm(e, tc=tc, half=half, pa=pa):
                    for ei in range(8):
                        e.matmul(out=psA[pa][:], lhsT=xnfm[:, ei, 128*tc:128*(tc+1)],
                                 rhs=wpbuf[:, ei, 512*half:512*(half+1)],
                                 start=(ei == 0), stop=False)
                    return e.matmul(out=psA[pa][:], lhsT=onesg[0:1, :],
                                    rhs=bvrow[:, 512*half:512*(half+1)],
                                    start=False, stop=True)
                w.step("tensor", _vm, reads=[R[f"xnfm{tc//4}"], R["wv"], R["biases"]],
                       writes=[R[f"psA{pa}"]])
                w.step("scalar", lambda e, tc=tc, half=half, pa=pa: e.copy(
                    out=vsb[:, tc, :].rearrange("p (h d) -> p h d", d=65)
                        [:, 8*half:8*(half+1), 0:64],
                    in_=psA[pa][:].rearrange("p (h d) -> p h d", d=64)),
                    reads=[R[f"psA{pa}"]], writes=[R["vsb"]])

    run_phase(ph_p1, barrier=True)
    if debug:
        def ph_dbg1b(w, R):
            w.step("sync", lambda e: e.dma_start(out=dbg_wp, in_=wpbuf[:]),
                   reads=[R["wv"]], sem="ST", n=16)
            w.step("sync", lambda e: e.dma_start(out=dbg_qh, in_=qhat[:]),
                   reads=[R["qhat"]], sem="ST", n=16)
            w.step("sync", lambda e: e.dma_start(out=dbg_kh, in_=khat[:]),
                   reads=[R["khat"]], sem="ST", n=16)
            w.step("sync", lambda e: e.dma_start(out=dbg_v, in_=vsb[:]),
                   reads=[R["vsb"]], sem="ST", n=16)
        run_phase(ph_dbg1b)


    def ph_p4(w, R):
        w.serial = SERIAL_P4
        w.step("gpsimd", lambda e: e.dma_start(
            out=wosb[:], in_=wo_d.rearrange("(ei p) eo -> p ei eo", p=128)),
            writes=[R["wo"]], sem="LC0", n=16)

        def geom(qi):
            nw = min(qi + 1, 3)
            return nw, qi - nw + 1, 128 * nw, 384 - 128 * nw

        def stA(qi, h):
            k = qi * NH + h
            nw, wc0, Wd, moff = geom(qi)
            ch, rb, pa = h // 2, 64 * (h % 2), k % 2
            def _row(e):
                e.matmul(out=psA[pa][:, 0:Wd],
                         lhsT=qhat[rb:rb+64, ch, 128*qi:128*(qi+1)],
                         rhs=khat[rb:rb+64, ch, 128*wc0:128*wc0+Wd],
                         start=True, stop=False)
                return e.matmul(out=psA[pa][:, 0:Wd], lhsT=identbf[:],
                                rhs=maskrow[:, moff:384], start=False, stop=True)
            w.step("tensor", _row, reads=[R["qhat"], R["khat"]],
                   writes=[R[f"psA{pa}"]])

        def stM(qi, h):
            k = qi * NH + h
            nw, wc0, Wd, moff = geom(qi)
            pa = k % 2
            w.step("vector", lambda e: e.reduce_max(
                out=mnegf[:, pa:pa+1], in_=psA[pa][:, 0:Wd], axis=AX.X, negate=True),
                reads=[R[f"psA{pa}"]], writes=[R[f"mneg{pa}"]])

        def stE(qi, h):
            k = qi * NH + h
            nw, wc0, Wd, moff = geom(qi)
            pa = k % 2
            w.step("scalar", lambda e: e.activation(
                out=prow[:, pa, 0:Wd], in_=psA[pa][:, 0:Wd], func=AF.Exp,
                bias=mnegf[:, pa:pa+1], scale=1.0),
                reads=[R[f"psA{pa}"], R[f"mneg{pa}"]], writes=[R[f"prow{pa}"]])

        def stT(qi, h):
            k = qi * NH + h
            nw, wc0, Wd, moff = geom(qi)
            pa = k % 2
            def _tt(e):
                for c in range(nw):
                    i = e.transpose(out=psUT[pa][:, 128*c:128*(c+1)],
                                    in_=prow[:, pa, 128*c:128*(c+1)],
                                    identity=identbf[:])
                return i
            w.step("tensor", _tt, reads=[R[f"prow{pa}"]],
                   writes=[R["psU"] if pa == 0 else R["psT"]])

        def stPC(qi, h):
            k = qi * NH + h
            nw, wc0, Wd, moff = geom(qi)
            pa = k % 2
            w.step("vector", lambda e: e.tensor_copy(
                out=ptsb[:, pa, 0:Wd], in_=psUT[pa][:, 0:Wd]),
                reads=[R["psU"] if pa == 0 else R["psT"]], writes=[R[f"ptsb{pa}"]])

        def stPV(qi, h):
            k = qi * NH + h
            nw, wc0, Wd, moff = geom(qi)
            pa = k % 2
            def _pv(e):
                for c in range(nw):
                    i = e.matmul(out=psC[pa][:, 0:65],
                                 lhsT=ptsb[:, pa, 128*c:128*(c+1)],
                                 rhs=vsb[:, wc0+c, 65*h:65*(h+1)],
                                 start=(c == 0), stop=(c == nw - 1))
                return i
            w.step("tensor", _pv, reads=[R[f"ptsb{pa}"], R["vsb"]],
                   writes=[R[f"psC{pa}"]])

        def stRZ(qi, h):
            k = qi * NH + h
            pa = k % 2
            w.step("vector", lambda e: e.reciprocal(
                out=recipz[:, pa:pa+1], in_=psC[pa][:, 64:65]),
                reads=[R[f"psC{pa}"]], writes=[R[f"recipz{pa}"]])

        def stEV(qi, h):
            k = qi * NH + h
            pa = k % 2
            w.step("vector", lambda e: e.tensor_scalar(
                out=attok[:, 64*h:64*(h+1)], in0=psC[pa][:, 0:64],
                scalar1=recipz[:, pa:pa+1], scalar2=None, op0=ALU.mult),
                reads=[R[f"psC{pa}"], R[f"recipz{pa}"]], writes=[R["attok"]])

        def tail(qi):
            def _tr(e):
                for s8 in range(8):
                    i = e.transpose(out=psT[:, 128*s8:128*(s8+1)],
                                    in_=attok[:, 128*s8:128*(s8+1)],
                                    identity=identbf[:])
                return i
            w.step("tensor", _tr, reads=[R["attok"]], writes=[R["psT"]])
            w.step("vector", lambda e: e.tensor_copy(
                out=afm[:], in_=psT[:].rearrange("p (ec t) -> p ec t", ec=8)),
                reads=[R["psT"]], writes=[R["afm"]])
            def _wo(e):
                for half in range(2):
                    for s8 in range(8):
                        e.matmul(out=psB[half][:], lhsT=afm[:, s8, :],
                                 rhs=wosb[:, s8, 512*half:512*(half+1)],
                                 start=(s8 == 0), stop=False)
                    i = e.matmul(out=psB[half][:], lhsT=onesg[0:1, :],
                                 rhs=borow[:, 512*half:512*(half+1)],
                                 start=False, stop=True)
                return i
            w.step("tensor", _wo, reads=[R["afm"], R["wo"], R["biases"]],
                   writes=[R["psB0"], R["psB1"]])
            pq = qi % 2
            def _ev(e):
                e.activation(out=attst[:, pq, 0:512], in_=psB[0][:], func=AF.Copy)
                return e.activation(out=attst[:, pq, 512:1024], in_=psB[1][:],
                                    func=AF.Copy)
            w.step("scalar", _ev, reads=[R["psB0"], R["psB1"]],
                   writes=[R[f"attst{pq}"]])
            w.step("sync", lambda e: e.dma_start(
                out=attno_hbm[128*qi:128*(qi+1), :], in_=attst[:, pq, :]),
                reads=[R[f"attst{pq}"]], writes=[R["d_attno"]], sem=f"SA{pq}",
                n=16)

        KK = NT * NH
        def hk(k):
            return k // NH, k % NH
        for k in range(KK + 3):
            if k < KK:
                qi, h = hk(k)
                stA(qi, h)
                stM(qi, h)
                stE(qi, h)
            if 1 <= k < KK + 1:
                qi, h = hk(k - 1)
                stT(qi, h)
                stPC(qi, h)
            if 2 <= k < KK + 2:
                qi, h = hk(k - 2)
                stPV(qi, h)
                stRZ(qi, h)
            if 3 <= k < KK + 3:
                qi, h = hk(k - 3)
                stEV(qi, h)
                if h == NH - 1:
                    tail(qi)

    run_phase(ph_p4)
    if debug:
        def ph_dbg4(w, R):
            w.step("sync", lambda e: e.dma_start(out=dbg_at, in_=attno_hbm),
                   reads=[R["d_attno"]], sem="ST", n=16)
        run_phase(ph_dbg4)
    p1ctx.close()
    lctx.close()

    pwctx = ExitStack()
    wdt = sb(pwctx, "wdt", [128, NI, E], BF16)
    p2ctx = ExitStack()
    xqi = sb(p2ctx, "xqi", [128, E], I32)
    xqb = sb(p2ctx, "xqb", [128, 2, E], BF16)

    def ph_p2(w, R):
        w.serial = SERIAL_P2
        def _ab(e):
            e.tensor_scalar(out=acol[:], in0=stat[:, 64:80], scalar1=rxsx[:, 0:1],
                            scalar2=None, op0=ALU.mult)
            return e.tensor_scalar(out=b2col[:], in0=stat[:, 80:96],
                                   scalar1=rxsx[:, 0:1], scalar2=None, op0=ALU.mult)
        w.step("vector", _ab, reads=[R["stat"], R["rxsx"]], writes=[R["acol"]])
        for tc in range(NT):
            par = tc % 2
            xb = R[f"xbuf{par}"]
            w.step("sync", lambda e, tc=tc, par=par: e.dma_start(
                out=xbuf[:, par, :], in_=x_d[128*tc:128*(tc+1), :]),
                writes=[xb], sem=f"LA{par}", n=16)
            def _q(e, tc=tc, par=par):
                w.zs(e, e.tensor_scalar(out=xqi[:], in0=xbuf[:, par, :],
                                        scalar1=acol[:, tc:tc+1],
                                        scalar2=b2col[:, tc:tc+1],
                                        op0=ALU.mult, op1=ALU.add))
                return e.tensor_scalar(out=xqb[:, par, :], in0=xqi[:],
                                       scalar1=-128.0, scalar2=127.0,
                                       op0=ALU.max, op1=ALU.min)
            w.step("vector", _q, reads=[xb, R["acol"]],
                   writes=[R["xqi"], R[f"xqb{par}"]])
            def _tr(e, par=par):
                for s8 in range(8):
                    i = e.transpose(out=psT[:, 128*s8:128*(s8+1)],
                                    in_=xqb[:, par, 128*s8:128*(s8+1)],
                                    identity=identbf[:])
                return i
            w.step("tensor", _tr, reads=[R[f"xqb{par}"]], writes=[R["psT"]])
            w.step("scalar", lambda e, tc=tc: e.copy(
                out=xnfm[:, :, 128*tc:128*(tc+1)],
                in_=psT[:].rearrange("p (ec t) -> p ec t", ec=8)),
                reads=[R["psT"]], writes=[R[f"xnfm{tc//4}"]])

    run_phase(ph_p2, barrier=True)

    p3ctx = ExitStack()
    wgt = sb(p3ctx, "wgt", [128, 2, 8, 128], BF16)
    wut = sb(p3ctx, "wut", [128, 2, 8, 128], BF16)
    silub = sb(p3ctx, "silub", [128, 512], F32)
    dbgslb = sb(p3ctx, "dbgslb", [128, 4, 512], F32) if debug else None
    upsb = sb(p3ctx, "upsb", [128, 512], F32)
    hbuf = sb(p3ctx, "hbuf", [128, 2, 512], F32)

    def ph_p3(w, R):
        w.serial = SERIAL_P3
        for c4 in range(4):
            w.step("gpsimd", lambda e, c4=c4: e.dma_start(
                out=wdt[:, :, 256*c4:256*(c4+1)],
                in_=wdT_hbm[:, :, 256*c4:256*(c4+1)].rearrange("g p e -> p g e")),
                reads=[R["d_wdT"]], writes=[R["wdt"]], sem="LG", n=16)
        for g in range(NI):
            r8, j = g // 3, g % 3
            gp = g % 2
            w.step("gpsimd", lambda e, r8=r8, j=j, gp=gp: e.dma_start(
                out=wgt[:, gp, :, :], in_=agg_o[r8, j]),
                reads=[R["d_aggo"]], writes=[R[f"wgt{gp}"]], sem=f"LB{gp}", n=16)
            w.step("gpsimd", lambda e, r8=r8, j=j, gp=gp: e.dma_start(
                out=wut[:, gp, :, :], in_=agu_o[r8, j]),
                reads=[R["d_aguo"]], writes=[R[f"wut{gp}"]], sem=f"LB{gp}", n=16)
            for ti in range(NTI):
                t0 = 512 * ti
                pg = ti % 2
                def _mm(e, gp=gp, pg=pg, t0=t0):
                    for ei in range(8):
                        e.matmul(out=psA[pg][:], lhsT=wgt[:, gp, ei, :],
                                 rhs=xnfm[:, ei, t0:t0+512],
                                 start=(ei == 0), stop=(ei == 7))
                    for ei in range(8):
                        i = e.matmul(out=psB[pg][:], lhsT=wut[:, gp, ei, :],
                                     rhs=xnfm[:, ei, t0:t0+512],
                                     start=(ei == 0), stop=(ei == 7))
                    return i
                w.step("tensor", _mm,
                       reads=[R[f"wgt{gp}"], R[f"wut{gp}"], R[f"xnfm{ti}"]],
                       writes=[R[f"psA{pg}"], R[f"psB{pg}"]])
                def _ep(e, g=g, pg=pg):
                    e.activation(out=silub[:], in_=psA[pg][:], func=AF.Silu,
                                 scale=sxswg[:, g:g+1])
                    return e.activation(out=upsb[:], in_=psB[pg][:], func=AF.Copy,
                                        scale=sxswu[:, g:g+1])
                w.step("scalar", _ep,
                       reads=[R[f"psA{pg}"], R[f"psB{pg}"], R["sxsw"]],
                       writes=[R["silub"], R["upsb"]])
                def _h(e, g=g, ti=ti, pg=pg):
                    w.zs(e, e.tensor_tensor(out=hbuf[:, pg, :], in0=silub[:],
                                            in1=upsb[:], op=ALU.mult))
                    return e.tensor_reduce(out=habs[:, 4*g+ti:4*g+ti+1],
                                           in_=hbuf[:, pg, :], axis=AX.X, op=ALU.max,
                                           apply_absolute_value=True)
                if debug and g == 0 and ti == 0:
                    def _dsl(e):
                        e.tensor_copy(out=dbgslb[:, 0, :], in_=silub[:])
                        e.tensor_copy(out=dbgslb[:, 1, :], in_=upsb[:])
                        e.tensor_copy(out=dbgslb[:, 2, :], in_=psA[0][:])
                        return e.tensor_copy(out=dbgslb[:, 3, :], in_=psB[0][:])
                    w.step("vector", _dsl,
                           reads=[R["silub"], R["upsb"], R["psA0"], R["psB0"]],
                           writes=[R["tmpdbg"]])
                    w.step("sync", lambda e: e.dma_start(out=dbg_sl, in_=dbgslb[:]),
                           reads=[R["tmpdbg"]], sem="ST", n=16)
                    w.step("sync", lambda e: e.dma_start(out=dbg_wg0, in_=wgt[:, 0, :, :]),
                           reads=[R["wgt0"]], sem="ST", n=16)
                    w.step("sync", lambda e: e.dma_start(out=dbg_ag0, in_=agg_o[0, 0]),
                           reads=[R["d_aggo"]], sem="ST", n=16)
                w.step("vector", _h, reads=[R["silub"], R["upsb"]],
                       writes=[R[f"hbuf{pg}"], R["habs"]])
                w.step("sync", lambda e, g=g, t0=t0, pg=pg: e.dma_start(
                    out=h_hbm[g, :, t0:t0+512], in_=hbuf[:, pg, :]),
                    reads=[R[f"hbuf{pg}"]], writes=[R["d_hhbm"]], sem=f"SA{pg}",
                    n=16)
        def _hr(e):
            e.memset(gtr[:], 0.0)
            return e.tensor_reduce(out=gtr[:, 0:1], in_=habs[:, 0:88], axis=AX.X,
                                   op=ALU.max)
        w.step("vector", _hr, reads=[R["habs"]], writes=[R["small"]])
        w.step("tensor", lambda e: e.transpose(out=psC1[0:32, 0:128],
                                               in_=gtr[:, 0:32], identity=identf[:]),
               reads=[R["small"]], writes=[R["psC1"]])
        w.step("vector", lambda e: e.reduce_max(out=sc11[:, 5:6],
                                                in_=psC1[0:1, 0:128], axis=AX.X),
               reads=[R["psC1"]], writes=[R["sc11"]])
        w.step("gpsimd", lambda e: e.dma_start(out=cc2i, in_=sc11[:, 5:6]),
               reads=[R["sc11"]], writes=[R["d_cc2i"]], sem="LC1", n=16)
        w.step("gpsimd", lambda e: e.collective_compute(
            "AllReduce", ALU.max, replica_groups=[list(range(8))],
            ins=[cc2i.opt()], outs=[cc2o.opt()]),
            reads=[R["d_cc2i"]], writes=[R["d_cc2o"]], sem="G")
        w.step("gpsimd", lambda e: e.dma_start(out=sc11[:, 5:6], in_=cc2o),
               reads=[R["d_cc2o"]], writes=[R["sc11"]], sem="LD0", n=16)
        w.step("scalar", lambda e: e.activation(out=sc11[:, 6:7], in_=clipd_s[:],
                                                func=AF.Sigmoid),
               reads=[R["sc11"]], writes=[R["sc11"]])
        def _sh(e):
            w.zs(e, e.tensor_tensor(out=sc11[:, 7:8], in0=sc11[:, 6:7],
                                    in1=sc11[:, 5:6], op=ALU.mult))
            w.zs(e, e.reciprocal(out=sc11[:, 5:6], in_=sc11[:, 7:8]))
            w.zs(e, e.tensor_scalar(out=sc11[:, 5:6], in0=sc11[:, 5:6], scalar1=QMAX,
                                    scalar2=None, op0=ALU.mult))
            return e.tensor_scalar(out=sc11[:, 6:7], in0=sc11[:, 7:8],
                                   scalar1=1.0/QMAX, scalar2=None, op0=ALU.mult)
        w.step("vector", _sh, reads=[R["sc11"]], writes=[R["sc11"]])
        def _bc(e):
            e.matmul(out=psC0[:, 2:3], lhsT=onesf[:], rhs=sc11[:, 5:6],
                     start=True, stop=False)
            return e.matmul(out=psC0[:, 3:4], lhsT=onesf[:], rhs=sc11[:, 6:7],
                            start=False, stop=True)
        w.step("tensor", _bc, reads=[R["sc11"]], writes=[R["psC0"]])
        w.step("vector", lambda e: e.tensor_copy(out=rxsx[:, 2:4], in_=psC0[:, 2:4]),
               reads=[R["psC0"]], writes=[R["rxsx"]])

    run_phase(ph_p3)
    if debug:
        def ph_dbg3(w, R):
            w.step("sync", lambda e: e.dma_start(out=dbg_h, in_=h_hbm),
                   reads=[R["d_hhbm"]], sem="ST", n=16)
            w.step("sync", lambda e: e.dma_start(out=dbg_sc, in_=sc11[:]),
                   reads=[R["sc11"]], sem="ST", n=16)
            w.step("sync", lambda e: e.dma_start(out=dbg_rx, in_=rxsx[:]),
                   reads=[R["rxsx"]], sem="ST", n=16)
            w.step("sync", lambda e: e.dma_start(out=dbg_ssw, in_=sxswg[:]),
                   reads=[R["sxsw"]], sem="ST", n=16)
            w.step("sync", lambda e: e.dma_start(out=dbg_agg, in_=agg_o),
                   reads=[R["d_aggo"]], sem="ST", n=16)
        run_phase(ph_dbg3)
    p3ctx.close()
    p2ctx.close()

    p5ctx = ExitStack()
    hqsb = sb(p5ctx, "hqsb", [128, 2, NI, 512], BF16)
    hstg = sb(p5ctx, "hstg", [128, 2, 512], F32)
    hi32 = sb(p5ctx, "hi32", [128, 512], I32)
    hclip = sb(p5ctx, "hclip", [128, 512], BF16)
    outsb = sb(p5ctx, "outsb", [128, 2, E], F32)
    tmpadd = sb(p5ctx, "tmpadd", [128, 512], F32)

    def ph_p5(w, R):
        def qblock(ti):
            t0 = 512 * ti
            hb = ti % 2
            w.serial = SERIAL_P5 in (1, 2)
            for g in range(NI):
                gp = g % 2
                w.step("sync", lambda e, g=g, t0=t0, gp=gp: e.dma_start(
                    out=hstg[:, gp, :], in_=h_hbm[g, :, t0:t0+512]),
                    reads=[R["d_hhbm"]], writes=[R[f"hstg{gp}"]], sem=f"LA{gp}",
                    n=16)
                def _q(e, gp=gp):
                    w.zs(e, e.tensor_scalar(out=hi32[:], in0=hstg[:, gp, :],
                                            scalar1=rxsx[:, 2:3], scalar2=None,
                                            op0=ALU.mult))
                    return e.tensor_scalar(out=hclip[:], in0=hi32[:],
                                           scalar1=-128.0, scalar2=127.0,
                                           op0=ALU.max, op1=ALU.min)
                w.step("vector", _q, reads=[R[f"hstg{gp}"], R["rxsx"]],
                       writes=[R["hint"]])
                w.step("scalar", lambda e, g=g, hb=hb: e.activation(
                    out=hqsb[:, hb, g, :], in_=hclip[:], func=AF.Copy,
                    scale=rxsx[:, 3:4]),
                    reads=[R["hint"], R["rxsx"]], writes=[R[f"hq{hb}_{g}"]])

        def dblock(ti):
            hb = ti % 2
            w.serial = SERIAL_P5 in (1, 3)
            for tq in range(4):
                qi = 4 * ti + tq
                pq = qi % 2
                w.step("gpsimd", lambda e, qi=qi, pq=pq: e.dma_start(
                    out=xbuf[:, pq, :], in_=x_d[128*qi:128*(qi+1), :]),
                    writes=[R[f"xbuf{pq}"]], sem=f"LB{pq}", n=16)
                w.step("gpsimd", lambda e, qi=qi, pq=pq: e.dma_start(
                    out=attst[:, pq, :], in_=attno_hbm[128*qi:128*(qi+1), :]),
                    reads=[R["d_attno"]], writes=[R[f"attst{pq}"]], sem=f"LC{pq}",
                    n=16)
                def _dn(e, tq=tq, pq=pq, hb=hb):
                    for half, bank in ((0, psA[pq]), (1, psB[pq])):
                        for g in range(NI):
                            i = e.matmul(out=bank[:],
                                         lhsT=hqsb[:, hb, g, 128*tq:128*(tq+1)],
                                         rhs=wdt[:, g, 512*half:512*(half+1)],
                                         start=(g == 0), stop=(g == NI - 1))
                    return i
                w.step("tensor", _dn,
                       reads=[R["wdt"]] + [R[f"hq{hb}_{g}"] for g in range(NI)],
                       writes=[R[f"psA{pq}"], R[f"psB{pq}"]])
                def _fin(e, pq=pq):
                    for half, bank in ((0, psA[pq]), (1, psB[pq])):
                        w.zs(e, e.tensor_tensor(out=tmpadd[:], in0=bank[:],
                                                in1=xbuf[:, pq, 512*half:512*(half+1)],
                                                op=ALU.add))
                        i = e.tensor_tensor(out=outsb[:, pq, 512*half:512*(half+1)],
                                            in0=tmpadd[:],
                                            in1=attst[:, pq, 512*half:512*(half+1)],
                                            op=ALU.add)
                    return i
                w.step("vector", _fin,
                       reads=[R[f"psA{pq}"], R[f"psB{pq}"], R[f"xbuf{pq}"],
                              R[f"attst{pq}"]],
                       writes=[R[f"outsb{pq}"], R["tmpadd"]])
                w.step("gpsimd", lambda e, qi=qi, pq=pq: e.dma_start(
                    out=out_d[128*qi:128*(qi+1), :], in_=outsb[:, pq, :]),
                    reads=[R[f"outsb{pq}"]], writes=[], sem=f"SA{pq}", n=16)

        for ti in range(NTI):
            qblock(ti)
            if ti > 0:
                dblock(ti - 1)
        dblock(NTI - 1)
        if debug:
            for g in range(NI):
                w.step("sync", lambda e, g=g: e.dma_start(
                    out=dbg_hq[:, g, :] if True else None, in_=hqsb[:, 1, g, :]),
                    reads=[R["hq1_" + str(g)]], sem="ST", n=16)
        if w.me == "sync":
            for s_ in ("ST", "SA0", "SA1"):
                if w.c[s_] > 0:
                    w.eng.wait_ge(semh[s_], w.c[s_])

    run_phase(ph_p5, barrier=True)
    p5ctx.close()
    pwctx.close()
    xctx.close()

    octx.close()
    return nc


_NC_CACHE = {}


def kernel(L0_attn_x, L0_attn_wq, L0_attn_bq, L0_attn_wk, L0_attn_bk, L0_attn_wv,
           L0_attn_bv, L0_attn_wo, L0_attn_bo, L0_attn_qn_w, L0_attn_kn_w,
           L0_ffn_clip_x, L0_ffn_clip_down, L0_ffn_w_gate, L0_ffn_w_up,
           L0_ffn_w_down):
    f32 = lambda a: np.ascontiguousarray(np.asarray(a, dtype=np.float32))
    x = f32(L0_attn_x)
    wq, wk, wv, wo = f32(L0_attn_wq), f32(L0_attn_wk), f32(L0_attn_wv), f32(L0_attn_wo)
    bq, bk = f32(L0_attn_bq).reshape(1, E), f32(L0_attn_bk).reshape(1, E)
    bv, bo = f32(L0_attn_bv).reshape(1, E), f32(L0_attn_bo).reshape(1, E)
    qn, kn = f32(L0_attn_qn_w).reshape(1, D), f32(L0_attn_kn_w).reshape(1, D)
    clipx = f32(L0_ffn_clip_x).reshape(1, 1)
    clipd = f32(L0_ffn_clip_down).reshape(1, 1)
    wg, wu, wd = f32(L0_ffn_w_gate), f32(L0_ffn_w_up), f32(L0_ffn_w_down)

    consts = _host_consts()
    if "nc" not in _NC_CACHE:
        _NC_CACHE["nc"] = build_nc()
    nc = _NC_CACHE["nc"]

    in_maps = _mk_inmaps(x, wq, wk, wv, wo, bq, bk, bv, bo, qn, kn, clipx, clipd,
                         wg, wu, wd, consts)
    res = run_bass_kernel_spmd(nc, in_maps, core_ids=list(range(8)))
    out = np.stack([res.results[c]["out"] for c in range(8)], axis=0)
    return out.astype(np.float32)


def _mk_inmaps(x, wq, wk, wv, wo, bq, bk, bv, bo, qn, kn, clipx, clipd,
               wg, wu, wd, consts):
    in_maps = []
    for c in range(8):
        lo = 3 * c
        gsh = np.ones((3, 128, E), np.float32)
        ush = np.ones((3, 128, E), np.float32)
        for j in range(3):
            g = lo + j
            if g < NI:
                gsh[j] = wg[128*g:128*(g+1), :]
                ush[j] = wu[128*g:128*(g+1), :]
        m = dict(x=np.ascontiguousarray(x[c]),
                 wq=wq, wk=wk, wv=wv, wo=wo, bq=bq, bk=bk, bv=bv, bo=bo,
                 qn=qn, kn=kn, clipx=clipx, clipd=clipd,
                 wg_sh=np.ascontiguousarray(gsh),
                 wu_sh=np.ascontiguousarray(ush),
                 wd_sh=np.ascontiguousarray(wd[128*c:128*(c+1), :]))
        m.update(consts)
        in_maps.append(m)
    return in_maps
